# version 05
# baseline (speedup 1.0000x reference)
import numpy as np
import ml_dtypes
import concourse.bass as bass
import concourse.mybir as mybir
from concourse.bass_utils import run_bass_kernel_spmd

F32 = mybir.dt.float32
BF16 = mybir.dt.bfloat16
FP8 = mybir.dt.float8e4
Alu = mybir.AluOpType
Act = mybir.ActivationFunctionType
Ax = mybir.AxisListType

C = 2048
NT = 1024
NH = 512
KC = 16
HC = 16
D = 128
L = 64
IC = 48
GSZ = 12
NG = IC // GSZ
SCALE = float(1.0 / np.sqrt(128.0))
NEG = -1.0e9
EPS = 1e-5


class Plan:
    def __init__(self):
        self.ops = []
        self.cnt = {}

    def bump(self, sem, amt):
        self.cnt[sem] = self.cnt.get(sem, 0) + amt
        return self.cnt[sem]

    def at(self, sem):
        return self.cnt.get(sem, 0)

    def add(self, eng, fn, waits=(), inc=None):
        incs = []
        tgt = None
        if inc is not None:
            sem, amt = inc
            tgt = self.bump(sem, amt)
            incs = [(sem, amt)]
        self.ops.append((eng, tuple(waits), fn, tuple(incs)))
        return tgt


ESEM = {"vector": "dv", "scalar": "ac", "tensor": "mm"}
NLANES = 4
QLANES = {"sync": ["sq%d" % i for i in range(NLANES)],
          "gpsimd": ["gq%d" % i for i in range(NLANES)]}


def build():
    nc = bass.Bass("TRN2", target_bir_lowering=False)
    p = Plan()

    def dram_in(name, shape, dt=F32):
        return nc.dram_tensor(name, shape, dt, kind="ExternalInput").ap()

    xT = dram_in("xT", [C, NT])
    xhT = dram_in("xhT", [C, 128])
    wk = dram_in("wk", [C, C])
    wv = dram_in("wv", [C, C])
    wg = dram_in("wg", [C, HC * L])
    wp = dram_in("wp", [C, C])
    wgate = dram_in("wgate", [IC * 128, C])
    wup = dram_in("wup", [IC * 128, C])
    wdown = dram_in("wdown", [NG * 16 * 128, GSZ * 128])
    lqd = dram_in("lq", [128, HC * L])
    ropec = dram_in("ropec", [128, 128])
    ropes = dram_in("ropes", [128, 128])
    maska = dram_in("maska", [64, 128])
    idf = dram_in("idf", [128, 128])
    gsd = dram_in("gs", [128, IC])
    usd = dram_in("us", [128, IC])
    dsd = dram_in("ds", [128, KC])
    out = nc.dram_tensor("out", [C, NT], F32, kind="ExternalOutput").ap()

    import contextlib
    st = contextlib.ExitStack()
    sb = lambda name, shape, dt: st.enter_context(nc.sbuf_tensor(name, shape, dt))
    psb = lambda name, shape, dt: st.enter_context(nc.psum_tensor(name, shape, dt))

    with st:
        x_r = sb("x_r", [128, KC * NT], F32)
        xs_r = sb("xs_r", [128, KC * NT], BF16)
        xh_r = sb("xh_r", [128, KC * 128], F32)
        xsh_r = sb("xsh_r", [128, KC * 128], BF16)
        big = sb("big", [128, KC * NT], BF16)
        gFM = sb("gFM", [128, 8 * NT], BF16)
        wfa = sb("wfa", [128, 2048], F32)
        wfb = sb("wfb", [128, 2048], F32)
        wqa = sb("wqa", [128, 2048], BF16)
        wqb = sb("wqb", [128, 2048], BF16)
        qi32a = sb("qi32a", [128, 512], mybir.dt.int32)
        qi32b = sb("qi32b", [128, 512], mybir.dt.int32)
        scr1 = sb("scr1", [128, NT], F32)
        scr3 = sb("scr3", [128, 128], F32)
        scr4 = sb("scr4", [128, 128], F32)
        lqb = sb("lqb", [128, HC * L], BF16)
        rC = sb("rC", [128, 128], F32)
        rS = sb("rS", [128, 128], F32)
        mskt = sb("mskt", [64, 128], F32)
        idft = sb("idft", [128, 128], F32)
        ksw = sb("ksw", [128, 128], F32)
        k_sb = sb("k_sb", [128, HC * 128], BF16)
        v_sb = sb("v_sb", [128, HC * 128], BF16)

        wT_sb = sb("wT_sb", [128, 2 * 64], BF16)
        attn_sb = sb("attn_sb", [128, 8 * 128], BF16)
        atmp = sb("atmp", [64, 128], BF16)
        vech = sb("vech", [4, 128], F32)
        vec_hm, vec_hr, vec_h1, vec_h2 = (vech[0:1, :], vech[1:2, :],
                                          vech[2:3, :], vech[3:4, :])
        ones_c = sb("ones_c", [128, 1], F32)
        eps_t = sb("eps_t", [128, 1], F32)
        ones_r2 = sb("ones_r2", [1, 128], F32)
        ones_r = ones_r2[0:1, :]
        gs_t = sb("gs_t", [128, IC], F32)
        us_t = sb("us_t", [128, IC], F32)
        ds_t = sb("ds_t", [128, KC], F32)
        mx8 = sb("mx8", [128, 8], F32)
        nm8 = sb("nm8", [128, 8], F32)
        sm8 = sb("sm8", [128, 8], F32)
        rs8 = sb("rs8", [128, 8], F32)
        sc_mx = sb("sc_mx", [64, 1], F32)
        sc_nm = sb("sc_nm", [64, 1], F32)
        sc_s = sb("sc_s", [64, 1], F32)
        sc_r = sb("sc_r", [64, 1], F32)

        kt1 = scr3
        smt = scr3[0:64, :]
        ewt = scr4[0:64, :]
        wfp = ksw[0:64, :]
        ps_big = psb("ps_big", [128, 2048], F32)
        ps_b1 = psb("ps_b1", [128, 1024], F32)
        ps_b2 = psb("ps_b2", [128, 512], F32)
        ps_b3 = psb("ps_b3", [128, 512], F32)

        lane_idx = {"sync": 0, "gpsimd": 0}

        def dma(q, out_ap, in_ap, waits=()):
            lane = QLANES[q][lane_idx[q] % NLANES]
            lane_idx[q] += 1
            w = list(waits)
            if p.at(lane) > 0:
                w.append((lane, p.at(lane)))
            fn = (lambda o, i: (lambda eng: eng.dma_start(out=o, in_=i)))(out_ap, in_ap)
            cnt = p.add(q, fn, waits=w, inc=(lane, 16))
            return (lane, cnt)

        def op(eng, fn, waits=()):
            return p.add(eng, fn, waits=waits, inc=(ESEM[eng], 1))

        d_lq = dma("gpsimd", wfa[:, 0:HC * L], lqd[:, :])
        dma("gpsimd", rC[:, :], ropec[:, :])
        dma("gpsimd", rS[:, :], ropes[:, :])
        dma("gpsimd", mskt[:, :], maska[:, :])
        dma("gpsimd", idft[:, :], idf[:, :])
        dma("gpsimd", gs_t[:, :], gsd[:, :])
        dma("gpsimd", us_t[:, :], usd[:, :])
        dma("gpsimd", ds_t[:, :], dsd[:, :])
        d_xh = [dma("sync", xh_r[:, c * 128:(c + 1) * 128], xhT[c * 128:(c + 1) * 128, :])
                for c in range(KC)]
        d_x = [dma("sync", x_r[:, c * NT:(c + 1) * NT], xT[c * 128:(c + 1) * 128, :])
               for c in range(KC)]

        op("vector", lambda eng: eng.memset(ones_c[:, :], 1.0))
        op("vector", lambda eng: eng.memset(ones_r, 1.0))
        op("vector", lambda eng: eng.memset(eps_t[:, :], EPS))
        c_ones = p.at("dv")
        op("vector", lambda eng: eng.tensor_copy(lqb[:, :], wfa[:, 0:HC * L]),
           waits=[d_lq])
        slab0_init = p.at("dv")

        sq_done = []
        for c in range(KC):
            w = [d_x[c]]
            if c >= 1:
                w.append(("mm", sq_done[c - 1]))
            scr = scr1
            op("scalar", (lambda cc, s: (lambda eng: eng.activation(
                s[:, :], x_r[:, cc * NT:(cc + 1) * NT], Act.Square)))(c, scr), waits=w)
            a_sq = p.at("ac")
            last = None
            for h in range(2):
                def mmfn(cc, s, hh):
                    def f(eng):
                        eng.matmul(ps_big[0:1, hh * NH:(hh + 1) * NH], ones_c[:, :],
                                   x_r[:, cc * NT + hh * NH: cc * NT + (hh + 1) * NH],
                                   start=(cc == 0), stop=(cc == KC - 1))
                        return eng.matmul(ps_big[0:1, 1024 + hh * NH:1024 + (hh + 1) * NH],
                                          ones_c[:, :], s[:, hh * NH:(hh + 1) * NH],
                                          start=(cc == 0), stop=(cc == KC - 1))
                    return f
                wmm = [("ac", a_sq)] + ([("dv", c_ones)] if c == 0 else [])
                last = op("tensor", mmfn(c, scr, h), waits=wmm)
            sq_done.append(last)
        mm_stats = p.at("mm")

        hs_done = []
        for c in range(KC):
            w = [d_xh[c]]
            if c >= 1:
                w.append(("mm", hs_done[c - 1]))
            op("scalar", (lambda cc: (lambda eng: eng.activation(
                scr3[:, :], xh_r[:, cc * 128:(cc + 1) * 128], Act.Square)))(c), waits=w)
            a_sq = p.at("ac")

            def mmfn(cc):
                def f(eng):
                    eng.matmul(ps_b1[0:1, 0:128], ones_c[:, :],
                               xh_r[:, cc * 128:(cc + 1) * 128],
                               start=(cc == 0), stop=(cc == KC - 1))
                    return eng.matmul(ps_b1[0:1, 512:640], ones_c[:, :], scr3[:, :],
                                      start=(cc == 0), stop=(cc == KC - 1))
                return f
            hs_done.append(op("tensor", mmfn(c), waits=[("ac", a_sq)]))
        mm_hstats = p.at("mm")

        inv_n = 1.0 / float(C)
        A = wfa[0:1, 0:1024]
        B = scr1[0:1, 0:1024]

        op("vector", lambda eng: eng.tensor_scalar(A, ps_big[0:1, 0:1024], inv_n,
                                                   None, Alu.mult),
           waits=[("mm", mm_hstats), ("dv", slab0_init)])
        c_m = p.at("dv")

        def bcast_m(eng):
            r = None
            for h in range(2):
                r = eng.matmul(ps_big[:, h * NH:(h + 1) * NH], ones_r,
                               wfa[0:1, h * NH:(h + 1) * NH], start=True, stop=True)
            return r
        mm_bm = op("tensor", bcast_m, waits=[("dv", c_m)])

        def fin_v(eng):
            eng.tensor_scalar(B, ps_big[0:1, 1024:2048], inv_n, None, Alu.mult)
            eng.tensor_tensor(A, A, A, Alu.mult)
            return eng.tensor_tensor(B, B, A, Alu.subtract)
        op("vector", fin_v, waits=[("mm", mm_bm)])
        c_var = p.at("dv")
        op("scalar", lambda eng: eng.activation(A, B, Act.Sqrt, bias=eps_t[0:1, :]),
           waits=[("dv", c_var)])
        a_std = p.at("ac")
        op("vector", lambda eng: eng.reciprocal(B, A), waits=[("ac", a_std)])
        c_r = p.at("dv")

        def bcast_r(eng):
            r = None
            for h in range(2):
                r = eng.matmul(ps_big[:, 1024 + h * NH:1024 + (h + 1) * NH], ones_r,
                               scr1[0:1, h * NH:(h + 1) * NH], start=True, stop=True)
            return r
        mm_br = op("tensor", bcast_r, waits=[("dv", c_r)])

        A2 = vech[0:1, :]
        B2 = scr1[0:1, 0:128]
        op("vector", lambda eng: eng.tensor_scalar(A2, ps_b1[0:1, 0:128], inv_n,
                                                   None, Alu.mult),
           waits=[("mm", mm_br)])
        c_hm = p.at("dv")
        mm_bhm = op("tensor", lambda eng: eng.matmul(ps_b2[:, 0:128], ones_r, A2,
                                                     start=True, stop=True),
                    waits=[("dv", c_hm)])

        def fin_hv(eng):
            eng.tensor_scalar(B2, ps_b1[0:1, 512:640], inv_n, None, Alu.mult)
            eng.tensor_tensor(A2, A2, A2, Alu.mult)
            return eng.tensor_tensor(B2, B2, A2, Alu.subtract)
        op("vector", fin_hv, waits=[("mm", mm_bhm)])
        c_hv = p.at("dv")
        op("scalar", lambda eng: eng.activation(A2, B2, Act.Sqrt, bias=eps_t[0:1, :]),
           waits=[("dv", c_hv)])
        a_hstd = p.at("ac")
        op("vector", lambda eng: eng.reciprocal(B2, A2), waits=[("ac", a_hstd)])
        c_hr = p.at("dv")
        mm_bcast = op("tensor", lambda eng: eng.matmul(ps_b2[:, 128:256], ones_r, B2,
                                                       start=True, stop=True),
                      waits=[("dv", c_hr)])

        for c in range(KC):
            def xnfn(cc):
                def f(eng):
                    eng.tensor_tensor(scr1[:, :], x_r[:, cc * NT:(cc + 1) * NT],
                                      ps_big[:, 0:1024], Alu.subtract)
                    return eng.tensor_tensor(xs_r[:, cc * NT:(cc + 1) * NT], scr1[:, :],
                                             ps_big[:, 1024:2048], Alu.mult)
                return f
            op("vector", xnfn(c), waits=[("mm", mm_bcast)] if c == 0 else ())
        for c in range(KC):
            def xnhfn(cc):
                def f(eng):
                    eng.tensor_tensor(scr3[:, :], xh_r[:, cc * 128:(cc + 1) * 128],
                                      ps_b2[:, 0:128], Alu.subtract)
                    return eng.tensor_tensor(xsh_r[:, cc * 128:(cc + 1) * 128], scr3[:, :],
                                             ps_b2[:, 128:256], Alu.mult)
                return f
            op("vector", xnhfn(c))
        c_xn = p.at("dv")

        slab_state = {"dv": [slab0_init, 0], "mm": [0, 0]}
        QAF = [xh_r[:, 0:512], xh_r[:, 1024:1536]]
        QX = [xh_r[:, 512:1024], xh_r[:, 1536:2048]]
        QB = [k_sb, v_sb]
        QI = [qi32a, qi32b]
        qstate = {"d2": [0, 0], "nxt": 0}
        T1, T2, T3, T4 = 2.0 ** -5, 2.0 ** -4, 2.0 ** -3, 2.0 ** -2
        EPS24 = 2.0 ** -24

        def quant_piece(wf, wq, c0, setid, first_waits=()):
            qaf, qx, qi = QAF[setid], QX[setid], QI[setid]
            bb = QB[setid]
            z1, z2, z3, z4 = (bb[:, 0:512], bb[:, 512:1024],
                              bb[:, 1024:1536], bb[:, 1536:2048])
            w1 = list(first_waits)
            if qstate["d2"][setid]:
                w1.append(("dv", qstate["d2"][setid]))
            a1 = op("scalar", (lambda w_, q_: (lambda eng: eng.activation(
                q_, w_, Act.Abs)))(wf[:, c0:c0 + 512], qaf), waits=w1)

            def casc(w_, q_):
                def f(eng):
                    eng.tensor_scalar(z1, q_, T1, -256.0, Alu.is_ge, Alu.mult)
                    eng.tensor_scalar(z2, q_, T2, -128.0, Alu.is_ge, Alu.mult)
                    eng.scalar_tensor_tensor(z1, z1, 512.0, z2, Alu.add, Alu.add)
                    eng.tensor_scalar(z3, q_, T3, -64.0, Alu.is_ge, Alu.mult)
                    eng.tensor_scalar(z4, q_, T4, -32.0, Alu.is_ge, Alu.mult)
                    eng.tensor_tensor(z3, z3, z4, Alu.add)
                    eng.tensor_tensor(z2, z1, z3, Alu.add)
                    return eng.tensor_tensor(qx, w_, z2, Alu.mult)
                return f
            d1 = op("vector", casc(wf[:, c0:c0 + 512], qaf), waits=[("ac", a1)])

            def convs(eng):
                eng.activation(qi[:, :], qx, Act.Copy, bias=1024.0)
                eng.activation(qaf, qi[:, :], Act.Copy)
                return eng.activation(qi[:, :], qi[:, :], Act.Copy,
                                      bias=0.25, scale=0.5)
            a2 = op("scalar", convs, waits=[("dv", d1)])

            def corr(q_out):
                def f(eng):
                    eng.scalar_tensor_tensor(z4, qi[:, :], -2.0, qaf, Alu.mult,
                                             Alu.add)
                    eng.scalar_tensor_tensor(qx, qaf, -1024.0, qx,
                                             Alu.add, Alu.subtract)
                    eng.scalar_tensor_tensor(qx, z4, -EPS24, qx, Alu.mult,
                                             Alu.add)
                    eng.tensor_scalar(z1, qx, -0.5, None, Alu.is_lt)
                    eng.tensor_scalar(z3, qx, 0.5, None, Alu.is_gt)
                    eng.scalar_tensor_tensor(qx, qaf, -1024.0, z1,
                                             Alu.add, Alu.add)
                    eng.tensor_tensor(qx, qx, z3, Alu.subtract)
                    eng.reciprocal(qaf, z2)
                    eng.drain()
                    return eng.tensor_tensor(q_out, qx, qaf, Alu.mult)
                return f
            d2 = op("vector", corr(wq[:, c0:c0 + 512]), waits=[("ac", a2)])
            qstate["d2"][setid] = d2
            return d2

        def stream_slab(src_ap, ncols, slot, quantize, extra_waits=()):
            wf = (wfa, wfb)[slot]
            wq = (wqa, wqb)[slot]
            w = list(extra_waits)
            if slab_state["dv"][slot]:
                w.append(("dv", slab_state["dv"][slot]))
            d = dma("sync", wf[:, 0:ncols], src_ap, waits=w)
            w2 = [d]
            if slab_state["mm"][slot]:
                w2.append(("mm", slab_state["mm"][slot]))
            if quantize:
                cv = 0
                np_ = ncols // 512
                for pi in range(np_):
                    cv = quant_piece(wf, wq, pi * 512, qstate["nxt"] % 2,
                                     first_waits=(w2 if pi == 0 else ()))
                    qstate["nxt"] += 1
                slab_state["dv"][slot] = cv
                return wq, cv

            def cast(eng):
                return eng.tensor_copy(wq[:, 0:ncols], wf[:, 0:ncols])
            cv = op("vector", cast, waits=w2)
            slab_state["dv"][slot] = cv
            return wq, cv

        def slab_used(slot, mm_count):
            slab_state["mm"][slot] = mm_count

        for h in range(HC):
            wqv, cv = stream_slab(wk[h * 128:(h + 1) * 128, :], 2048, h % 2, False,
                                  extra_waits=[("mm", mm_bcast)] if h < 2 else ())

            def kmm(hh, wt):
                def f(eng):
                    r = None
                    for kc in range(KC):
                        r = eng.matmul(ps_big[:, hh * 128:(hh + 1) * 128],
                                       wt[:, kc * 128:(kc + 1) * 128],
                                       xsh_r[:, kc * 128:(kc + 1) * 128],
                                       start=(kc == 0), stop=(kc == KC - 1))
                    return r
                return f
            m = op("tensor", kmm(h, wqv), waits=[("dv", cv)])
            slab_used(h % 2, m)
        mm_k = p.at("mm")

        def vb(h):
            if h < 8:
                return ps_b1[:, h * 128:(h + 1) * 128]
            if h < 12:
                return ps_b2[:, (h - 8) * 128:(h - 7) * 128]
            return ps_b3[:, (h - 12) * 128:(h - 11) * 128]

        for h in range(HC):
            wqv, cv = stream_slab(wv[h * 128:(h + 1) * 128, :], 2048, h % 2, False)

            def vmm(hh, wt):
                def f(eng):
                    r = None
                    for kc in range(KC):
                        r = eng.matmul(vb(hh), xsh_r[:, kc * 128:(kc + 1) * 128],
                                       wt[:, kc * 128:(kc + 1) * 128],
                                       start=(kc == 0), stop=(kc == KC - 1))
                    return r
                return f
            m = op("tensor", vmm(h, wqv), waits=[("dv", cv)])
            slab_used(h % 2, m)
        mm_v = p.at("mm")

        for h in range(HC):
            op("scalar", (lambda hh: (lambda eng: eng.activation(
                v_sb[:, hh * 128:(hh + 1) * 128], vb(hh), Act.Copy)))(h),
               waits=[("mm", mm_v)] if h == 0 else ())
        a_vevac = p.at("ac")

        rope_done = []
        for h in range(HC):
            w = [("mm", mm_k), ("ac", a_vevac)] if h == 0 else [("dv", rope_done[-1])]
            a_kf = op("scalar", (lambda hh: (lambda eng: eng.activation(
                scr4[:, :], ps_big[:, hh * 128:(hh + 1) * 128], Act.Copy)))(h), waits=w)
            dsw1 = dma("gpsimd", ksw[0:64, :], scr4[64:128, :], waits=[("ac", a_kf)])
            dsw2 = dma("gpsimd", ksw[64:128, :], scr4[0:64, :])

            def rope(hh):
                def f(eng):
                    eng.tensor_tensor(kt1[:, :], scr4[:, :], rC[:, :], Alu.mult)
                    eng.tensor_tensor(ksw[:, :], ksw[:, :], rS[:, :], Alu.mult)
                    return eng.tensor_tensor(k_sb[:, hh * 128:(hh + 1) * 128],
                                             kt1[:, :], ksw[:, :], Alu.add)
                return f
            rope_done.append(op("vector", rope(h), waits=[dsw1, dsw2]))
        c_ropes = p.at("dv")

        attn_tail = []
        shift_dmas = []
        for h in range(HC):
            sps = (ps_b2, ps_b3)[h % 2]
            w = [("dv", c_ropes)] if h == 0 else []
            if h >= 2:
                w.append(("dv", attn_tail[h - 2]))
            mm_sc = op("tensor", (lambda hh, sp: (lambda eng: eng.matmul(
                sp[0:64, 0:128], lqb[:, hh * L:(hh + 1) * L],
                k_sb[:, hh * 128:(hh + 1) * 128], start=True, stop=True)))(h, sps),
                waits=w)

            def smax(hh, sp):
                def f(eng):
                    eng.scalar_tensor_tensor(smt, sp[0:64, 0:128], SCALE, mskt[:, :],
                                             Alu.mult, Alu.add)
                    eng.reduce_max(sc_mx[:, :], smt, Ax.X)
                    eng.drain()
                    return eng.tensor_scalar(sc_nm[:, :], sc_mx[:, :], -1.0, None, Alu.mult)
                return f
            c_sm = op("vector", smax(h, sps), waits=[("mm", mm_sc)])
            a_exp = op("scalar", lambda eng: eng.activation(ewt, smt, Act.Exp,
                                                            bias=sc_nm[:, :]),
                       waits=[("dv", c_sm)])

            def norm(eng):
                eng.reduce_sum(sc_s[:, :], ewt, Ax.X)
                eng.drain()
                return eng.reciprocal(sc_r[:, :], sc_s[:, :])
            c_nrm = op("vector", norm, waits=[("ac", a_exp)])
            a_wn = op("scalar", lambda eng: eng.activation(wfp, ewt, Act.Copy,
                                                           scale=sc_r[:, :]),
                      waits=[("dv", c_nrm)])
            tcol = (h % 2) * 512
            mm_t = op("tensor", (lambda tc: (lambda eng: eng.matmul(
                ps_big[0:128, tc:tc + 64], wfp, idft[0:64, 0:64], start=True, stop=True,
                is_transpose=True)))(tcol), waits=[("ac", a_wn)])
            c_wt = op("vector", (lambda hh, tc: (lambda eng: eng.tensor_copy(
                wT_sb[:, (hh % 2) * 64:(hh % 2 + 1) * 64], ps_big[0:128, tc:tc + 64])))(h, tcol),
                waits=[("mm", mm_t)])
            acol = 1024 + (h % 2) * 512
            mm_a = op("tensor", (lambda hh, cl: (lambda eng: eng.matmul(
                ps_big[0:64, cl:cl + 128], wT_sb[:, (hh % 2) * 64:(hh % 2 + 1) * 64],
                v_sb[:, hh * 128:(hh + 1) * 128], start=True, stop=True)))(h, acol),
                waits=[("dv", c_wt)])
            if h % 2 == 0:
                c_ae = op("vector", (lambda hh, cl: (lambda eng: eng.tensor_copy(
                    attn_sb[0:64, (hh // 2) * 128:(hh // 2 + 1) * 128],
                    ps_big[0:64, cl:cl + 128])))(h, acol), waits=[("mm", mm_a)])
            else:
                c_ae = op("vector", (lambda cl: (lambda eng: eng.tensor_copy(
                    atmp[:, :], ps_big[0:64, cl:cl + 128])))(acol),
                    waits=[("mm", mm_a)] + ([shift_dmas[-1]] if shift_dmas else []))
                shift_dmas.append(
                    dma("gpsimd", attn_sb[64:128, (h // 2) * 128:(h // 2 + 1) * 128],
                        atmp[:, :], waits=[("dv", c_ae)]))
            attn_tail.append(c_ae)
        c_attn = p.at("dv")
        g_attn = list(shift_dmas)

        for H in range(2):
            wg_cast = []
            for kc in range(KC):
                if kc < 2:
                    w = [("dv", p.at("dv"))] if H == 0 else [("mm", p.at("mm"))]
                else:
                    w = [("dv", wg_cast[kc - 2])]
                d = dma("sync", (wfa, wfb)[kc % 2][:, 0:512],
                        wg[kc * 128:(kc + 1) * 128, H * 512:(H + 1) * 512], waits=w)
                cv = op("vector", (lambda k2: (lambda eng: eng.tensor_copy(
                    big[:, k2 * 512:(k2 + 1) * 512],
                    (wfa, wfb)[k2 % 2][:, 0:512])))(kc), waits=[d])
                wg_cast.append(cv)
            c_wgh = p.at("dv")

            gt_done = []
            for t in range(8):
                gps = (ps_b2, ps_b3)[t % 2]

                def gmm(tt, gp):
                    def f(eng):
                        r = None
                        for kc in range(KC):
                            r = eng.matmul(gp[:, 0:512],
                                           xs_r[:, kc * NT + tt * 128:kc * NT + (tt + 1) * 128],
                                           big[:, kc * 512:(kc + 1) * 512],
                                           start=(kc == 0), stop=(kc == KC - 1))
                        return r
                    return f
                w = [("dv", c_wgh)]
                if t >= 2:
                    w.append(("dv", gt_done[t - 2]))
                elif H == 0:
                    w.append(("dv", c_attn))
                mm_g = op("tensor", gmm(t, gps), waits=w)

                def gsm1(gp):
                    def f(eng):
                        eng.tensor_reduce(mx8[:, 0:8].rearrange("p (s o) -> p s o", o=1),
                                          gp[:, 0:512].rearrange("p (s t) -> p s t", s=8),
                                          Ax.X, Alu.max)
                        eng.drain()
                        return eng.tensor_scalar(nm8[:, 0:8], mx8[:, 0:8], -SCALE, None,
                                                 Alu.mult)
                    return f
                c_g1 = op("vector", gsm1(gps), waits=[("mm", mm_g)])

                def gexp(gp):
                    def f(eng):
                        r = None
                        for s in range(8):
                            r = eng.activation(scr1[:, s * 64:(s + 1) * 64],
                                               gp[:, s * 64:(s + 1) * 64], Act.Exp,
                                               bias=nm8[:, s:s + 1], scale=SCALE)
                        return r
                    return f
                a_ge = op("scalar", gexp(gps), waits=[("dv", c_g1)])

                def gsum(eng):
                    eng.tensor_reduce(sm8[:, 0:8].rearrange("p (s o) -> p s o", o=1),
                                      scr1[:, 0:512].rearrange("p (s t) -> p s t", s=8),
                                      Ax.X, Alu.add)
                    eng.drain()
                    return eng.reciprocal(rs8[:, 0:8], sm8[:, 0:8])
                c_gs = op("vector", gsum, waits=[("ac", a_ge)])

                def gnorm(eng):
                    r = None
                    for s in range(8):
                        r = eng.activation(scr1[:, 512 + s * 64:512 + (s + 1) * 64],
                                           scr1[:, s * 64:(s + 1) * 64], Act.Copy,
                                           scale=rs8[:, s:s + 1])
                    return r
                a_gn = op("scalar", gnorm, waits=[("dv", c_gs)])

                last = None
                for p2 in range(4):
                    pair = H * 4 + p2
                    tcol = (p2 % 2) * 512
                    mm_tr = op("tensor", (lambda p3, tc: (lambda eng: eng.matmul(
                        ps_b1[0:128, tc:tc + 128], scr1[:, 512 + p3 * 128:512 + (p3 + 1) * 128],
                        idft[:, :], start=True, stop=True, is_transpose=True)))(p2, tcol),
                        waits=[("ac", a_gn)] if p2 == 0 else [("dv", p.at("dv"))])
                    last = op("vector", (lambda pr, tc, tt: (lambda eng: eng.tensor_copy(
                        gFM[:, pr * NT + tt * 128:pr * NT + (tt + 1) * 128],
                        ps_b1[0:128, tc:tc + 128])))(pair, tcol, t),
                        waits=[("mm", mm_tr)])
                gt_done.append(last)
        c_gates = p.at("dv")

        mix_evac = []
        for h in range(HC):
            pair, row = h // 2, (h % 2) * 64
            for half in range(2):
                mps = (ps_b2, ps_b3)[half]
                w = []
                if h == 0 and half == 0:
                    w = [("dv", c_gates)] + g_attn
                if len(mix_evac) >= 2:
                    w.append(("ac", mix_evac[-2]))

                def mmix(hh, pr, rw, hf, mp):
                    def f(eng):
                        return eng.matmul(
                            mp[:, 0:512], attn_sb[rw:rw + 64, pr * 128:(pr + 1) * 128],
                            gFM[rw:rw + 64, pr * NT + hf * NH:pr * NT + (hf + 1) * NH],
                            start=True, stop=True)
                    return f
                mm_m = op("tensor", mmix(h, pair, row, half, mps), waits=w)
                a_me = op("scalar", (lambda hh, hf, mp: (lambda eng: eng.activation(
                    big[:, hh * NT + hf * NH:hh * NT + (hf + 1) * NH], mp[:, 0:512],
                    Act.Copy)))(h, half, mps), waits=[("mm", mm_m)])
                mix_evac.append(a_me)
        a_mix = p.at("ac")

        mm_p = 0
        for oc in range(KC):
            wqv, cv = stream_slab(wp[oc * 128:(oc + 1) * 128, :], 2048, oc % 2, False,
                                  extra_waits=[("ac", a_mix)] if oc < 2 else ())
            for half in range(2):
                pps = (ps_b2, ps_b3)[half]

                def pmm(wt, hf, pp):
                    def f(eng):
                        r = None
                        for kc in range(KC):
                            r = eng.matmul(pp[:, 0:512], wt[:, kc * 128:(kc + 1) * 128],
                                           big[:, kc * NT + hf * NH:kc * NT + (hf + 1) * NH],
                                           start=(kc == 0), stop=(kc == KC - 1))
                        return r
                    return f
                w = [("dv", cv)]
                if oc > 0 or half > 0:
                    w.append(("dv", p.at("dv")))
                mm_p = op("tensor", pmm(wqv, half, pps), waits=w)
                op("vector", (lambda o2, hf, pp: (lambda eng: eng.tensor_tensor(
                    x_r[:, o2 * NT + hf * NH:o2 * NT + (hf + 1) * NH],
                    x_r[:, o2 * NT + hf * NH:o2 * NT + (hf + 1) * NH],
                    pp[:, 0:512], Alu.add)))(oc, half, pps), waits=[("mm", mm_p)])
            slab_used(oc % 2, mm_p)
        c_wp = p.at("dv")

        mm_d = 0
        for g in range(NG):
            for j in range(GSZ):
                ic = g * GSZ + j
                wqg, cvg = stream_slab(wgate[ic * 128:(ic + 1) * 128, :], 2048, 0, True,
                                       extra_waits=[("dv", c_wp)] if (g == 0 and j == 0) else ())

                def gmm2(wt):
                    def f(eng):
                        r = None
                        for half in range(2):
                            for kc in range(KC):
                                r = eng.matmul(ps_big[:, half * NH:(half + 1) * NH],
                                               wt[:, kc * 128:(kc + 1) * 128],
                                               xs_r[:, kc * NT + half * NH:kc * NT + (half + 1) * NH],
                                               start=(kc == 0), stop=(kc == KC - 1))
                        return r
                    return f
                w = [("dv", cvg)]
                if not (g == 0 and j == 0):
                    w.append(("ac", p.at("ac")))
                mm_gg = op("tensor", gmm2(wqg), waits=w)
                slab_used(0, mm_gg)

                wqu, cvu = stream_slab(wup[ic * 128:(ic + 1) * 128, :], 2048, 1, True)

                def umm(wt):
                    def f(eng):
                        r = None
                        for half in range(2):
                            for kc in range(KC):
                                r = eng.matmul(ps_big[:, 1024 + half * NH:1024 + (half + 1) * NH],
                                               wt[:, kc * 128:(kc + 1) * 128],
                                               xs_r[:, kc * NT + half * NH:kc * NT + (half + 1) * NH],
                                               start=(kc == 0), stop=(kc == KC - 1))
                        return r
                    return f
                mm_uu = op("tensor", umm(wqu), waits=[("dv", cvu)])
                slab_used(1, mm_uu)

                for half in range(2):
                    def epi(icc, hf):
                        def f(eng):
                            eng.activation(scr1[:, 0:NH], ps_big[:, hf * NH:(hf + 1) * NH],
                                           Act.Silu, scale=gs_t[:, icc:icc + 1])
                            return eng.activation(scr1[:, NH:NT],
                                                  ps_big[:, 1024 + hf * NH:1024 + (hf + 1) * NH],
                                                  Act.Copy, scale=us_t[:, icc:icc + 1])
                        return f
                    w_ep = [("mm", mm_uu)]
                    if half == 1:
                        w_ep.append(("dv", p.at("dv")))
                    a_ep = op("scalar", epi(ic, half), waits=w_ep)
                    op("vector", (lambda jj, hf: (lambda eng: eng.tensor_tensor(
                        big[:, jj * NT + hf * NH:jj * NT + (hf + 1) * NH],
                        scr1[:, 0:NH], scr1[:, NH:NT], Alu.mult)))(j, half),
                        waits=[("ac", a_ep)])
            c_fuse = p.at("dv")

            for oc in range(KC):
                wqd, cvd = stream_slab(
                    wdown[(g * 16 + oc) * 128:(g * 16 + oc + 1) * 128, :],
                    GSZ * 128, oc % 2, True,
                    extra_waits=[("dv", c_fuse)] if oc < 2 else ())
                for half in range(2):
                    def dmmf(wt, hf):
                        def f(eng):
                            r = None
                            for j in range(GSZ):
                                r = eng.matmul(ps_b1[:, hf * NH:(hf + 1) * NH],
                                               wt[:, j * 128:(j + 1) * 128],
                                               big[:, j * NT + hf * NH:j * NT + (hf + 1) * NH],
                                               start=(j == 0), stop=(j == GSZ - 1))
                            return r
                        return f
                    w = [("dv", cvd)]
                    if oc > 0 or half > 0 or g > 0:
                        w.append(("dv", p.at("dv")))
                    mm_d = op("tensor", dmmf(wqd, half), waits=w)
                    a_de = op("scalar", (lambda o2, hf: (lambda eng: eng.activation(
                        scr1[:, hf * NH:(hf + 1) * NH], ps_b1[:, hf * NH:(hf + 1) * NH],
                        Act.Copy, scale=ds_t[:, o2:o2 + 1])))(oc, half),
                        waits=[("mm", mm_d)])
                    op("vector", (lambda o2, hf: (lambda eng: eng.tensor_tensor(
                        x_r[:, o2 * NT + hf * NH:o2 * NT + (hf + 1) * NH],
                        x_r[:, o2 * NT + hf * NH:o2 * NT + (hf + 1) * NH],
                        scr1[:, hf * NH:(hf + 1) * NH], Alu.add)))(oc, half),
                        waits=[("ac", a_de)])
                slab_used(oc % 2, mm_d)
        c_ffn = p.at("dv")

        st_dmas = []
        for c in range(KC):
            st_dmas.append(dma("gpsimd", out[c * 128:(c + 1) * 128, :],
                               x_r[:, c * NT:(c + 1) * NT],
                               waits=[("dv", c_ffn)] if c == 0 else ()))
        final_waits = {}
        for (ln, v) in st_dmas:
            final_waits[ln] = max(final_waits.get(ln, 0), v)
        p.add("gpsimd", lambda eng: None, waits=sorted(final_waits.items()))

        sems = {}
        with contextlib.ExitStack() as sst:
            for s in (["dv", "ac", "mm"] + QLANES["sync"] + QLANES["gpsimd"]):
                sems[s] = sst.enter_context(nc.semaphore(s))
            _SER_OPS = {"tensor_scalar", "tensor_tensor", "scalar_tensor_tensor",
                        "tensor_copy", "tensor_reduce", "reduce_max", "reduce_sum",
                        "reciprocal", "memset", "activation", "select",
                        "copy_predicated", "tensor_add", "tensor_sub", "tensor_mul",
                        "transpose"}

            class _Ser:
                def __init__(self, eng):
                    self._eng = eng
                    self._any = False

                def __getattr__(self, name):
                    a = getattr(self._eng, name)
                    if name in _SER_OPS:
                        def w(*args, **kw):
                            if self._any:
                                self._eng.drain()
                            self._any = True
                            return a(*args, **kw)
                        return w
                    return a

            with nc.Block() as block:
                def make(eng_name):
                    def stream(eng):
                        eng2 = _Ser(eng) if eng_name in ("vector", "scalar") else eng
                        lastw = {}
                        for (e, waits, fn, incs) in p.ops:
                            if e != eng_name:
                                continue
                            for (s, v) in waits:
                                if v > 0 and lastw.get(s, -1) < v:
                                    eng.wait_ge(sems[s], v)
                                    lastw[s] = v
                            instr = fn(eng2)
                            for (s, amt) in incs:
                                instr.then_inc(sems[s], amt)
                    return stream
                block.sync(make("sync"))
                block.gpsimd(make("gpsimd"))
                block.scalar(make("scalar"))
                block.vector(make("vector"))
                block.tensor(make("tensor"))
    return nc


def _host_layouts(inputs):
    x = np.asarray(inputs["L0_attn_x"], dtype=np.float32)
    Wk = np.asarray(inputs["L0_attn_Wk"], dtype=np.float32)
    Wv = np.asarray(inputs["L0_attn_Wv"], dtype=np.float32)
    Wg = np.asarray(inputs["L0_attn_Wg"], dtype=np.float32)
    Wp = np.asarray(inputs["L0_attn_Wp"], dtype=np.float32)
    gw = np.asarray(inputs["L0_ffn_gate_w"], dtype=np.float32)
    uw = np.asarray(inputs["L0_ffn_up_w"], dtype=np.float32)
    dw = np.asarray(inputs["L0_ffn_down_w"], dtype=np.float32)
    gs = np.asarray(inputs["L0_ffn_gate_scale"], dtype=np.float32)
    us = np.asarray(inputs["L0_ffn_up_scale"], dtype=np.float32)
    ds = np.asarray(inputs["L0_ffn_down_scale"], dtype=np.float32)
    lq = np.asarray(inputs["L0_attn_latent_queries"], dtype=np.float32)

    perm_d = np.concatenate([np.arange(0, 128, 2), np.arange(1, 128, 2)])

    Wk_p = Wk.reshape(C, HC, D)[:, :, perm_d].reshape(C, C)
    wk_r = np.ascontiguousarray(
        Wk_p.reshape(KC, 128, HC, D).transpose(2, 1, 0, 3).reshape(C, C))
    wv_r = np.ascontiguousarray(
        Wv.reshape(KC, 128, HC, D).transpose(2, 1, 0, 3).reshape(C, C))
    wp_r = np.ascontiguousarray(
        Wp.reshape(KC, 128, KC, 128).transpose(2, 1, 0, 3).reshape(C, C))
    wgate_r = np.ascontiguousarray(
        gw.reshape(KC, 128, IC, 128).transpose(2, 1, 0, 3).reshape(IC * 128, C))
    wup_r = np.ascontiguousarray(
        uw.reshape(KC, 128, IC, 128).transpose(2, 1, 0, 3).reshape(IC * 128, C))
    dwr = dw.reshape(NG, GSZ, 128, KC, 128).transpose(0, 3, 2, 1, 4)
    wdown_r = np.ascontiguousarray(dwr.reshape(NG * KC * 128, GSZ * 128))
    lq_r = np.ascontiguousarray(
        lq[0].transpose(1, 2, 0)[:, perm_d, :].transpose(1, 0, 2).reshape(128, HC * L))

    import jax
    import jax.numpy as jnp
    cpu = jax.devices("cpu")[0]
    with jax.default_device(cpu):
        invf = 1.0 / 10000.0 ** (jnp.arange(0, 64, dtype=jnp.float32) * 2.0 / 128.0)
        ang = jnp.arange(128, dtype=jnp.float32)[:, None] * invf[None, :]
        cosv = np.asarray(jnp.cos(ang), dtype=np.float32)
        sinv = np.asarray(jnp.sin(ang), dtype=np.float32)
    rCh = np.empty((128, 128), np.float32)
    rSh = np.empty((128, 128), np.float32)
    rCh[0:64, :] = cosv.T
    rCh[64:128, :] = cosv.T
    rSh[0:64, :] = -sinv.T
    rSh[64:128, :] = sinv.T

    maska = np.where(np.arange(128)[None, :] <= np.arange(64)[:, None],
                     np.float32(0.0), np.float32(NEG)).astype(np.float32)
    idf = np.eye(128, dtype=np.float32)
    gs_r = np.ascontiguousarray(gs.reshape(IC, 128).T)
    us_r = np.ascontiguousarray(us.reshape(IC, 128).T)
    ds_r = np.ascontiguousarray(ds.reshape(KC, 128).T)

    shared = dict(wk=wk_r, wv=wv_r, wg=np.ascontiguousarray(Wg), wp=wp_r,
                  wgate=wgate_r, wup=wup_r, wdown=wdown_r, lq=lq_r,
                  ropec=rCh, ropes=rSh, maska=maska, idf=idf,
                  gs=gs_r, us=us_r, ds=ds_r)
    in_maps = []
    for c in range(8):
        b, q = c // 4, c % 4
        m = dict(shared)
        m["xT"] = np.ascontiguousarray(x[b, q * NT:(q + 1) * NT, :].T)
        m["xhT"] = np.ascontiguousarray(x[b, 0:128, :].T)
        in_maps.append(m)
    return in_maps


def kernel(**inputs):
    in_maps = _host_layouts(inputs)
    nc = build()
    res = run_bass_kernel_spmd(nc, in_maps, core_ids=list(range(8)))
    x = np.asarray(inputs["L0_attn_x"])
    outp = np.empty((2, 4096, C), dtype=np.float32)
    for c in range(8):
        b, q = c // 4, c % 4
        outp[b, q * NT:(q + 1) * NT, :] = np.asarray(res.results[c]["out"]).T
    return outp


# version 06
# speedup vs baseline: 1.2623x; 1.2623x over previous
import numpy as np
import ml_dtypes
import concourse.bass as bass
import concourse.mybir as mybir
from concourse.bass_utils import run_bass_kernel_spmd

F32 = mybir.dt.float32
BF16 = mybir.dt.bfloat16
FP8 = mybir.dt.float8e4
Alu = mybir.AluOpType
Act = mybir.ActivationFunctionType
Ax = mybir.AxisListType

C = 2048
NT = 1024
NH = 512
KC = 16
HC = 16
D = 128
L = 64
IC = 48
GSZ = 12
NG = IC // GSZ
SCALE = float(1.0 / np.sqrt(128.0))
NEG = -1.0e9
EPS = 1e-5


class Plan:
    def __init__(self):
        self.ops = []
        self.cnt = {}

    def bump(self, sem, amt):
        self.cnt[sem] = self.cnt.get(sem, 0) + amt
        return self.cnt[sem]

    def at(self, sem):
        return self.cnt.get(sem, 0)

    def add(self, eng, fn, waits=(), inc=None):
        incs = []
        tgt = None
        if inc is not None:
            sem, amt = inc
            tgt = self.bump(sem, amt)
            incs = [(sem, amt)]
        self.ops.append((eng, tuple(waits), fn, tuple(incs)))
        return tgt


ESEM = {"vector": "dv", "scalar": "ac", "tensor": "mm"}
NLANES = 4
QLANES = {"sync": ["sq%d" % i for i in range(NLANES)],
          "gpsimd": ["gq%d" % i for i in range(NLANES)]}


def build():
    nc = bass.Bass("TRN2", target_bir_lowering=False)
    p = Plan()

    def dram_in(name, shape, dt=F32):
        return nc.dram_tensor(name, shape, dt, kind="ExternalInput").ap()

    xT = dram_in("xT", [C, NT])
    xhT = dram_in("xhT", [C, 128])
    wk = dram_in("wk", [C, C])
    wv = dram_in("wv", [C, C])
    wg = dram_in("wg", [C, HC * L])
    wp = dram_in("wp", [C, C])
    wgate = dram_in("wgate", [IC * 128, C])
    wup = dram_in("wup", [IC * 128, C])
    wdown = dram_in("wdown", [NG * 16 * 128, GSZ * 128])
    lqd = dram_in("lq", [128, HC * L])
    ropec = dram_in("ropec", [128, 128])
    ropes = dram_in("ropes", [128, 128])
    maska = dram_in("maska", [64, 128])
    idf = dram_in("idf", [128, 128])
    gsd = dram_in("gs", [128, IC])
    usd = dram_in("us", [128, IC])
    dsd = dram_in("ds", [128, KC])
    out = nc.dram_tensor("out", [C, NT], F32, kind="ExternalOutput").ap()

    import contextlib
    st = contextlib.ExitStack()
    sb = lambda name, shape, dt: st.enter_context(nc.sbuf_tensor(name, shape, dt))
    psb = lambda name, shape, dt: st.enter_context(nc.psum_tensor(name, shape, dt))

    with st:
        x_r = sb("x_r", [128, KC * NT], F32)
        xs_r = sb("xs_r", [128, KC * NT], BF16)
        xh_r = sb("xh_r", [128, KC * 128], F32)
        xsh_r = sb("xsh_r", [128, KC * 128], BF16)
        big = sb("big", [128, KC * NT], BF16)
        gFM = sb("gFM", [128, 8 * NT], BF16)
        wfa = sb("wfa", [128, 2048], F32)
        wfb = sb("wfb", [128, 2048], F32)
        wqa = sb("wqa", [128, 2048], BF16)
        wqb = sb("wqb", [128, 2048], BF16)
        qi32a = sb("qi32a", [128, 1024], mybir.dt.int16)
        qi32b = sb("qi32b", [128, 1024], mybir.dt.int16)
        scr1 = sb("scr1", [128, NT], F32)
        scr3 = sb("scr3", [128, 128], F32)
        scr4 = sb("scr4", [128, 128], F32)
        lqb = sb("lqb", [128, HC * L], BF16)
        rC = sb("rC", [128, 128], F32)
        rS = sb("rS", [128, 128], F32)
        mskt = sb("mskt", [64, 128], F32)
        idft = sb("idft", [128, 128], F32)
        ksw = sb("ksw", [128, 128], F32)
        k_sb = sb("k_sb", [128, HC * 128], BF16)
        v_sb = sb("v_sb", [128, HC * 128], BF16)

        wT_sb = sb("wT_sb", [128, 2 * 64], BF16)
        attn_sb = sb("attn_sb", [128, 8 * 128], BF16)
        atmp = sb("atmp", [64, 128], BF16)
        vech = sb("vech", [4, 128], F32)
        vec_hm, vec_hr, vec_h1, vec_h2 = (vech[0:1, :], vech[1:2, :],
                                          vech[2:3, :], vech[3:4, :])
        ones_c = sb("ones_c", [128, 1], F32)
        eps_t = sb("eps_t", [128, 1], F32)
        ones_r2 = sb("ones_r2", [1, 128], F32)
        ones_r = ones_r2[0:1, :]
        gs_t = sb("gs_t", [128, IC], F32)
        us_t = sb("us_t", [128, IC], F32)
        ds_t = sb("ds_t", [128, KC], F32)
        mx8 = sb("mx8", [128, 8], F32)
        nm8 = sb("nm8", [128, 8], F32)
        sm8 = sb("sm8", [128, 8], F32)
        rs8 = sb("rs8", [128, 8], F32)
        sc_mx = sb("sc_mx", [64, 1], F32)
        sc_nm = sb("sc_nm", [64, 1], F32)
        sc_s = sb("sc_s", [64, 1], F32)
        sc_r = sb("sc_r", [64, 1], F32)

        kt1 = scr3
        smt = scr3[0:64, :]
        ewt = scr4[0:64, :]
        wfp = ksw[0:64, :]
        ps_big = psb("ps_big", [128, 2048], F32)
        ps_b1 = psb("ps_b1", [128, 1024], F32)
        ps_b2 = psb("ps_b2", [128, 512], F32)
        ps_b3 = psb("ps_b3", [128, 512], F32)

        lane_idx = {"sync": 0, "gpsimd": 0}

        def dma(q, out_ap, in_ap, waits=()):
            lane = QLANES[q][lane_idx[q] % NLANES]
            lane_idx[q] += 1
            w = list(waits)
            if p.at(lane) > 0:
                w.append((lane, p.at(lane)))
            fn = (lambda o, i: (lambda eng: eng.dma_start(out=o, in_=i)))(out_ap, in_ap)
            cnt = p.add(q, fn, waits=w, inc=(lane, 16))
            return (lane, cnt)

        def op(eng, fn, waits=()):
            return p.add(eng, fn, waits=waits, inc=(ESEM[eng], 1))

        d_lq = dma("gpsimd", wfa[:, 0:HC * L], lqd[:, :])
        dma("gpsimd", rC[:, :], ropec[:, :])
        dma("gpsimd", rS[:, :], ropes[:, :])
        dma("gpsimd", mskt[:, :], maska[:, :])
        dma("gpsimd", idft[:, :], idf[:, :])
        dma("gpsimd", gs_t[:, :], gsd[:, :])
        dma("gpsimd", us_t[:, :], usd[:, :])
        dma("gpsimd", ds_t[:, :], dsd[:, :])
        d_xh = [dma("sync", xh_r[:, c * 128:(c + 1) * 128], xhT[c * 128:(c + 1) * 128, :])
                for c in range(KC)]
        d_x = [dma("sync", x_r[:, c * NT:(c + 1) * NT], xT[c * 128:(c + 1) * 128, :])
               for c in range(KC)]

        op("vector", lambda eng: eng.memset(ones_c[:, :], 1.0))
        op("vector", lambda eng: eng.memset(ones_r, 1.0))
        op("vector", lambda eng: eng.memset(eps_t[:, :], EPS))
        c_ones = p.at("dv")
        op("vector", lambda eng: eng.tensor_copy(lqb[:, :], wfa[:, 0:HC * L]),
           waits=[d_lq])
        slab0_init = p.at("dv")

        sq_done = []
        for c in range(KC):
            w = [d_x[c]]
            if c >= 1:
                w.append(("mm", sq_done[c - 1]))
            scr = scr1
            op("scalar", (lambda cc, s: (lambda eng: eng.activation(
                s[:, :], x_r[:, cc * NT:(cc + 1) * NT], Act.Square)))(c, scr), waits=w)
            a_sq = p.at("ac")
            last = None
            for h in range(2):
                def mmfn(cc, s, hh):
                    def f(eng):
                        eng.matmul(ps_big[0:1, hh * NH:(hh + 1) * NH], ones_c[:, :],
                                   x_r[:, cc * NT + hh * NH: cc * NT + (hh + 1) * NH],
                                   start=(cc == 0), stop=(cc == KC - 1))
                        return eng.matmul(ps_big[0:1, 1024 + hh * NH:1024 + (hh + 1) * NH],
                                          ones_c[:, :], s[:, hh * NH:(hh + 1) * NH],
                                          start=(cc == 0), stop=(cc == KC - 1))
                    return f
                wmm = [("ac", a_sq)] + ([("dv", c_ones)] if c == 0 else [])
                last = op("tensor", mmfn(c, scr, h), waits=wmm)
            sq_done.append(last)
        mm_stats = p.at("mm")

        hs_done = []
        for c in range(KC):
            w = [d_xh[c]]
            if c >= 1:
                w.append(("mm", hs_done[c - 1]))
            op("scalar", (lambda cc: (lambda eng: eng.activation(
                scr3[:, :], xh_r[:, cc * 128:(cc + 1) * 128], Act.Square)))(c), waits=w)
            a_sq = p.at("ac")

            def mmfn(cc):
                def f(eng):
                    eng.matmul(ps_b1[0:1, 0:128], ones_c[:, :],
                               xh_r[:, cc * 128:(cc + 1) * 128],
                               start=(cc == 0), stop=(cc == KC - 1))
                    return eng.matmul(ps_b1[0:1, 512:640], ones_c[:, :], scr3[:, :],
                                      start=(cc == 0), stop=(cc == KC - 1))
                return f
            hs_done.append(op("tensor", mmfn(c), waits=[("ac", a_sq)]))
        mm_hstats = p.at("mm")

        inv_n = 1.0 / float(C)
        A = wfa[0:1, 0:1024]
        B = scr1[0:1, 0:1024]

        op("vector", lambda eng: eng.tensor_scalar(A, ps_big[0:1, 0:1024], inv_n,
                                                   None, Alu.mult),
           waits=[("mm", mm_hstats), ("dv", slab0_init)])
        c_m = p.at("dv")

        def bcast_m(eng):
            r = None
            for h in range(2):
                r = eng.matmul(ps_big[:, h * NH:(h + 1) * NH], ones_r,
                               wfa[0:1, h * NH:(h + 1) * NH], start=True, stop=True)
            return r
        mm_bm = op("tensor", bcast_m, waits=[("dv", c_m)])

        def fin_v(eng):
            eng.tensor_scalar(B, ps_big[0:1, 1024:2048], inv_n, None, Alu.mult)
            eng.tensor_tensor(A, A, A, Alu.mult)
            eng.drain()
            return eng.tensor_tensor(B, B, A, Alu.subtract)
        op("vector", fin_v, waits=[("mm", mm_bm)])
        c_var = p.at("dv")
        op("scalar", lambda eng: eng.activation(A, B, Act.Sqrt, bias=eps_t[0:1, :]),
           waits=[("dv", c_var)])
        a_std = p.at("ac")
        op("vector", lambda eng: eng.reciprocal(B, A), waits=[("ac", a_std)])
        c_r = p.at("dv")

        def bcast_r(eng):
            r = None
            for h in range(2):
                r = eng.matmul(ps_big[:, 1024 + h * NH:1024 + (h + 1) * NH], ones_r,
                               scr1[0:1, h * NH:(h + 1) * NH], start=True, stop=True)
            return r
        mm_br = op("tensor", bcast_r, waits=[("dv", c_r)])

        A2 = vech[0:1, :]
        B2 = scr1[0:1, 0:128]
        op("vector", lambda eng: eng.tensor_scalar(A2, ps_b1[0:1, 0:128], inv_n,
                                                   None, Alu.mult),
           waits=[("mm", mm_br)])
        c_hm = p.at("dv")
        mm_bhm = op("tensor", lambda eng: eng.matmul(ps_b2[:, 0:128], ones_r, A2,
                                                     start=True, stop=True),
                    waits=[("dv", c_hm)])

        def fin_hv(eng):
            eng.tensor_scalar(B2, ps_b1[0:1, 512:640], inv_n, None, Alu.mult)
            eng.tensor_tensor(A2, A2, A2, Alu.mult)
            eng.drain()
            return eng.tensor_tensor(B2, B2, A2, Alu.subtract)
        op("vector", fin_hv, waits=[("mm", mm_bhm)])
        c_hv = p.at("dv")
        op("scalar", lambda eng: eng.activation(A2, B2, Act.Sqrt, bias=eps_t[0:1, :]),
           waits=[("dv", c_hv)])
        a_hstd = p.at("ac")
        op("vector", lambda eng: eng.reciprocal(B2, A2), waits=[("ac", a_hstd)])
        c_hr = p.at("dv")
        mm_bcast = op("tensor", lambda eng: eng.matmul(ps_b2[:, 128:256], ones_r, B2,
                                                       start=True, stop=True),
                      waits=[("dv", c_hr)])

        for c in range(KC):
            def xnfn(cc):
                def f(eng):
                    eng.drain()
                    eng.tensor_tensor(scr1[:, :], x_r[:, cc * NT:(cc + 1) * NT],
                                      ps_big[:, 0:1024], Alu.subtract)
                    eng.drain()
                    return eng.tensor_tensor(xs_r[:, cc * NT:(cc + 1) * NT], scr1[:, :],
                                             ps_big[:, 1024:2048], Alu.mult)
                return f
            op("vector", xnfn(c), waits=[("mm", mm_bcast)] if c == 0 else ())
        for c in range(KC):
            def xnhfn(cc):
                def f(eng):
                    eng.drain()
                    eng.tensor_tensor(scr3[:, :], xh_r[:, cc * 128:(cc + 1) * 128],
                                      ps_b2[:, 0:128], Alu.subtract)
                    eng.drain()
                    return eng.tensor_tensor(xsh_r[:, cc * 128:(cc + 1) * 128], scr3[:, :],
                                             ps_b2[:, 128:256], Alu.mult)
                return f
            op("vector", xnhfn(c))
        c_xn = p.at("dv")

        slab_state = {"dv": [slab0_init, 0], "mm": [0, 0]}
        QAF = [xh_r[:, 0:1024], xh_r[:, 1024:2048]]
        QI = [qi32a, qi32b]
        QB6 = []
        for s_ in range(2):
            base = s_ * 4096
            tl = [gFM[:, base + j_ * 1024: base + (j_ + 1) * 1024]
                  for j_ in range(4)]
            QB6.append(tl)
        QB6[0] += [k_sb[:, 0:1024], k_sb[:, 1024:2048]]
        QB6[1] += [v_sb[:, 0:1024], v_sb[:, 1024:2048]]
        qstate = {"d2": [0, 0], "a2": [0, 0], "nxt": 0}
        T1, T2, T3, T4 = 2.0 ** -5, 2.0 ** -4, 2.0 ** -3, 2.0 ** -2
        EPS24 = 2.0 ** -24

        def quant_piece(wf, wq, c0, w_, setid, first_waits=()):
            qaf = QAF[setid][:, 0:w_]
            qi = QI[setid][:, 0:w_]
            A, B, Cc, D, E, F = [t[:, 0:w_] for t in QB6[setid]]
            qx = wf[:, c0:c0 + w_]
            w1 = list(first_waits)
            if qstate["d2"][setid]:
                w1.append(("dv", qstate["d2"][setid]))
            a1 = op("scalar", (lambda s_, d_: (lambda eng: eng.activation(
                d_, s_, Act.Abs)))(qx, qaf), waits=w1)

            def casc(eng):
                eng.tensor_scalar(A, qaf, T1, -256.0, Alu.is_ge, Alu.mult)
                eng.tensor_scalar(B, qaf, T2, -128.0, Alu.is_ge, Alu.mult)
                eng.tensor_scalar(Cc, qaf, T3, -64.0, Alu.is_ge, Alu.mult)
                eng.tensor_scalar(D, qaf, T4, -32.0, Alu.is_ge, Alu.mult)
                eng.drain()
                eng.scalar_tensor_tensor(A, A, 512.0, B, Alu.add, Alu.add)
                eng.tensor_tensor(Cc, Cc, D, Alu.add)
                eng.drain()
                eng.tensor_tensor(B, A, Cc, Alu.add)
                eng.drain()
                eng.tensor_scalar(A, qaf, T1, 2.0 ** -9, Alu.is_ge, Alu.mult)
                eng.tensor_scalar(Cc, qaf, T2, 2.0 ** -8, Alu.is_ge, Alu.mult)
                eng.tensor_scalar(D, qaf, T3, 2.0 ** -7, Alu.is_ge, Alu.mult)
                eng.tensor_scalar(E, qaf, T4, 2.0 ** -6, Alu.is_ge, Alu.mult)
                eng.tensor_tensor(qx, qx, B, Alu.mult)
                eng.drain()
                eng.scalar_tensor_tensor(A, A, 2.0 ** -9, Cc, Alu.add, Alu.add)
                eng.tensor_tensor(D, D, E, Alu.add)
                eng.drain()
                return eng.tensor_tensor(E, A, D, Alu.add)
            d1 = op("vector", casc, waits=[("ac", a1)])

            def convs(eng):
                eng.activation(qi, qx, Act.Copy, bias=1024.0)
                eng.drain()
                eng.activation(qaf, qi, Act.Copy)
                eng.drain()
                return eng.activation(qi, qi, Act.Copy, bias=0.25, scale=0.5)
            a2 = op("scalar", convs, waits=[("dv", d1)])
            qstate["a2"][setid] = a2

            def corr(q_out):
                def f(eng):
                    eng.scalar_tensor_tensor(D, qi, -2.0, qaf, Alu.mult,
                                             Alu.add)
                    eng.scalar_tensor_tensor(qx, qaf, -1024.0, qx,
                                             Alu.add, Alu.subtract)
                    eng.drain()
                    eng.scalar_tensor_tensor(qx, D, -EPS24, qx, Alu.mult,
                                             Alu.add)
                    eng.drain()
                    eng.tensor_scalar(A, qx, -0.5, None, Alu.is_lt)
                    eng.tensor_scalar(Cc, qx, 0.5, None, Alu.is_gt)
                    eng.drain()
                    eng.scalar_tensor_tensor(qx, qaf, -1024.0, A,
                                             Alu.add, Alu.add)
                    eng.drain()
                    eng.tensor_tensor(qx, qx, Cc, Alu.subtract)
                    eng.drain()
                    return eng.tensor_tensor(q_out, qx, E, Alu.mult)
                return f
            d2 = op("vector", corr(wq[:, c0:c0 + w_]), waits=[("ac", a2)])
            qstate["d2"][setid] = d2
            return d2

        def stream_slab(src_ap, ncols, slot, quantize, extra_waits=()):
            wf = (wfa, wfb)[slot]
            wq = (wqa, wqb)[slot]
            w = list(extra_waits)
            if slab_state["dv"][slot]:
                w.append(("dv", slab_state["dv"][slot]))
            d = dma("sync", wf[:, 0:ncols], src_ap, waits=w)
            w2 = [d]
            if slab_state["mm"][slot]:
                w2.append(("mm", slab_state["mm"][slot]))
            if quantize:
                cv = 0
                c0 = 0
                while c0 < ncols:
                    w_ = min(1024, ncols - c0)
                    cv = quant_piece(wf, wq, c0, w_, qstate["nxt"] % 2,
                                     first_waits=(w2 if c0 == 0 else ()))
                    qstate["nxt"] += 1
                    c0 += w_
                slab_state["dv"][slot] = cv
                return wq, cv

            def cast(eng):
                return eng.tensor_copy(wq[:, 0:ncols], wf[:, 0:ncols])
            cv = op("vector", cast, waits=w2)
            slab_state["dv"][slot] = cv
            return wq, cv

        def slab_used(slot, mm_count):
            slab_state["mm"][slot] = mm_count

        for h in range(HC):
            wqv, cv = stream_slab(wk[h * 128:(h + 1) * 128, :], 2048, h % 2, False,
                                  extra_waits=[("mm", mm_bcast)] if h < 2 else ())

            def kmm(hh, wt):
                def f(eng):
                    r = None
                    for kc in range(KC):
                        r = eng.matmul(ps_big[:, hh * 128:(hh + 1) * 128],
                                       wt[:, kc * 128:(kc + 1) * 128],
                                       xsh_r[:, kc * 128:(kc + 1) * 128],
                                       start=(kc == 0), stop=(kc == KC - 1))
                    return r
                return f
            m = op("tensor", kmm(h, wqv), waits=[("dv", cv)])
            slab_used(h % 2, m)
        mm_k = p.at("mm")

        def vb(h):
            if h < 8:
                return ps_b1[:, h * 128:(h + 1) * 128]
            if h < 12:
                return ps_b2[:, (h - 8) * 128:(h - 7) * 128]
            return ps_b3[:, (h - 12) * 128:(h - 11) * 128]

        for h in range(HC):
            wqv, cv = stream_slab(wv[h * 128:(h + 1) * 128, :], 2048, h % 2, False)

            def vmm(hh, wt):
                def f(eng):
                    r = None
                    for kc in range(KC):
                        r = eng.matmul(vb(hh), xsh_r[:, kc * 128:(kc + 1) * 128],
                                       wt[:, kc * 128:(kc + 1) * 128],
                                       start=(kc == 0), stop=(kc == KC - 1))
                    return r
                return f
            m = op("tensor", vmm(h, wqv), waits=[("dv", cv)])
            slab_used(h % 2, m)
        mm_v = p.at("mm")

        for h in range(HC):
            op("scalar", (lambda hh: (lambda eng: eng.activation(
                v_sb[:, hh * 128:(hh + 1) * 128], vb(hh), Act.Copy)))(h),
               waits=[("mm", mm_v)] if h == 0 else ())
        a_vevac = p.at("ac")

        rope_done = []
        for h in range(HC):
            w = [("mm", mm_k), ("ac", a_vevac)] if h == 0 else [("dv", rope_done[-1])]
            a_kf = op("scalar", (lambda hh: (lambda eng: eng.activation(
                scr4[:, :], ps_big[:, hh * 128:(hh + 1) * 128], Act.Copy)))(h), waits=w)
            dsw1 = dma("gpsimd", ksw[0:64, :], scr4[64:128, :], waits=[("ac", a_kf)])
            dsw2 = dma("gpsimd", ksw[64:128, :], scr4[0:64, :])

            def rope(hh):
                def f(eng):
                    eng.drain()
                    eng.tensor_tensor(kt1[:, :], scr4[:, :], rC[:, :], Alu.mult)
                    eng.tensor_tensor(ksw[:, :], ksw[:, :], rS[:, :], Alu.mult)
                    eng.drain()
                    return eng.tensor_tensor(k_sb[:, hh * 128:(hh + 1) * 128],
                                             kt1[:, :], ksw[:, :], Alu.add)
                return f
            rope_done.append(op("vector", rope(h), waits=[dsw1, dsw2]))
        c_ropes = p.at("dv")

        attn_tail = []
        shift_dmas = []
        for h in range(HC):
            sps = (ps_b2, ps_b3)[h % 2]
            w = [("dv", c_ropes)] if h == 0 else []
            if h >= 2:
                w.append(("dv", attn_tail[h - 2]))
            mm_sc = op("tensor", (lambda hh, sp: (lambda eng: eng.matmul(
                sp[0:64, 0:128], lqb[:, hh * L:(hh + 1) * L],
                k_sb[:, hh * 128:(hh + 1) * 128], start=True, stop=True)))(h, sps),
                waits=w)

            def smax(hh, sp):
                def f(eng):
                    eng.drain()
                    eng.scalar_tensor_tensor(smt, sp[0:64, 0:128], SCALE, mskt[:, :],
                                             Alu.mult, Alu.add)
                    eng.drain()
                    eng.reduce_max(sc_mx[:, :], smt, Ax.X)
                    eng.drain()
                    return eng.tensor_scalar(sc_nm[:, :], sc_mx[:, :], -1.0, None, Alu.mult)
                return f
            c_sm = op("vector", smax(h, sps), waits=[("mm", mm_sc)])
            a_exp = op("scalar", lambda eng: eng.activation(ewt, smt, Act.Exp,
                                                            bias=sc_nm[:, :]),
                       waits=[("dv", c_sm)])

            def norm(eng):
                eng.reduce_sum(sc_s[:, :], ewt, Ax.X)
                eng.drain()
                return eng.reciprocal(sc_r[:, :], sc_s[:, :])
            c_nrm = op("vector", norm, waits=[("ac", a_exp)])
            a_wn = op("scalar", lambda eng: eng.activation(wfp, ewt, Act.Copy,
                                                           scale=sc_r[:, :]),
                      waits=[("dv", c_nrm)])
            tcol = (h % 2) * 512
            mm_t = op("tensor", (lambda tc: (lambda eng: eng.matmul(
                ps_big[0:128, tc:tc + 64], wfp, idft[0:64, 0:64], start=True, stop=True,
                is_transpose=True)))(tcol), waits=[("ac", a_wn)])
            c_wt = op("vector", (lambda hh, tc: (lambda eng: eng.tensor_copy(
                wT_sb[:, (hh % 2) * 64:(hh % 2 + 1) * 64], ps_big[0:128, tc:tc + 64])))(h, tcol),
                waits=[("mm", mm_t)])
            acol = 1024 + (h % 2) * 512
            mm_a = op("tensor", (lambda hh, cl: (lambda eng: eng.matmul(
                ps_big[0:64, cl:cl + 128], wT_sb[:, (hh % 2) * 64:(hh % 2 + 1) * 64],
                v_sb[:, hh * 128:(hh + 1) * 128], start=True, stop=True)))(h, acol),
                waits=[("dv", c_wt)])
            if h % 2 == 0:
                c_ae = op("vector", (lambda hh, cl: (lambda eng: eng.tensor_copy(
                    attn_sb[0:64, (hh // 2) * 128:(hh // 2 + 1) * 128],
                    ps_big[0:64, cl:cl + 128])))(h, acol), waits=[("mm", mm_a)])
            else:
                c_ae = op("vector", (lambda cl: (lambda eng: eng.tensor_copy(
                    atmp[:, :], ps_big[0:64, cl:cl + 128])))(acol),
                    waits=[("mm", mm_a)] + ([shift_dmas[-1]] if shift_dmas else []))
                shift_dmas.append(
                    dma("gpsimd", attn_sb[64:128, (h // 2) * 128:(h // 2 + 1) * 128],
                        atmp[:, :], waits=[("dv", c_ae)]))
            attn_tail.append(c_ae)
        c_attn = p.at("dv")
        g_attn = list(shift_dmas)

        for H in range(2):
            wg_cast = []
            for kc in range(KC):
                if kc < 2:
                    w = [("dv", p.at("dv"))] if H == 0 else [("mm", p.at("mm"))]
                else:
                    w = [("dv", wg_cast[kc - 2])]
                d = dma("sync", (wfa, wfb)[kc % 2][:, 0:512],
                        wg[kc * 128:(kc + 1) * 128, H * 512:(H + 1) * 512], waits=w)
                cv = op("vector", (lambda k2: (lambda eng: eng.tensor_copy(
                    big[:, k2 * 512:(k2 + 1) * 512],
                    (wfa, wfb)[k2 % 2][:, 0:512])))(kc), waits=[d])
                wg_cast.append(cv)
            c_wgh = p.at("dv")

            gt_done = []
            for t in range(8):
                gps = (ps_b2, ps_b3)[t % 2]

                def gmm(tt, gp):
                    def f(eng):
                        r = None
                        for kc in range(KC):
                            r = eng.matmul(gp[:, 0:512],
                                           xs_r[:, kc * NT + tt * 128:kc * NT + (tt + 1) * 128],
                                           big[:, kc * 512:(kc + 1) * 512],
                                           start=(kc == 0), stop=(kc == KC - 1))
                        return r
                    return f
                w = [("dv", c_wgh)]
                if t >= 2:
                    w.append(("dv", gt_done[t - 2]))
                elif H == 0:
                    w.append(("dv", c_attn))
                mm_g = op("tensor", gmm(t, gps), waits=w)

                def gsm1(gp):
                    def f(eng):
                        eng.tensor_reduce(mx8[:, 0:8].rearrange("p (s o) -> p s o", o=1),
                                          gp[:, 0:512].rearrange("p (s t) -> p s t", s=8),
                                          Ax.X, Alu.max)
                        eng.drain()
                        return eng.tensor_scalar(nm8[:, 0:8], mx8[:, 0:8], -SCALE, None,
                                                 Alu.mult)
                    return f
                c_g1 = op("vector", gsm1(gps), waits=[("mm", mm_g)])

                def gexp(gp):
                    def f(eng):
                        r = None
                        for s in range(8):
                            r = eng.activation(scr1[:, s * 64:(s + 1) * 64],
                                               gp[:, s * 64:(s + 1) * 64], Act.Exp,
                                               bias=nm8[:, s:s + 1], scale=SCALE)
                        return r
                    return f
                a_ge = op("scalar", gexp(gps), waits=[("dv", c_g1)])

                def gsum(eng):
                    eng.tensor_reduce(sm8[:, 0:8].rearrange("p (s o) -> p s o", o=1),
                                      scr1[:, 0:512].rearrange("p (s t) -> p s t", s=8),
                                      Ax.X, Alu.add)
                    eng.drain()
                    return eng.reciprocal(rs8[:, 0:8], sm8[:, 0:8])
                c_gs = op("vector", gsum, waits=[("ac", a_ge)])

                def gnorm(eng):
                    r = None
                    for s in range(8):
                        r = eng.activation(scr1[:, 512 + s * 64:512 + (s + 1) * 64],
                                           scr1[:, s * 64:(s + 1) * 64], Act.Copy,
                                           scale=rs8[:, s:s + 1])
                    return r
                a_gn = op("scalar", gnorm, waits=[("dv", c_gs)])

                last = None
                for p2 in range(4):
                    pair = H * 4 + p2
                    tcol = (p2 % 2) * 512
                    mm_tr = op("tensor", (lambda p3, tc: (lambda eng: eng.matmul(
                        ps_b1[0:128, tc:tc + 128], scr1[:, 512 + p3 * 128:512 + (p3 + 1) * 128],
                        idft[:, :], start=True, stop=True, is_transpose=True)))(p2, tcol),
                        waits=[("ac", a_gn)] if p2 == 0 else [("dv", p.at("dv"))])
                    last = op("vector", (lambda pr, tc, tt: (lambda eng: eng.tensor_copy(
                        gFM[:, pr * NT + tt * 128:pr * NT + (tt + 1) * 128],
                        ps_b1[0:128, tc:tc + 128])))(pair, tcol, t),
                        waits=[("mm", mm_tr)])
                gt_done.append(last)
        c_gates = p.at("dv")

        mix_evac = []
        for h in range(HC):
            pair, row = h // 2, (h % 2) * 64
            for half in range(2):
                mps = (ps_b2, ps_b3)[half]
                w = []
                if h == 0 and half == 0:
                    w = [("dv", c_gates)] + g_attn
                if len(mix_evac) >= 2:
                    w.append(("ac", mix_evac[-2]))

                def mmix(hh, pr, rw, hf, mp):
                    def f(eng):
                        return eng.matmul(
                            mp[:, 0:512], attn_sb[rw:rw + 64, pr * 128:(pr + 1) * 128],
                            gFM[rw:rw + 64, pr * NT + hf * NH:pr * NT + (hf + 1) * NH],
                            start=True, stop=True)
                    return f
                mm_m = op("tensor", mmix(h, pair, row, half, mps), waits=w)
                a_me = op("scalar", (lambda hh, hf, mp: (lambda eng: eng.activation(
                    big[:, hh * NT + hf * NH:hh * NT + (hf + 1) * NH], mp[:, 0:512],
                    Act.Copy)))(h, half, mps), waits=[("mm", mm_m)])
                mix_evac.append(a_me)
        a_mix = p.at("ac")

        mm_p = 0
        for oc in range(KC):
            wqv, cv = stream_slab(wp[oc * 128:(oc + 1) * 128, :], 2048, oc % 2, False,
                                  extra_waits=[("ac", a_mix)] if oc < 2 else ())
            for half in range(2):
                pps = (ps_b2, ps_b3)[half]

                def pmm(wt, hf, pp):
                    def f(eng):
                        r = None
                        for kc in range(KC):
                            r = eng.matmul(pp[:, 0:512], wt[:, kc * 128:(kc + 1) * 128],
                                           big[:, kc * NT + hf * NH:kc * NT + (hf + 1) * NH],
                                           start=(kc == 0), stop=(kc == KC - 1))
                        return r
                    return f
                w = [("dv", cv)]
                if oc > 0 or half > 0:
                    w.append(("dv", p.at("dv")))
                mm_p = op("tensor", pmm(wqv, half, pps), waits=w)
                op("vector", (lambda o2, hf, pp: (lambda eng: eng.tensor_tensor(
                    x_r[:, o2 * NT + hf * NH:o2 * NT + (hf + 1) * NH],
                    x_r[:, o2 * NT + hf * NH:o2 * NT + (hf + 1) * NH],
                    pp[:, 0:512], Alu.add)))(oc, half, pps), waits=[("mm", mm_p)])
            slab_used(oc % 2, mm_p)
        c_wp = p.at("dv")

        mm_d = 0
        for g in range(NG):
            for j in range(GSZ):
                ic = g * GSZ + j
                wqg, cvg = stream_slab(wgate[ic * 128:(ic + 1) * 128, :], 2048, 0, True,
                                       extra_waits=[("dv", c_wp)] if (g == 0 and j == 0) else ())

                def gmm2(wt):
                    def f(eng):
                        r = None
                        for half in range(2):
                            for kc in range(KC):
                                r = eng.matmul(ps_big[:, half * NH:(half + 1) * NH],
                                               wt[:, kc * 128:(kc + 1) * 128],
                                               xs_r[:, kc * NT + half * NH:kc * NT + (half + 1) * NH],
                                               start=(kc == 0), stop=(kc == KC - 1))
                        return r
                    return f
                w = [("dv", cvg)]
                if not (g == 0 and j == 0):
                    w.append(("ac", p.at("ac")))
                mm_gg = op("tensor", gmm2(wqg), waits=w)
                slab_used(0, mm_gg)

                wqu, cvu = stream_slab(wup[ic * 128:(ic + 1) * 128, :], 2048, 1, True)

                def umm(wt):
                    def f(eng):
                        r = None
                        for half in range(2):
                            for kc in range(KC):
                                r = eng.matmul(ps_big[:, 1024 + half * NH:1024 + (half + 1) * NH],
                                               wt[:, kc * 128:(kc + 1) * 128],
                                               xs_r[:, kc * NT + half * NH:kc * NT + (half + 1) * NH],
                                               start=(kc == 0), stop=(kc == KC - 1))
                        return r
                    return f
                mm_uu = op("tensor", umm(wqu), waits=[("dv", cvu)])
                slab_used(1, mm_uu)

                for half in range(2):
                    def epi(icc, hf):
                        def f(eng):
                            eng.activation(scr1[:, 0:NH], ps_big[:, hf * NH:(hf + 1) * NH],
                                           Act.Silu, scale=gs_t[:, icc:icc + 1])
                            return eng.activation(scr1[:, NH:NT],
                                                  ps_big[:, 1024 + hf * NH:1024 + (hf + 1) * NH],
                                                  Act.Copy, scale=us_t[:, icc:icc + 1])
                        return f
                    w_ep = [("mm", mm_uu)]
                    if half == 1:
                        w_ep.append(("dv", p.at("dv")))
                    a_ep = op("scalar", epi(ic, half), waits=w_ep)
                    op("vector", (lambda jj, hf: (lambda eng: eng.tensor_tensor(
                        big[:, jj * NT + hf * NH:jj * NT + (hf + 1) * NH],
                        scr1[:, 0:NH], scr1[:, NH:NT], Alu.mult)))(j, half),
                        waits=[("ac", a_ep)])
            c_fuse = p.at("dv")

            for oc in range(KC):
                wqd, cvd = stream_slab(
                    wdown[(g * 16 + oc) * 128:(g * 16 + oc + 1) * 128, :],
                    GSZ * 128, oc % 2, True,
                    extra_waits=[("dv", c_fuse)] if oc < 2 else ())
                for half in range(2):
                    def dmmf(wt, hf):
                        def f(eng):
                            r = None
                            for j in range(GSZ):
                                r = eng.matmul(ps_b1[:, hf * NH:(hf + 1) * NH],
                                               wt[:, j * 128:(j + 1) * 128],
                                               big[:, j * NT + hf * NH:j * NT + (hf + 1) * NH],
                                               start=(j == 0), stop=(j == GSZ - 1))
                            return r
                        return f
                    w = [("dv", cvd)]
                    if oc > 0 or half > 0 or g > 0:
                        w.append(("dv", p.at("dv")))
                    mm_d = op("tensor", dmmf(wqd, half), waits=w)
                    a_de = op("scalar", (lambda o2, hf: (lambda eng: eng.activation(
                        scr1[:, hf * NH:(hf + 1) * NH], ps_b1[:, hf * NH:(hf + 1) * NH],
                        Act.Copy, scale=ds_t[:, o2:o2 + 1])))(oc, half),
                        waits=[("mm", mm_d)])
                    op("vector", (lambda o2, hf: (lambda eng: eng.tensor_tensor(
                        x_r[:, o2 * NT + hf * NH:o2 * NT + (hf + 1) * NH],
                        x_r[:, o2 * NT + hf * NH:o2 * NT + (hf + 1) * NH],
                        scr1[:, hf * NH:(hf + 1) * NH], Alu.add)))(oc, half),
                        waits=[("ac", a_de)])
                slab_used(oc % 2, mm_d)
        c_ffn = p.at("dv")

        st_dmas = []
        for c in range(KC):
            st_dmas.append(dma("gpsimd", out[c * 128:(c + 1) * 128, :],
                               x_r[:, c * NT:(c + 1) * NT],
                               waits=[("dv", c_ffn)] if c == 0 else ()))
        final_waits = {}
        for (ln, v) in st_dmas:
            final_waits[ln] = max(final_waits.get(ln, 0), v)
        p.add("gpsimd", lambda eng: None, waits=sorted(final_waits.items()))

        sems = {}
        with contextlib.ExitStack() as sst:
            for s in (["dv", "ac", "mm"] + QLANES["sync"] + QLANES["gpsimd"]):
                sems[s] = sst.enter_context(nc.semaphore(s))
            with nc.Block() as block:
                def make(eng_name):
                    def stream(eng):
                        lastw = {}
                        for (e, waits, fn, incs) in p.ops:
                            if e != eng_name:
                                continue
                            for (s, v) in waits:
                                if v > 0 and lastw.get(s, -1) < v:
                                    eng.wait_ge(sems[s], v)
                                    lastw[s] = v
                            instr = fn(eng)
                            for (s, amt) in incs:
                                instr.then_inc(sems[s], amt)
                    return stream
                block.sync(make("sync"))
                block.gpsimd(make("gpsimd"))
                block.scalar(make("scalar"))
                block.vector(make("vector"))
                block.tensor(make("tensor"))
    return nc


def _host_layouts(inputs):
    x = np.asarray(inputs["L0_attn_x"], dtype=np.float32)
    Wk = np.asarray(inputs["L0_attn_Wk"], dtype=np.float32)
    Wv = np.asarray(inputs["L0_attn_Wv"], dtype=np.float32)
    Wg = np.asarray(inputs["L0_attn_Wg"], dtype=np.float32)
    Wp = np.asarray(inputs["L0_attn_Wp"], dtype=np.float32)
    gw = np.asarray(inputs["L0_ffn_gate_w"], dtype=np.float32)
    uw = np.asarray(inputs["L0_ffn_up_w"], dtype=np.float32)
    dw = np.asarray(inputs["L0_ffn_down_w"], dtype=np.float32)
    gs = np.asarray(inputs["L0_ffn_gate_scale"], dtype=np.float32)
    us = np.asarray(inputs["L0_ffn_up_scale"], dtype=np.float32)
    ds = np.asarray(inputs["L0_ffn_down_scale"], dtype=np.float32)
    lq = np.asarray(inputs["L0_attn_latent_queries"], dtype=np.float32)

    perm_d = np.concatenate([np.arange(0, 128, 2), np.arange(1, 128, 2)])

    Wk_p = Wk.reshape(C, HC, D)[:, :, perm_d].reshape(C, C)
    wk_r = np.ascontiguousarray(
        Wk_p.reshape(KC, 128, HC, D).transpose(2, 1, 0, 3).reshape(C, C))
    wv_r = np.ascontiguousarray(
        Wv.reshape(KC, 128, HC, D).transpose(2, 1, 0, 3).reshape(C, C))
    wp_r = np.ascontiguousarray(
        Wp.reshape(KC, 128, KC, 128).transpose(2, 1, 0, 3).reshape(C, C))
    wgate_r = np.ascontiguousarray(
        gw.reshape(KC, 128, IC, 128).transpose(2, 1, 0, 3).reshape(IC * 128, C))
    wup_r = np.ascontiguousarray(
        uw.reshape(KC, 128, IC, 128).transpose(2, 1, 0, 3).reshape(IC * 128, C))
    dwr = dw.reshape(NG, GSZ, 128, KC, 128).transpose(0, 3, 2, 1, 4)
    wdown_r = np.ascontiguousarray(dwr.reshape(NG * KC * 128, GSZ * 128))
    lq_r = np.ascontiguousarray(
        lq[0].transpose(1, 2, 0)[:, perm_d, :].transpose(1, 0, 2).reshape(128, HC * L))

    import jax
    import jax.numpy as jnp
    cpu = jax.devices("cpu")[0]
    with jax.default_device(cpu):
        invf = 1.0 / 10000.0 ** (jnp.arange(0, 64, dtype=jnp.float32) * 2.0 / 128.0)
        ang = jnp.arange(128, dtype=jnp.float32)[:, None] * invf[None, :]
        cosv = np.asarray(jnp.cos(ang), dtype=np.float32)
        sinv = np.asarray(jnp.sin(ang), dtype=np.float32)
    rCh = np.empty((128, 128), np.float32)
    rSh = np.empty((128, 128), np.float32)
    rCh[0:64, :] = cosv.T
    rCh[64:128, :] = cosv.T
    rSh[0:64, :] = -sinv.T
    rSh[64:128, :] = sinv.T

    maska = np.where(np.arange(128)[None, :] <= np.arange(64)[:, None],
                     np.float32(0.0), np.float32(NEG)).astype(np.float32)
    idf = np.eye(128, dtype=np.float32)
    gs_r = np.ascontiguousarray(gs.reshape(IC, 128).T)
    us_r = np.ascontiguousarray(us.reshape(IC, 128).T)
    ds_r = np.ascontiguousarray(ds.reshape(KC, 128).T)

    shared = dict(wk=wk_r, wv=wv_r, wg=np.ascontiguousarray(Wg), wp=wp_r,
                  wgate=wgate_r, wup=wup_r, wdown=wdown_r, lq=lq_r,
                  ropec=rCh, ropes=rSh, maska=maska, idf=idf,
                  gs=gs_r, us=us_r, ds=ds_r)
    in_maps = []
    for c in range(8):
        b, q = c // 4, c % 4
        m = dict(shared)
        m["xT"] = np.ascontiguousarray(x[b, q * NT:(q + 1) * NT, :].T)
        m["xhT"] = np.ascontiguousarray(x[b, 0:128, :].T)
        in_maps.append(m)
    return in_maps


def kernel(**inputs):
    in_maps = _host_layouts(inputs)
    nc = build()
    res = run_bass_kernel_spmd(nc, in_maps, core_ids=list(range(8)))
    x = np.asarray(inputs["L0_attn_x"])
    outp = np.empty((2, 4096, C), dtype=np.float32)
    for c in range(8):
        b, q = c // 4, c % 4
        outp[b, q * NT:(q + 1) * NT, :] = np.asarray(res.results[c]["out"]).T
    return outp


# version 08
# speedup vs baseline: 3.0572x; 2.4219x over previous
import numpy as np
import ml_dtypes
import concourse.bass as bass
import concourse.mybir as mybir
from concourse.bass_utils import run_bass_kernel_spmd

F32 = mybir.dt.float32
BF16 = mybir.dt.bfloat16
FP8 = mybir.dt.float8e4
Alu = mybir.AluOpType
Act = mybir.ActivationFunctionType
Ax = mybir.AxisListType

C = 2048
NT = 1024
NH = 512
KC = 16
HC = 16
D = 128
L = 64
IC = 48
GSZ = 12
NG = IC // GSZ
SCALE = float(1.0 / np.sqrt(128.0))

QSH = 4718592
QCOLS = QSH // 128


def _build_qtab():
    tab = []
    off = 0
    for ic in range(IC):
        tab.append(["gate", ic, off, 2048])
        off += 128 * 2048
    for ic in range(IC):
        tab.append(["up", ic, off, 2048])
        off += 128 * 2048
    for gi in range(NG * 16):
        tab.append(["down", gi, off, 1536])
        off += 128 * 1536
    assert off == 8 * QSH
    out = []
    for kind, idx, off, cols in tab:
        rank = off // QSH
        assert (off + 128 * cols - 1) // QSH == rank
        col0 = (off - rank * QSH) // 128
        out.append((kind, idx, rank, col0, cols))
    return out


QTAB = _build_qtab()
QBYKIND = {(k, i): (r, c0, cs) for (k, i, r, c0, cs) in QTAB}
NEG = -1.0e9
EPS = 1e-5


class Plan:
    def __init__(self):
        self.ops = []
        self.cnt = {}

    def bump(self, sem, amt):
        self.cnt[sem] = self.cnt.get(sem, 0) + amt
        return self.cnt[sem]

    def at(self, sem):
        return self.cnt.get(sem, 0)

    def add(self, eng, fn, waits=(), inc=None):
        incs = []
        tgt = None
        if inc is not None:
            sem, amt = inc
            tgt = self.bump(sem, amt)
            incs = [(sem, amt)]
        self.ops.append((eng, tuple(waits), fn, tuple(incs)))
        return tgt


ESEM = {"vector": "dv", "scalar": "ac", "tensor": "mm"}
NLANES = 4
QLANES = {"sync": ["sq%d" % i for i in range(NLANES)],
          "gpsimd": ["gq%d" % i for i in range(NLANES)]}


def build():
    nc = bass.Bass("TRN2", target_bir_lowering=False)
    p = Plan()

    def dram_in(name, shape, dt=F32):
        return nc.dram_tensor(name, shape, dt, kind="ExternalInput").ap()

    xT = dram_in("xT", [C, NT])
    xhT = dram_in("xhT", [C, 128])
    wk = dram_in("wk", [C, C])
    wv = dram_in("wv", [C, C])
    wg = dram_in("wg", [C, HC * L])
    wp = dram_in("wp", [C, C])
    qsrc = dram_in("qsrc", [128, QCOLS])
    qshard = nc.dram_tensor("qshard", [128, QCOLS], FP8, kind="Internal").ap()
    qfull = nc.dram_tensor("qfull", [8, 128, QCOLS], FP8, kind="Internal").ap()
    lqd = dram_in("lq", [128, HC * L])
    ropec = dram_in("ropec", [128, 128])
    ropes = dram_in("ropes", [128, 128])
    maska = dram_in("maska", [64, 128])
    idf = dram_in("idf", [128, 128])
    gsd = dram_in("gs", [128, IC])
    usd = dram_in("us", [128, IC])
    dsd = dram_in("ds", [128, KC])
    out = nc.dram_tensor("out", [C, NT], F32, kind="ExternalOutput").ap()

    import contextlib
    st = contextlib.ExitStack()
    sb = lambda name, shape, dt: st.enter_context(nc.sbuf_tensor(name, shape, dt))
    psb = lambda name, shape, dt: st.enter_context(nc.psum_tensor(name, shape, dt))

    with st:
        x_r = sb("x_r", [128, KC * NT], F32)
        xs_r = sb("xs_r", [128, KC * NT], BF16)
        xh_r = sb("xh_r", [128, KC * 128], F32)
        xsh_r = sb("xsh_r", [128, KC * 128], BF16)
        big = sb("big", [128, KC * NT], BF16)
        gFM = sb("gFM", [128, 8 * NT], BF16)
        wfa = sb("wfa", [128, 2048], F32)
        wfb = sb("wfb", [128, 2048], F32)
        wqa = sb("wqa", [128, 2048], BF16)
        wqb = sb("wqb", [128, 2048], BF16)
        qi32a = sb("qi32a", [128, 512], mybir.dt.int16)
        qi32b = sb("qi32b", [128, 512], mybir.dt.int16)
        f8x = sb("f8x", [128, 2048], FP8)
        scr1 = sb("scr1", [128, NT], F32)
        scr3 = sb("scr3", [128, 128], F32)
        scr4 = sb("scr4", [128, 128], F32)
        lqb = sb("lqb", [128, HC * L], BF16)
        rC = sb("rC", [128, 128], F32)
        rS = sb("rS", [128, 128], F32)
        mskt = sb("mskt", [64, 128], F32)
        idft = sb("idft", [128, 128], F32)
        ksw = sb("ksw", [128, 128], F32)
        k_sb = sb("k_sb", [128, HC * 128], BF16)
        v_sb = sb("v_sb", [128, HC * 128], BF16)

        wT_sb = sb("wT_sb", [128, 2 * 64], BF16)
        attn_sb = sb("attn_sb", [128, 8 * 128], BF16)
        atmp = sb("atmp", [64, 128], BF16)
        vech = sb("vech", [4, 128], F32)
        vec_hm, vec_hr, vec_h1, vec_h2 = (vech[0:1, :], vech[1:2, :],
                                          vech[2:3, :], vech[3:4, :])
        ones_c = sb("ones_c", [128, 1], F32)
        eps_t = sb("eps_t", [128, 1], F32)
        ones_r2 = sb("ones_r2", [1, 128], F32)
        ones_r = ones_r2[0:1, :]
        gs_t = sb("gs_t", [128, IC], F32)
        us_t = sb("us_t", [128, IC], F32)
        ds_t = sb("ds_t", [128, KC], F32)
        mx8 = sb("mx8", [128, 8], F32)
        nm8 = sb("nm8", [128, 8], F32)
        sm8 = sb("sm8", [128, 8], F32)
        rs8 = sb("rs8", [128, 8], F32)
        sc_mx = sb("sc_mx", [64, 1], F32)
        sc_nm = sb("sc_nm", [64, 1], F32)
        sc_s = sb("sc_s", [64, 1], F32)
        sc_r = sb("sc_r", [64, 1], F32)

        kt1 = scr3
        smt = scr3[0:64, :]
        ewt = scr4[0:64, :]
        wfp = ksw[0:64, :]
        ps_big = psb("ps_big", [128, 2048], F32)
        ps_b1 = psb("ps_b1", [128, 1024], F32)
        ps_b2 = psb("ps_b2", [128, 512], F32)
        ps_b3 = psb("ps_b3", [128, 512], F32)

        lane_idx = {"sync": 0, "gpsimd": 0}

        def dma(q, out_ap, in_ap, waits=()):
            lane = QLANES[q][lane_idx[q] % NLANES]
            lane_idx[q] += 1
            w = list(waits)
            if p.at(lane) > 0:
                w.append((lane, p.at(lane)))
            fn = (lambda o, i: (lambda eng: eng.dma_start(out=o, in_=i)))(out_ap, in_ap)
            cnt = p.add(q, fn, waits=w, inc=(lane, 16))
            return (lane, cnt)

        def op(eng, fn, waits=()):
            return p.add(eng, fn, waits=waits, inc=(ESEM[eng], 1))

        d_lq = dma("gpsimd", wfa[:, 0:HC * L], lqd[:, :])
        dma("gpsimd", rC[:, :], ropec[:, :])
        dma("gpsimd", rS[:, :], ropes[:, :])
        dma("gpsimd", mskt[:, :], maska[:, :])
        dma("gpsimd", idft[:, :], idf[:, :])
        dma("gpsimd", gs_t[:, :], gsd[:, :])
        dma("gpsimd", us_t[:, :], usd[:, :])
        dma("gpsimd", ds_t[:, :], dsd[:, :])
        d_xh = [dma("sync", xh_r[:, c * 128:(c + 1) * 128], xhT[c * 128:(c + 1) * 128, :])
                for c in range(KC)]
        d_x = [dma("sync", x_r[:, c * NT:(c + 1) * NT], xT[c * 128:(c + 1) * 128, :])
               for c in range(KC)]

        op("vector", lambda eng: eng.memset(ones_c[:, :], 1.0))
        op("vector", lambda eng: eng.memset(ones_r, 1.0))
        op("vector", lambda eng: eng.memset(eps_t[:, :], EPS))
        c_ones = p.at("dv")
        op("vector", lambda eng: eng.tensor_copy(lqb[:, :], wfa[:, 0:HC * L]),
           waits=[d_lq])
        slab0_init = p.at("dv")

        sq_done = []
        for c in range(KC):
            w = [d_x[c]]
            if c >= 1:
                w.append(("mm", sq_done[c - 1]))
            scr = scr1
            op("scalar", (lambda cc, s: (lambda eng: eng.activation(
                s[:, :], x_r[:, cc * NT:(cc + 1) * NT], Act.Square)))(c, scr), waits=w)
            a_sq = p.at("ac")
            last = None
            for h in range(2):
                def mmfn(cc, s, hh):
                    def f(eng):
                        eng.matmul(ps_big[0:1, hh * NH:(hh + 1) * NH], ones_c[:, :],
                                   x_r[:, cc * NT + hh * NH: cc * NT + (hh + 1) * NH],
                                   start=(cc == 0), stop=(cc == KC - 1))
                        return eng.matmul(ps_big[0:1, 1024 + hh * NH:1024 + (hh + 1) * NH],
                                          ones_c[:, :], s[:, hh * NH:(hh + 1) * NH],
                                          start=(cc == 0), stop=(cc == KC - 1))
                    return f
                wmm = [("ac", a_sq)] + ([("dv", c_ones)] if c == 0 else [])
                last = op("tensor", mmfn(c, scr, h), waits=wmm)
            sq_done.append(last)
        mm_stats = p.at("mm")

        hs_done = []
        for c in range(KC):
            w = [d_xh[c]]
            if c >= 1:
                w.append(("mm", hs_done[c - 1]))
            op("scalar", (lambda cc: (lambda eng: eng.activation(
                scr3[:, :], xh_r[:, cc * 128:(cc + 1) * 128], Act.Square)))(c), waits=w)
            a_sq = p.at("ac")

            def mmfn(cc):
                def f(eng):
                    eng.matmul(ps_b1[0:1, 0:128], ones_c[:, :],
                               xh_r[:, cc * 128:(cc + 1) * 128],
                               start=(cc == 0), stop=(cc == KC - 1))
                    return eng.matmul(ps_b1[0:1, 512:640], ones_c[:, :], scr3[:, :],
                                      start=(cc == 0), stop=(cc == KC - 1))
                return f
            hs_done.append(op("tensor", mmfn(c), waits=[("ac", a_sq)]))
        mm_hstats = p.at("mm")

        inv_n = 1.0 / float(C)
        A = wfa[0:1, 0:1024]
        B = scr1[0:1, 0:1024]

        op("vector", lambda eng: eng.tensor_scalar(A, ps_big[0:1, 0:1024], inv_n,
                                                   None, Alu.mult),
           waits=[("mm", mm_hstats), ("dv", slab0_init)])
        c_m = p.at("dv")

        def bcast_m(eng):
            r = None
            for h in range(2):
                r = eng.matmul(ps_big[:, h * NH:(h + 1) * NH], ones_r,
                               wfa[0:1, h * NH:(h + 1) * NH], start=True, stop=True)
            return r
        mm_bm = op("tensor", bcast_m, waits=[("dv", c_m)])

        def fin_v(eng):
            eng.tensor_scalar(B, ps_big[0:1, 1024:2048], inv_n, None, Alu.mult)
            eng.tensor_tensor(A, A, A, Alu.mult)
            eng.drain()
            return eng.tensor_tensor(B, B, A, Alu.subtract)
        op("vector", fin_v, waits=[("mm", mm_bm)])
        c_var = p.at("dv")
        op("scalar", lambda eng: eng.activation(A, B, Act.Sqrt, bias=eps_t[0:1, :]),
           waits=[("dv", c_var)])
        a_std = p.at("ac")
        op("vector", lambda eng: eng.reciprocal(B, A), waits=[("ac", a_std)])
        c_r = p.at("dv")

        def bcast_r(eng):
            r = None
            for h in range(2):
                r = eng.matmul(ps_big[:, 1024 + h * NH:1024 + (h + 1) * NH], ones_r,
                               scr1[0:1, h * NH:(h + 1) * NH], start=True, stop=True)
            return r
        mm_br = op("tensor", bcast_r, waits=[("dv", c_r)])

        A2 = vech[0:1, :]
        B2 = scr1[0:1, 0:128]
        op("vector", lambda eng: eng.tensor_scalar(A2, ps_b1[0:1, 0:128], inv_n,
                                                   None, Alu.mult),
           waits=[("mm", mm_br)])
        c_hm = p.at("dv")
        mm_bhm = op("tensor", lambda eng: eng.matmul(ps_b2[:, 0:128], ones_r, A2,
                                                     start=True, stop=True),
                    waits=[("dv", c_hm)])

        def fin_hv(eng):
            eng.tensor_scalar(B2, ps_b1[0:1, 512:640], inv_n, None, Alu.mult)
            eng.tensor_tensor(A2, A2, A2, Alu.mult)
            eng.drain()
            return eng.tensor_tensor(B2, B2, A2, Alu.subtract)
        op("vector", fin_hv, waits=[("mm", mm_bhm)])
        c_hv = p.at("dv")
        op("scalar", lambda eng: eng.activation(A2, B2, Act.Sqrt, bias=eps_t[0:1, :]),
           waits=[("dv", c_hv)])
        a_hstd = p.at("ac")
        op("vector", lambda eng: eng.reciprocal(B2, A2), waits=[("ac", a_hstd)])
        c_hr = p.at("dv")
        mm_bcast = op("tensor", lambda eng: eng.matmul(ps_b2[:, 128:256], ones_r, B2,
                                                       start=True, stop=True),
                      waits=[("dv", c_hr)])

        for c in range(KC):
            def xnfn(cc):
                def f(eng):
                    eng.drain()
                    eng.tensor_tensor(scr1[:, :], x_r[:, cc * NT:(cc + 1) * NT],
                                      ps_big[:, 0:1024], Alu.subtract)
                    eng.drain()
                    return eng.tensor_tensor(xs_r[:, cc * NT:(cc + 1) * NT], scr1[:, :],
                                             ps_big[:, 1024:2048], Alu.mult)
                return f
            op("vector", xnfn(c), waits=[("mm", mm_bcast)] if c == 0 else ())
        for c in range(KC):
            def xnhfn(cc):
                def f(eng):
                    eng.drain()
                    eng.tensor_tensor(scr3[:, :], xh_r[:, cc * 128:(cc + 1) * 128],
                                      ps_b2[:, 0:128], Alu.subtract)
                    eng.drain()
                    return eng.tensor_tensor(xsh_r[:, cc * 128:(cc + 1) * 128], scr3[:, :],
                                             ps_b2[:, 128:256], Alu.mult)
                return f
            op("vector", xnhfn(c))
        c_xn = p.at("dv")

        slab_state = {"dv": [slab0_init, 0], "mm": [0, 0]}
        QSTG = [scr1[:, 0:512], scr1[:, 512:1024]]
        QAF = [xh_r[:, 0:512], xh_r[:, 512:1024]]
        QI = [qi32a, qi32b]
        QB6 = []
        for s_ in range(2):
            base = s_ * 3072
            QB6.append([gFM[:, base + j_ * 512: base + (j_ + 1) * 512]
                        for j_ in range(6)])
        qstate = {"d2": [0, 0], "nxt": 0, "f8dma": [None] * 4}
        T1, T2, T3, T4 = 2.0 ** -5, 2.0 ** -4, 2.0 ** -3, 2.0 ** -2
        EPS24 = 2.0 ** -24

        def quant_piece(pi, extra_waits=()):
            setid = pi % 2
            stg, qaf, qi = QSTG[setid], QAF[setid], QI[setid]
            A, B, Cc, D, E, F = QB6[setid]
            w0 = list(extra_waits)
            if qstate["d2"][setid]:
                w0.append(("dv", qstate["d2"][setid]))
            dsrc = dma("sync", stg, qsrc[:, pi * 512:(pi + 1) * 512], waits=w0)
            a1 = op("scalar", (lambda s_, d_: (lambda eng: eng.activation(
                d_, s_, Act.Abs)))(stg, qaf), waits=[dsrc] + list(extra_waits))

            def casc(eng):
                eng.tensor_scalar(A, qaf, T1, -256.0, Alu.is_ge, Alu.mult)
                eng.tensor_scalar(B, qaf, T2, -128.0, Alu.is_ge, Alu.mult)
                eng.tensor_scalar(Cc, qaf, T3, -64.0, Alu.is_ge, Alu.mult)
                eng.tensor_scalar(D, qaf, T4, -32.0, Alu.is_ge, Alu.mult)
                eng.drain()
                eng.scalar_tensor_tensor(A, A, 512.0, B, Alu.add, Alu.add)
                eng.tensor_tensor(Cc, Cc, D, Alu.add)
                eng.drain()
                eng.tensor_tensor(B, A, Cc, Alu.add)
                eng.drain()
                eng.tensor_scalar(A, qaf, T1, 2.0 ** -9, Alu.is_ge, Alu.mult)
                eng.tensor_scalar(Cc, qaf, T2, 2.0 ** -8, Alu.is_ge, Alu.mult)
                eng.tensor_scalar(D, qaf, T3, 2.0 ** -7, Alu.is_ge, Alu.mult)
                eng.tensor_scalar(E, qaf, T4, 2.0 ** -6, Alu.is_ge, Alu.mult)
                eng.tensor_tensor(stg, stg, B, Alu.mult)
                eng.drain()
                eng.scalar_tensor_tensor(A, A, 2.0 ** -9, Cc, Alu.add, Alu.add)
                eng.tensor_tensor(D, D, E, Alu.add)
                eng.drain()
                return eng.tensor_tensor(E, A, D, Alu.add)
            d1 = op("vector", casc, waits=[("ac", a1)])

            def convs(eng):
                eng.activation(qi[:, :], stg, Act.Copy, bias=1024.0)
                eng.drain()
                eng.activation(qaf, qi[:, :], Act.Copy)
                eng.drain()
                return eng.activation(qi[:, :], qi[:, :], Act.Copy,
                                      bias=0.25, scale=0.5)
            a2 = op("scalar", convs, waits=[("dv", d1)])

            fq = f8x[:, (pi % 4) * 512:(pi % 4 + 1) * 512]

            def corr(eng):
                eng.scalar_tensor_tensor(D, qi[:, :], -2.0, qaf, Alu.mult, Alu.add)
                eng.scalar_tensor_tensor(stg, qaf, -1024.0, stg,
                                         Alu.add, Alu.subtract)
                eng.drain()
                eng.scalar_tensor_tensor(stg, D, -EPS24, stg, Alu.mult, Alu.add)
                eng.drain()
                eng.tensor_scalar(A, stg, -0.5, None, Alu.is_lt)
                eng.tensor_scalar(Cc, stg, 0.5, None, Alu.is_gt)
                eng.drain()
                eng.scalar_tensor_tensor(stg, qaf, -1024.0, A, Alu.add, Alu.add)
                eng.drain()
                eng.tensor_tensor(stg, stg, Cc, Alu.subtract)
                eng.drain()
                return eng.tensor_tensor(fq, stg, E, Alu.mult)
            w2 = [("ac", a2)]
            if qstate["f8dma"][pi % 4] is not None:
                w2.append(qstate["f8dma"][pi % 4])
            d2 = op("vector", corr, waits=w2)
            qstate["d2"][setid] = d2
            qstate["f8dma"][pi % 4] = dma(
                "gpsimd", qshard[:, pi * 512:(pi + 1) * 512], fq,
                waits=[("dv", d2)])
            return d2

        def stream_slab(src_ap, ncols, slot, extra_waits=()):
            wf = (wfa, wfb)[slot]
            wq = (wqa, wqb)[slot]
            w = list(extra_waits)
            if slab_state["dv"][slot]:
                w.append(("dv", slab_state["dv"][slot]))
            d = dma("sync", wf[:, 0:ncols], src_ap, waits=w)
            w2 = [d]
            if slab_state["mm"][slot]:
                w2.append(("mm", slab_state["mm"][slot]))

            def cast(eng):
                return eng.tensor_copy(wq[:, 0:ncols], wf[:, 0:ncols])
            cv = op("vector", cast, waits=w2)
            slab_state["dv"][slot] = cv
            return wq, cv

        f8state = {"dv": 0, "cc": False}

        def stream_fp8(kind, idx, slot, extra_waits=()):
            rank, col0, cols = QBYKIND[(kind, idx)]
            wq = (wqa, wqb)[slot]
            w = list(extra_waits)
            if not f8state["cc"]:
                w.append(("cc", 1))
                f8state["cc"] = True
            if f8state["dv"]:
                w.append(("dv", f8state["dv"]))
            dmas = []
            c0 = 0
            while c0 < cols:
                cw = min(1024, cols - c0)
                dmas.append(dma("sync", f8x[:, c0:c0 + cw],
                                qfull[rank, :, col0 + c0:col0 + c0 + cw],
                                waits=(w if c0 == 0 else ())))
                c0 += cw
            w2 = list(dmas)
            if slab_state["mm"][slot]:
                w2.append(("mm", slab_state["mm"][slot]))

            def cast(eng):
                return eng.tensor_copy(wq[:, 0:cols], f8x[:, 0:cols])
            cv = op("vector", cast, waits=w2)
            f8state["dv"] = cv
            slab_state["dv"][slot] = cv
            return wq, cv

        def slab_used(slot, mm_count):
            slab_state["mm"][slot] = mm_count

        phq = {"next": 0}

        def phq_emit(n):
            for _ in range(n):
                if phq["next"] < 72:
                    ew = [("dv", c_xn)] if phq["next"] < 2 else ()
                    quant_piece(phq["next"], extra_waits=ew)
                    phq["next"] += 1

        for h in range(HC):
            wqv, cv = stream_slab(wk[h * 128:(h + 1) * 128, :], 2048, h % 2,
                                  extra_waits=[("mm", mm_bcast)] if h < 2 else ())

            def kmm(hh, wt):
                def f(eng):
                    r = None
                    for kc in range(KC):
                        r = eng.matmul(ps_big[:, hh * 128:(hh + 1) * 128],
                                       wt[:, kc * 128:(kc + 1) * 128],
                                       xsh_r[:, kc * 128:(kc + 1) * 128],
                                       start=(kc == 0), stop=(kc == KC - 1))
                    return r
                return f
            m = op("tensor", kmm(h, wqv), waits=[("dv", cv)])
            slab_used(h % 2, m)
            phq_emit(2)
        mm_k = p.at("mm")

        def vb(h):
            if h < 8:
                return ps_b1[:, h * 128:(h + 1) * 128]
            if h < 12:
                return ps_b2[:, (h - 8) * 128:(h - 7) * 128]
            return ps_b3[:, (h - 12) * 128:(h - 11) * 128]

        for h in range(HC):
            wqv, cv = stream_slab(wv[h * 128:(h + 1) * 128, :], 2048, h % 2)

            def vmm(hh, wt):
                def f(eng):
                    r = None
                    for kc in range(KC):
                        r = eng.matmul(vb(hh), xsh_r[:, kc * 128:(kc + 1) * 128],
                                       wt[:, kc * 128:(kc + 1) * 128],
                                       start=(kc == 0), stop=(kc == KC - 1))
                    return r
                return f
            m = op("tensor", vmm(h, wqv), waits=[("dv", cv)])
            slab_used(h % 2, m)
            phq_emit(2)
        mm_v = p.at("mm")

        for h in range(HC):
            op("scalar", (lambda hh: (lambda eng: eng.activation(
                v_sb[:, hh * 128:(hh + 1) * 128], vb(hh), Act.Copy)))(h),
               waits=[("mm", mm_v)] if h == 0 else ())
        a_vevac = p.at("ac")

        rope_done = []
        for h in range(HC):
            w = [("mm", mm_k), ("ac", a_vevac)] if h == 0 else [("dv", rope_done[-1])]
            a_kf = op("scalar", (lambda hh: (lambda eng: eng.activation(
                scr4[:, :], ps_big[:, hh * 128:(hh + 1) * 128], Act.Copy)))(h), waits=w)
            dsw1 = dma("gpsimd", ksw[0:64, :], scr4[64:128, :], waits=[("ac", a_kf)])
            dsw2 = dma("gpsimd", ksw[64:128, :], scr4[0:64, :])

            def rope(hh):
                def f(eng):
                    eng.drain()
                    eng.tensor_tensor(kt1[:, :], scr4[:, :], rC[:, :], Alu.mult)
                    eng.tensor_tensor(ksw[:, :], ksw[:, :], rS[:, :], Alu.mult)
                    eng.drain()
                    return eng.tensor_tensor(k_sb[:, hh * 128:(hh + 1) * 128],
                                             kt1[:, :], ksw[:, :], Alu.add)
                return f
            rope_done.append(op("vector", rope(h), waits=[dsw1, dsw2]))
        c_ropes = p.at("dv")

        attn_tail = []
        shift_dmas = []
        for h in range(HC):
            sps = (ps_b2, ps_b3)[h % 2]
            w = [("dv", c_ropes)] if h == 0 else []
            if h >= 2:
                w.append(("dv", attn_tail[h - 2]))
            mm_sc = op("tensor", (lambda hh, sp: (lambda eng: eng.matmul(
                sp[0:64, 0:128], lqb[:, hh * L:(hh + 1) * L],
                k_sb[:, hh * 128:(hh + 1) * 128], start=True, stop=True)))(h, sps),
                waits=w)

            def smax(hh, sp):
                def f(eng):
                    eng.drain()
                    eng.scalar_tensor_tensor(smt, sp[0:64, 0:128], SCALE, mskt[:, :],
                                             Alu.mult, Alu.add)
                    eng.drain()
                    eng.reduce_max(sc_mx[:, :], smt, Ax.X)
                    eng.drain()
                    return eng.tensor_scalar(sc_nm[:, :], sc_mx[:, :], -1.0, None, Alu.mult)
                return f
            c_sm = op("vector", smax(h, sps), waits=[("mm", mm_sc)])
            a_exp = op("scalar", lambda eng: eng.activation(ewt, smt, Act.Exp,
                                                            bias=sc_nm[:, :]),
                       waits=[("dv", c_sm)])

            def norm(eng):
                eng.reduce_sum(sc_s[:, :], ewt, Ax.X)
                eng.drain()
                return eng.reciprocal(sc_r[:, :], sc_s[:, :])
            c_nrm = op("vector", norm, waits=[("ac", a_exp)])
            a_wn = op("scalar", lambda eng: eng.activation(wfp, ewt, Act.Copy,
                                                           scale=sc_r[:, :]),
                      waits=[("dv", c_nrm)])
            tcol = (h % 2) * 512
            mm_t = op("tensor", (lambda tc: (lambda eng: eng.matmul(
                ps_big[0:128, tc:tc + 64], wfp, idft[0:64, 0:64], start=True, stop=True,
                is_transpose=True)))(tcol), waits=[("ac", a_wn)])
            c_wt = op("vector", (lambda hh, tc: (lambda eng: eng.tensor_copy(
                wT_sb[:, (hh % 2) * 64:(hh % 2 + 1) * 64], ps_big[0:128, tc:tc + 64])))(h, tcol),
                waits=[("mm", mm_t)])
            acol = 1024 + (h % 2) * 512
            mm_a = op("tensor", (lambda hh, cl: (lambda eng: eng.matmul(
                ps_big[0:64, cl:cl + 128], wT_sb[:, (hh % 2) * 64:(hh % 2 + 1) * 64],
                v_sb[:, hh * 128:(hh + 1) * 128], start=True, stop=True)))(h, acol),
                waits=[("dv", c_wt)])
            if h % 2 == 0:
                c_ae = op("vector", (lambda hh, cl: (lambda eng: eng.tensor_copy(
                    attn_sb[0:64, (hh // 2) * 128:(hh // 2 + 1) * 128],
                    ps_big[0:64, cl:cl + 128])))(h, acol), waits=[("mm", mm_a)])
            else:
                c_ae = op("vector", (lambda cl: (lambda eng: eng.tensor_copy(
                    atmp[:, :], ps_big[0:64, cl:cl + 128])))(acol),
                    waits=[("mm", mm_a)] + ([shift_dmas[-1]] if shift_dmas else []))
                shift_dmas.append(
                    dma("gpsimd", attn_sb[64:128, (h // 2) * 128:(h // 2 + 1) * 128],
                        atmp[:, :], waits=[("dv", c_ae)]))
            attn_tail.append(c_ae)
            phq_emit(1)
        c_attn = p.at("dv")
        g_attn = list(shift_dmas)
        phq_emit(72)
        ag_waits = {}
        for t_ in qstate["f8dma"]:
            if t_ is not None:
                ln, v = t_
                ag_waits[ln] = max(ag_waits.get(ln, 0), v)
        op_ag = p.add("gpsimd", lambda eng: eng.collective_compute(
            "AllGather", Alu.bypass, ins=[qshard[:, :]], outs=[qfull[:, :, :]],
            replica_groups=[[0, 1, 2, 3, 4, 5, 6, 7]]),
            waits=sorted(ag_waits.items()), inc=("cc", 1))

        for H in range(2):
            wg_cast = []
            for kc in range(KC):
                if kc < 2:
                    w = [("dv", p.at("dv"))] if H == 0 else [("mm", p.at("mm"))]
                else:
                    w = [("dv", wg_cast[kc - 2])]
                d = dma("sync", (wfa, wfb)[kc % 2][:, 0:512],
                        wg[kc * 128:(kc + 1) * 128, H * 512:(H + 1) * 512], waits=w)
                cv = op("vector", (lambda k2: (lambda eng: eng.tensor_copy(
                    big[:, k2 * 512:(k2 + 1) * 512],
                    (wfa, wfb)[k2 % 2][:, 0:512])))(kc), waits=[d])
                wg_cast.append(cv)
            c_wgh = p.at("dv")

            gt_done = []
            for t in range(8):
                gps = (ps_b2, ps_b3)[t % 2]

                def gmm(tt, gp):
                    def f(eng):
                        r = None
                        for kc in range(KC):
                            r = eng.matmul(gp[:, 0:512],
                                           xs_r[:, kc * NT + tt * 128:kc * NT + (tt + 1) * 128],
                                           big[:, kc * 512:(kc + 1) * 512],
                                           start=(kc == 0), stop=(kc == KC - 1))
                        return r
                    return f
                w = [("dv", c_wgh)]
                if t >= 2:
                    w.append(("dv", gt_done[t - 2]))
                elif H == 0:
                    w.append(("dv", c_attn))
                mm_g = op("tensor", gmm(t, gps), waits=w)

                def gsm1(gp):
                    def f(eng):
                        eng.tensor_reduce(mx8[:, 0:8].rearrange("p (s o) -> p s o", o=1),
                                          gp[:, 0:512].rearrange("p (s t) -> p s t", s=8),
                                          Ax.X, Alu.max)
                        eng.drain()
                        return eng.tensor_scalar(nm8[:, 0:8], mx8[:, 0:8], -SCALE, None,
                                                 Alu.mult)
                    return f
                c_g1 = op("vector", gsm1(gps), waits=[("mm", mm_g)])

                def gexp(gp):
                    def f(eng):
                        r = None
                        for s in range(8):
                            r = eng.activation(scr1[:, s * 64:(s + 1) * 64],
                                               gp[:, s * 64:(s + 1) * 64], Act.Exp,
                                               bias=nm8[:, s:s + 1], scale=SCALE)
                        return r
                    return f
                a_ge = op("scalar", gexp(gps), waits=[("dv", c_g1)])

                def gsum(eng):
                    eng.tensor_reduce(sm8[:, 0:8].rearrange("p (s o) -> p s o", o=1),
                                      scr1[:, 0:512].rearrange("p (s t) -> p s t", s=8),
                                      Ax.X, Alu.add)
                    eng.drain()
                    return eng.reciprocal(rs8[:, 0:8], sm8[:, 0:8])
                c_gs = op("vector", gsum, waits=[("ac", a_ge)])

                def gnorm(eng):
                    r = None
                    for s in range(8):
                        r = eng.activation(scr1[:, 512 + s * 64:512 + (s + 1) * 64],
                                           scr1[:, s * 64:(s + 1) * 64], Act.Copy,
                                           scale=rs8[:, s:s + 1])
                    return r
                a_gn = op("scalar", gnorm, waits=[("dv", c_gs)])

                last = None
                for p2 in range(4):
                    pair = H * 4 + p2
                    tcol = (p2 % 2) * 512
                    mm_tr = op("tensor", (lambda p3, tc: (lambda eng: eng.matmul(
                        ps_b1[0:128, tc:tc + 128], scr1[:, 512 + p3 * 128:512 + (p3 + 1) * 128],
                        idft[:, :], start=True, stop=True, is_transpose=True)))(p2, tcol),
                        waits=[("ac", a_gn)] if p2 == 0 else [("dv", p.at("dv"))])
                    last = op("vector", (lambda pr, tc, tt: (lambda eng: eng.tensor_copy(
                        gFM[:, pr * NT + tt * 128:pr * NT + (tt + 1) * 128],
                        ps_b1[0:128, tc:tc + 128])))(pair, tcol, t),
                        waits=[("mm", mm_tr)])
                gt_done.append(last)
        c_gates = p.at("dv")

        mix_evac = []
        for h in range(HC):
            pair, row = h // 2, (h % 2) * 64
            for half in range(2):
                mps = (ps_b2, ps_b3)[half]
                w = []
                if h == 0 and half == 0:
                    w = [("dv", c_gates)] + g_attn
                if len(mix_evac) >= 2:
                    w.append(("ac", mix_evac[-2]))

                def mmix(hh, pr, rw, hf, mp):
                    def f(eng):
                        return eng.matmul(
                            mp[:, 0:512], attn_sb[rw:rw + 64, pr * 128:(pr + 1) * 128],
                            gFM[rw:rw + 64, pr * NT + hf * NH:pr * NT + (hf + 1) * NH],
                            start=True, stop=True)
                    return f
                mm_m = op("tensor", mmix(h, pair, row, half, mps), waits=w)
                a_me = op("scalar", (lambda hh, hf, mp: (lambda eng: eng.activation(
                    big[:, hh * NT + hf * NH:hh * NT + (hf + 1) * NH], mp[:, 0:512],
                    Act.Copy)))(h, half, mps), waits=[("mm", mm_m)])
                mix_evac.append(a_me)
        a_mix = p.at("ac")

        mm_p = 0
        for oc in range(KC):
            wqv, cv = stream_slab(wp[oc * 128:(oc + 1) * 128, :], 2048, oc % 2,
                                  extra_waits=[("ac", a_mix)] if oc < 2 else ())
            for half in range(2):
                pps = (ps_b2, ps_b3)[half]

                def pmm(wt, hf, pp):
                    def f(eng):
                        r = None
                        for kc in range(KC):
                            r = eng.matmul(pp[:, 0:512], wt[:, kc * 128:(kc + 1) * 128],
                                           big[:, kc * NT + hf * NH:kc * NT + (hf + 1) * NH],
                                           start=(kc == 0), stop=(kc == KC - 1))
                        return r
                    return f
                w = [("dv", cv)]
                if oc > 0 or half > 0:
                    w.append(("dv", p.at("dv")))
                mm_p = op("tensor", pmm(wqv, half, pps), waits=w)
                op("vector", (lambda o2, hf, pp: (lambda eng: eng.tensor_tensor(
                    x_r[:, o2 * NT + hf * NH:o2 * NT + (hf + 1) * NH],
                    x_r[:, o2 * NT + hf * NH:o2 * NT + (hf + 1) * NH],
                    pp[:, 0:512], Alu.add)))(oc, half, pps), waits=[("mm", mm_p)])
            slab_used(oc % 2, mm_p)
        c_wp = p.at("dv")

        mm_d = 0
        for g in range(NG):
            for j in range(GSZ):
                ic = g * GSZ + j
                wqg, cvg = stream_fp8("gate", ic, 0,
                                      extra_waits=[("dv", c_wp)] if (g == 0 and j == 0) else ())

                def gmm2(wt):
                    def f(eng):
                        r = None
                        for half in range(2):
                            for kc in range(KC):
                                r = eng.matmul(ps_big[:, half * NH:(half + 1) * NH],
                                               wt[:, kc * 128:(kc + 1) * 128],
                                               xs_r[:, kc * NT + half * NH:kc * NT + (half + 1) * NH],
                                               start=(kc == 0), stop=(kc == KC - 1))
                        return r
                    return f
                w = [("dv", cvg)]
                if not (g == 0 and j == 0):
                    w.append(("ac", p.at("ac")))
                mm_gg = op("tensor", gmm2(wqg), waits=w)
                slab_used(0, mm_gg)

                wqu, cvu = stream_fp8("up", ic, 1)

                def umm(wt):
                    def f(eng):
                        r = None
                        for half in range(2):
                            for kc in range(KC):
                                r = eng.matmul(ps_big[:, 1024 + half * NH:1024 + (half + 1) * NH],
                                               wt[:, kc * 128:(kc + 1) * 128],
                                               xs_r[:, kc * NT + half * NH:kc * NT + (half + 1) * NH],
                                               start=(kc == 0), stop=(kc == KC - 1))
                        return r
                    return f
                mm_uu = op("tensor", umm(wqu), waits=[("dv", cvu)])
                slab_used(1, mm_uu)

                for half in range(2):
                    def epi(icc, hf):
                        def f(eng):
                            eng.activation(scr1[:, 0:NH], ps_big[:, hf * NH:(hf + 1) * NH],
                                           Act.Silu, scale=gs_t[:, icc:icc + 1])
                            return eng.activation(scr1[:, NH:NT],
                                                  ps_big[:, 1024 + hf * NH:1024 + (hf + 1) * NH],
                                                  Act.Copy, scale=us_t[:, icc:icc + 1])
                        return f
                    w_ep = [("mm", mm_uu)]
                    if half == 1:
                        w_ep.append(("dv", p.at("dv")))
                    a_ep = op("scalar", epi(ic, half), waits=w_ep)
                    op("vector", (lambda jj, hf: (lambda eng: eng.tensor_tensor(
                        big[:, jj * NT + hf * NH:jj * NT + (hf + 1) * NH],
                        scr1[:, 0:NH], scr1[:, NH:NT], Alu.mult)))(j, half),
                        waits=[("ac", a_ep)])
            c_fuse = p.at("dv")

            for oc in range(KC):
                wqd, cvd = stream_fp8("down", g * 16 + oc, oc % 2,
                                      extra_waits=[("dv", c_fuse)] if oc < 2 else ())
                for half in range(2):
                    def dmmf(wt, hf):
                        def f(eng):
                            r = None
                            for j in range(GSZ):
                                r = eng.matmul(ps_b1[:, hf * NH:(hf + 1) * NH],
                                               wt[:, j * 128:(j + 1) * 128],
                                               big[:, j * NT + hf * NH:j * NT + (hf + 1) * NH],
                                               start=(j == 0), stop=(j == GSZ - 1))
                            return r
                        return f
                    w = [("dv", cvd)]
                    if oc > 0 or half > 0 or g > 0:
                        w.append(("dv", p.at("dv")))
                    mm_d = op("tensor", dmmf(wqd, half), waits=w)
                    a_de = op("scalar", (lambda o2, hf: (lambda eng: eng.activation(
                        scr1[:, hf * NH:(hf + 1) * NH], ps_b1[:, hf * NH:(hf + 1) * NH],
                        Act.Copy, scale=ds_t[:, o2:o2 + 1])))(oc, half),
                        waits=[("mm", mm_d)])
                    op("vector", (lambda o2, hf: (lambda eng: eng.tensor_tensor(
                        x_r[:, o2 * NT + hf * NH:o2 * NT + (hf + 1) * NH],
                        x_r[:, o2 * NT + hf * NH:o2 * NT + (hf + 1) * NH],
                        scr1[:, hf * NH:(hf + 1) * NH], Alu.add)))(oc, half),
                        waits=[("ac", a_de)])
                slab_used(oc % 2, mm_d)
        c_ffn = p.at("dv")

        st_dmas = []
        for c in range(KC):
            st_dmas.append(dma("gpsimd", out[c * 128:(c + 1) * 128, :],
                               x_r[:, c * NT:(c + 1) * NT],
                               waits=[("dv", c_ffn)] if c == 0 else ()))
        final_waits = {}
        for (ln, v) in st_dmas:
            final_waits[ln] = max(final_waits.get(ln, 0), v)
        p.add("gpsimd", lambda eng: None, waits=sorted(final_waits.items()))

        sems = {}
        with contextlib.ExitStack() as sst:
            for s in (["dv", "ac", "mm", "cc"] + QLANES["sync"] + QLANES["gpsimd"]):
                sems[s] = sst.enter_context(nc.semaphore(s))
            with nc.Block() as block:
                def make(eng_name):
                    def stream(eng):
                        lastw = {}
                        for (e, waits, fn, incs) in p.ops:
                            if e != eng_name:
                                continue
                            for (s, v) in waits:
                                if v > 0 and lastw.get(s, -1) < v:
                                    eng.wait_ge(sems[s], v)
                                    lastw[s] = v
                            instr = fn(eng)
                            for (s, amt) in incs:
                                instr.then_inc(sems[s], amt)
                    return stream
                block.sync(make("sync"))
                block.gpsimd(make("gpsimd"))
                block.scalar(make("scalar"))
                block.vector(make("vector"))
                block.tensor(make("tensor"))
    return nc


def _host_layouts(inputs):
    x = np.asarray(inputs["L0_attn_x"], dtype=np.float32)
    Wk = np.asarray(inputs["L0_attn_Wk"], dtype=np.float32)
    Wv = np.asarray(inputs["L0_attn_Wv"], dtype=np.float32)
    Wg = np.asarray(inputs["L0_attn_Wg"], dtype=np.float32)
    Wp = np.asarray(inputs["L0_attn_Wp"], dtype=np.float32)
    gw = np.asarray(inputs["L0_ffn_gate_w"], dtype=np.float32)
    uw = np.asarray(inputs["L0_ffn_up_w"], dtype=np.float32)
    dw = np.asarray(inputs["L0_ffn_down_w"], dtype=np.float32)
    gs = np.asarray(inputs["L0_ffn_gate_scale"], dtype=np.float32)
    us = np.asarray(inputs["L0_ffn_up_scale"], dtype=np.float32)
    ds = np.asarray(inputs["L0_ffn_down_scale"], dtype=np.float32)
    lq = np.asarray(inputs["L0_attn_latent_queries"], dtype=np.float32)

    perm_d = np.concatenate([np.arange(0, 128, 2), np.arange(1, 128, 2)])

    Wk_p = Wk.reshape(C, HC, D)[:, :, perm_d].reshape(C, C)
    wk_r = np.ascontiguousarray(
        Wk_p.reshape(KC, 128, HC, D).transpose(2, 1, 0, 3).reshape(C, C))
    wv_r = np.ascontiguousarray(
        Wv.reshape(KC, 128, HC, D).transpose(2, 1, 0, 3).reshape(C, C))
    wp_r = np.ascontiguousarray(
        Wp.reshape(KC, 128, KC, 128).transpose(2, 1, 0, 3).reshape(C, C))
    wgate_r = gw.reshape(KC, 128, IC, 128).transpose(2, 1, 0, 3).reshape(IC * 128, C)
    wup_r = uw.reshape(KC, 128, IC, 128).transpose(2, 1, 0, 3).reshape(IC * 128, C)
    dwr = dw.reshape(NG, GSZ, 128, KC, 128).transpose(0, 3, 2, 1, 4)
    wdown_r = dwr.reshape(NG * KC * 128, GSZ * 128)
    srcs = {"gate": wgate_r, "up": wup_r, "down": wdown_r}
    qsrc_by_rank = {}
    for c in range(8):
        blocks = [srcs[k][i * 128:(i + 1) * 128, :]
                  for (k, i, r, c0, cs) in QTAB if r == c]
        qsrc_by_rank[c] = np.ascontiguousarray(
            np.concatenate(blocks, axis=1))
    lq_r = np.ascontiguousarray(
        lq[0].transpose(1, 2, 0)[:, perm_d, :].transpose(1, 0, 2).reshape(128, HC * L))

    import jax
    import jax.numpy as jnp
    cpu = jax.devices("cpu")[0]
    with jax.default_device(cpu):
        invf = 1.0 / 10000.0 ** (jnp.arange(0, 64, dtype=jnp.float32) * 2.0 / 128.0)
        ang = jnp.arange(128, dtype=jnp.float32)[:, None] * invf[None, :]
        cosv = np.asarray(jnp.cos(ang), dtype=np.float32)
        sinv = np.asarray(jnp.sin(ang), dtype=np.float32)
    rCh = np.empty((128, 128), np.float32)
    rSh = np.empty((128, 128), np.float32)
    rCh[0:64, :] = cosv.T
    rCh[64:128, :] = cosv.T
    rSh[0:64, :] = -sinv.T
    rSh[64:128, :] = sinv.T

    maska = np.where(np.arange(128)[None, :] <= np.arange(64)[:, None],
                     np.float32(0.0), np.float32(NEG)).astype(np.float32)
    idf = np.eye(128, dtype=np.float32)
    gs_r = np.ascontiguousarray(gs.reshape(IC, 128).T)
    us_r = np.ascontiguousarray(us.reshape(IC, 128).T)
    ds_r = np.ascontiguousarray(ds.reshape(KC, 128).T)

    shared = dict(wk=wk_r, wv=wv_r, wg=np.ascontiguousarray(Wg), wp=wp_r,
                  lq=lq_r, ropec=rCh, ropes=rSh, maska=maska, idf=idf,
                  gs=gs_r, us=us_r, ds=ds_r)
    in_maps = []
    for c in range(8):
        b, q = c // 4, c % 4
        m = dict(shared)
        m["xT"] = np.ascontiguousarray(x[b, q * NT:(q + 1) * NT, :].T)
        m["qsrc"] = qsrc_by_rank[c]
        m["xhT"] = np.ascontiguousarray(x[b, 0:128, :].T)
        in_maps.append(m)
    return in_maps


def kernel(**inputs):
    in_maps = _host_layouts(inputs)
    nc = build()
    res = run_bass_kernel_spmd(nc, in_maps, core_ids=list(range(8)))
    x = np.asarray(inputs["L0_attn_x"])
    outp = np.empty((2, 4096, C), dtype=np.float32)
    for c in range(8):
        b, q = c // 4, c % 4
        outp[b, q * NT:(q + 1) * NT, :] = np.asarray(res.results[c]["out"]).T
    return outp


# version 09
# speedup vs baseline: 3.0685x; 1.0037x over previous
import numpy as np
import ml_dtypes
import concourse.bass as bass
import concourse.mybir as mybir
from concourse.bass_utils import run_bass_kernel_spmd

F32 = mybir.dt.float32
BF16 = mybir.dt.bfloat16
FP8 = mybir.dt.float8e4
Alu = mybir.AluOpType
Act = mybir.ActivationFunctionType
Ax = mybir.AxisListType

C = 2048
NT = 1024
NH = 512
KC = 16
HC = 16
D = 128
L = 64
IC = 48
GSZ = 12
NG = IC // GSZ
SCALE = float(1.0 / np.sqrt(128.0))

QSH = 4718592
QCOLS = QSH // 128


def _build_qtab():
    tab = []
    off = 0
    for ic in range(IC):
        tab.append(["gate", ic, off, 2048])
        off += 128 * 2048
    for ic in range(IC):
        tab.append(["up", ic, off, 2048])
        off += 128 * 2048
    for gi in range(NG * 16):
        tab.append(["down", gi, off, 1536])
        off += 128 * 1536
    assert off == 8 * QSH
    out = []
    for kind, idx, off, cols in tab:
        rank = off // QSH
        assert (off + 128 * cols - 1) // QSH == rank
        col0 = (off - rank * QSH) // 128
        out.append((kind, idx, rank, col0, cols))
    return out


QTAB = _build_qtab()
QBYKIND = {(k, i): (r, c0, cs) for (k, i, r, c0, cs) in QTAB}
NEG = -1.0e9
EPS = 1e-5


class Plan:
    def __init__(self):
        self.ops = []
        self.cnt = {}

    def bump(self, sem, amt):
        self.cnt[sem] = self.cnt.get(sem, 0) + amt
        return self.cnt[sem]

    def at(self, sem):
        return self.cnt.get(sem, 0)

    def add(self, eng, fn, waits=(), inc=None):
        incs = []
        tgt = None
        if inc is not None:
            sem, amt = inc
            tgt = self.bump(sem, amt)
            incs = [(sem, amt)]
        self.ops.append((eng, tuple(waits), fn, tuple(incs)))
        return tgt


ESEM = {"vector": "dv", "scalar": "ac", "tensor": "mm"}
NLANES = 4
QLANES = {"sync": ["sq%d" % i for i in range(NLANES)],
          "gpsimd": ["gq%d" % i for i in range(NLANES)]}


def build():
    nc = bass.Bass("TRN2", target_bir_lowering=False)
    p = Plan()

    def dram_in(name, shape, dt=F32):
        return nc.dram_tensor(name, shape, dt, kind="ExternalInput").ap()

    xT = dram_in("xT", [C, NT])
    xhT = dram_in("xhT", [C, 128])
    wk = dram_in("wk", [C, C])
    wv = dram_in("wv", [C, C])
    wg = dram_in("wg", [C, HC * L])
    wp = dram_in("wp", [C, C])
    qsrc = dram_in("qsrc", [128, QCOLS])
    qshard = nc.dram_tensor("qshard", [128, QCOLS], FP8, kind="Internal").ap()
    qfull = nc.dram_tensor("qfull", [8, 128, QCOLS], FP8, kind="Internal").ap()
    lqd = dram_in("lq", [128, HC * L])
    ropec = dram_in("ropec", [128, 128])
    ropes = dram_in("ropes", [128, 128])
    maska = dram_in("maska", [64, 128])
    idf = dram_in("idf", [128, 128])
    gsd = dram_in("gs", [128, IC])
    usd = dram_in("us", [128, IC])
    dsd = dram_in("ds", [128, KC])
    out = nc.dram_tensor("out", [C, NT], F32, kind="ExternalOutput").ap()

    import contextlib
    st = contextlib.ExitStack()
    sb = lambda name, shape, dt: st.enter_context(nc.sbuf_tensor(name, shape, dt))
    psb = lambda name, shape, dt: st.enter_context(nc.psum_tensor(name, shape, dt))

    with st:
        x_r = sb("x_r", [128, KC * NT], F32)
        xs_r = sb("xs_r", [128, KC * NT], BF16)
        xh_r = sb("xh_r", [128, KC * 128], F32)
        xsh_r = sb("xsh_r", [128, KC * 128], BF16)
        big = sb("big", [128, KC * NT], BF16)
        gFM = sb("gFM", [128, 8 * NT], BF16)
        wfa = sb("wfa", [128, 2048], F32)
        wfb = sb("wfb", [128, 2048], F32)
        wqa = sb("wqa", [128, 2048], BF16)
        wqb = sb("wqb", [128, 2048], BF16)
        qi32a = sb("qi32a", [128, 512], mybir.dt.int16)
        qi32b = sb("qi32b", [128, 512], mybir.dt.int16)
        f8x = sb("f8x", [128, 2048], FP8)
        scr1 = sb("scr1", [128, NT], F32)
        scr3 = sb("scr3", [128, 128], F32)
        scr4 = sb("scr4", [128, 128], F32)
        lqb = sb("lqb", [128, HC * L], BF16)
        rC = sb("rC", [128, 128], F32)
        rS = sb("rS", [128, 128], F32)
        mskt = sb("mskt", [64, 128], F32)
        idft = sb("idft", [128, 128], F32)
        ksw = sb("ksw", [128, 128], F32)
        k_sb = sb("k_sb", [128, HC * 128], BF16)
        v_sb = sb("v_sb", [128, HC * 128], BF16)

        wT_sb = sb("wT_sb", [128, 2 * 64], BF16)
        attn_sb = sb("attn_sb", [128, 8 * 128], BF16)
        atmp = sb("atmp", [64, 128], BF16)
        vech = sb("vech", [4, 128], F32)
        vec_hm, vec_hr, vec_h1, vec_h2 = (vech[0:1, :], vech[1:2, :],
                                          vech[2:3, :], vech[3:4, :])
        ones_c = sb("ones_c", [128, 1], F32)
        eps_t = sb("eps_t", [128, 1], F32)
        ones_r2 = sb("ones_r2", [1, 128], F32)
        ones_r = ones_r2[0:1, :]
        gs_t = sb("gs_t", [128, IC], F32)
        us_t = sb("us_t", [128, IC], F32)
        ds_t = sb("ds_t", [128, KC], F32)
        mx8 = sb("mx8", [128, 8], F32)
        nm8 = sb("nm8", [128, 8], F32)
        sm8 = sb("sm8", [128, 8], F32)
        rs8 = sb("rs8", [128, 8], F32)
        sc_mx = sb("sc_mx", [64, 1], F32)
        sc_nm = sb("sc_nm", [64, 1], F32)
        sc_s = sb("sc_s", [64, 1], F32)
        sc_r = sb("sc_r", [64, 1], F32)

        kt1 = scr3
        smt = scr3[0:64, :]
        ewt = scr4[0:64, :]
        wfp = ksw[0:64, :]
        ps_big = psb("ps_big", [128, 2048], F32)
        ps_b1 = psb("ps_b1", [128, 1024], F32)
        ps_b2 = psb("ps_b2", [128, 512], F32)
        ps_b3 = psb("ps_b3", [128, 512], F32)

        lane_idx = {"sync": 0, "gpsimd": 0}

        def dma(q, out_ap, in_ap, waits=()):
            lane = QLANES[q][lane_idx[q] % NLANES]
            lane_idx[q] += 1
            w = list(waits)
            if p.at(lane) > 0:
                w.append((lane, p.at(lane)))
            fn = (lambda o, i: (lambda eng: eng.dma_start(out=o, in_=i)))(out_ap, in_ap)
            cnt = p.add(q, fn, waits=w, inc=(lane, 16))
            return (lane, cnt)

        def op(eng, fn, waits=()):
            return p.add(eng, fn, waits=waits, inc=(ESEM[eng], 1))

        d_lq = dma("gpsimd", wfa[:, 0:HC * L], lqd[:, :])
        dma("gpsimd", rC[:, :], ropec[:, :])
        dma("gpsimd", rS[:, :], ropes[:, :])
        dma("gpsimd", mskt[:, :], maska[:, :])
        dma("gpsimd", idft[:, :], idf[:, :])
        dma("gpsimd", gs_t[:, :], gsd[:, :])
        dma("gpsimd", us_t[:, :], usd[:, :])
        dma("gpsimd", ds_t[:, :], dsd[:, :])
        d_xh = [dma("sync", xh_r[:, c * 128:(c + 1) * 128], xhT[c * 128:(c + 1) * 128, :])
                for c in range(KC)]
        d_x = [dma("sync", x_r[:, c * NT:(c + 1) * NT], xT[c * 128:(c + 1) * 128, :])
               for c in range(KC)]

        op("vector", lambda eng: eng.memset(ones_c[:, :], 1.0))
        op("vector", lambda eng: eng.memset(ones_r, 1.0))
        op("vector", lambda eng: eng.memset(eps_t[:, :], EPS))
        c_ones = p.at("dv")
        op("vector", lambda eng: eng.tensor_copy(lqb[:, :], wfa[:, 0:HC * L]),
           waits=[d_lq])
        slab0_init = p.at("dv")

        sq_done = []
        for c in range(KC):
            w = [d_x[c]]
            if c >= 1:
                w.append(("mm", sq_done[c - 1]))
            scr = scr1
            op("scalar", (lambda cc, s: (lambda eng: eng.activation(
                s[:, :], x_r[:, cc * NT:(cc + 1) * NT], Act.Square)))(c, scr), waits=w)
            a_sq = p.at("ac")
            last = None
            for h in range(2):
                def mmfn(cc, s, hh):
                    def f(eng):
                        eng.matmul(ps_big[0:1, hh * NH:(hh + 1) * NH], ones_c[:, :],
                                   x_r[:, cc * NT + hh * NH: cc * NT + (hh + 1) * NH],
                                   start=(cc == 0), stop=(cc == KC - 1))
                        return eng.matmul(ps_big[0:1, 1024 + hh * NH:1024 + (hh + 1) * NH],
                                          ones_c[:, :], s[:, hh * NH:(hh + 1) * NH],
                                          start=(cc == 0), stop=(cc == KC - 1))
                    return f
                wmm = [("ac", a_sq)] + ([("dv", c_ones)] if c == 0 else [])
                last = op("tensor", mmfn(c, scr, h), waits=wmm)
            sq_done.append(last)
        mm_stats = p.at("mm")

        hs_done = []
        for c in range(KC):
            w = [d_xh[c]]
            if c >= 1:
                w.append(("mm", hs_done[c - 1]))
            op("scalar", (lambda cc: (lambda eng: eng.activation(
                scr3[:, :], xh_r[:, cc * 128:(cc + 1) * 128], Act.Square)))(c), waits=w)
            a_sq = p.at("ac")

            def mmfn(cc):
                def f(eng):
                    eng.matmul(ps_b1[0:1, 0:128], ones_c[:, :],
                               xh_r[:, cc * 128:(cc + 1) * 128],
                               start=(cc == 0), stop=(cc == KC - 1))
                    return eng.matmul(ps_b1[0:1, 512:640], ones_c[:, :], scr3[:, :],
                                      start=(cc == 0), stop=(cc == KC - 1))
                return f
            hs_done.append(op("tensor", mmfn(c), waits=[("ac", a_sq)]))
        mm_hstats = p.at("mm")

        inv_n = 1.0 / float(C)
        A = wfa[0:1, 0:1024]
        B = scr1[0:1, 0:1024]

        op("vector", lambda eng: eng.tensor_scalar(A, ps_big[0:1, 0:1024], inv_n,
                                                   None, Alu.mult),
           waits=[("mm", mm_hstats), ("dv", slab0_init)])
        c_m = p.at("dv")

        def bcast_m(eng):
            r = None
            for h in range(2):
                r = eng.matmul(ps_big[:, h * NH:(h + 1) * NH], ones_r,
                               wfa[0:1, h * NH:(h + 1) * NH], start=True, stop=True)
            return r
        mm_bm = op("tensor", bcast_m, waits=[("dv", c_m)])

        def fin_v(eng):
            eng.tensor_scalar(B, ps_big[0:1, 1024:2048], inv_n, None, Alu.mult)
            eng.tensor_tensor(A, A, A, Alu.mult)
            eng.drain()
            return eng.tensor_tensor(B, B, A, Alu.subtract)
        op("vector", fin_v, waits=[("mm", mm_bm)])
        c_var = p.at("dv")
        op("scalar", lambda eng: eng.activation(A, B, Act.Sqrt, bias=eps_t[0:1, :]),
           waits=[("dv", c_var)])
        a_std = p.at("ac")
        op("vector", lambda eng: eng.reciprocal(B, A), waits=[("ac", a_std)])
        c_r = p.at("dv")

        def bcast_r(eng):
            r = None
            for h in range(2):
                r = eng.matmul(ps_big[:, 1024 + h * NH:1024 + (h + 1) * NH], ones_r,
                               scr1[0:1, h * NH:(h + 1) * NH], start=True, stop=True)
            return r
        mm_br = op("tensor", bcast_r, waits=[("dv", c_r)])

        A2 = vech[0:1, :]
        B2 = scr1[0:1, 0:128]
        op("vector", lambda eng: eng.tensor_scalar(A2, ps_b1[0:1, 0:128], inv_n,
                                                   None, Alu.mult),
           waits=[("mm", mm_br)])
        c_hm = p.at("dv")
        mm_bhm = op("tensor", lambda eng: eng.matmul(ps_b2[:, 0:128], ones_r, A2,
                                                     start=True, stop=True),
                    waits=[("dv", c_hm)])

        def fin_hv(eng):
            eng.tensor_scalar(B2, ps_b1[0:1, 512:640], inv_n, None, Alu.mult)
            eng.tensor_tensor(A2, A2, A2, Alu.mult)
            eng.drain()
            return eng.tensor_tensor(B2, B2, A2, Alu.subtract)
        op("vector", fin_hv, waits=[("mm", mm_bhm)])
        c_hv = p.at("dv")
        op("scalar", lambda eng: eng.activation(A2, B2, Act.Sqrt, bias=eps_t[0:1, :]),
           waits=[("dv", c_hv)])
        a_hstd = p.at("ac")
        op("vector", lambda eng: eng.reciprocal(B2, A2), waits=[("ac", a_hstd)])
        c_hr = p.at("dv")
        mm_bcast = op("tensor", lambda eng: eng.matmul(ps_b2[:, 128:256], ones_r, B2,
                                                       start=True, stop=True),
                      waits=[("dv", c_hr)])

        for c in range(KC):
            def xnfn(cc):
                def f(eng):
                    eng.drain()
                    eng.tensor_tensor(scr1[:, :], x_r[:, cc * NT:(cc + 1) * NT],
                                      ps_big[:, 0:1024], Alu.subtract)
                    eng.drain()
                    return eng.tensor_tensor(xs_r[:, cc * NT:(cc + 1) * NT], scr1[:, :],
                                             ps_big[:, 1024:2048], Alu.mult)
                return f
            op("vector", xnfn(c), waits=[("mm", mm_bcast)] if c == 0 else ())
        for c in range(KC):
            def xnhfn(cc):
                def f(eng):
                    eng.drain()
                    eng.tensor_tensor(scr3[:, :], xh_r[:, cc * 128:(cc + 1) * 128],
                                      ps_b2[:, 0:128], Alu.subtract)
                    eng.drain()
                    return eng.tensor_tensor(xsh_r[:, cc * 128:(cc + 1) * 128], scr3[:, :],
                                             ps_b2[:, 128:256], Alu.mult)
                return f
            op("vector", xnhfn(c))
        c_xn = p.at("dv")

        slab_state = {"dv": [slab0_init, 0], "mm": [0, 0]}
        QSTG = [scr1[:, 0:512], scr1[:, 512:1024]]
        QAF = [xh_r[:, 0:512], xh_r[:, 512:1024]]
        QI = [qi32a, qi32b]
        QB6 = []
        for s_ in range(2):
            base = s_ * 3072
            QB6.append([gFM[:, base + j_ * 512: base + (j_ + 1) * 512]
                        for j_ in range(6)])
        qstate = {"d2": [0, 0], "nxt": 0, "f8dma": [None] * 4}
        T1, T2, T3, T4 = 2.0 ** -5, 2.0 ** -4, 2.0 ** -3, 2.0 ** -2
        EPS24 = 2.0 ** -24

        def quant_piece(pi, extra_waits=()):
            setid = pi % 2
            stg, qaf, qi = QSTG[setid], QAF[setid], QI[setid]
            A, B, Cc, D, E, F = QB6[setid]
            w0 = list(extra_waits)
            if qstate["d2"][setid]:
                w0.append(("dv", qstate["d2"][setid]))
            dsrc = dma("sync", stg, qsrc[:, pi * 512:(pi + 1) * 512], waits=w0)
            a1 = op("scalar", (lambda s_, d_: (lambda eng: eng.activation(
                d_, s_, Act.Abs)))(stg, qaf), waits=[dsrc] + list(extra_waits))

            def casc(eng):
                eng.tensor_scalar(A, qaf, T1, -256.0, Alu.is_ge, Alu.mult)
                eng.tensor_scalar(B, qaf, T2, -128.0, Alu.is_ge, Alu.mult)
                eng.tensor_scalar(Cc, qaf, T3, -64.0, Alu.is_ge, Alu.mult)
                eng.tensor_scalar(D, qaf, T4, -32.0, Alu.is_ge, Alu.mult)
                eng.drain()
                eng.scalar_tensor_tensor(A, A, 512.0, B, Alu.add, Alu.add)
                eng.tensor_tensor(Cc, Cc, D, Alu.add)
                eng.drain()
                eng.tensor_tensor(B, A, Cc, Alu.add)
                eng.drain()
                eng.tensor_scalar(A, qaf, T1, 2.0 ** -9, Alu.is_ge, Alu.mult)
                eng.tensor_scalar(Cc, qaf, T2, 2.0 ** -8, Alu.is_ge, Alu.mult)
                eng.tensor_scalar(D, qaf, T3, 2.0 ** -7, Alu.is_ge, Alu.mult)
                eng.tensor_scalar(E, qaf, T4, 2.0 ** -6, Alu.is_ge, Alu.mult)
                eng.tensor_tensor(stg, stg, B, Alu.mult)
                eng.drain()
                eng.scalar_tensor_tensor(A, A, 2.0 ** -9, Cc, Alu.add, Alu.add)
                eng.tensor_tensor(D, D, E, Alu.add)
                eng.drain()
                return eng.tensor_tensor(E, A, D, Alu.add)
            d1 = op("vector", casc, waits=[("ac", a1)])

            def convs(eng):
                eng.activation(qi[:, :], stg, Act.Copy, bias=1024.0)
                eng.drain()
                eng.activation(qaf, qi[:, :], Act.Copy)
                eng.drain()
                return eng.activation(qi[:, :], qi[:, :], Act.Copy,
                                      bias=0.25, scale=0.5)
            a2 = op("scalar", convs, waits=[("dv", d1)])

            fq = f8x[:, (pi % 4) * 512:(pi % 4 + 1) * 512]

            def corr(eng):
                eng.scalar_tensor_tensor(D, qi[:, :], -2.0, qaf, Alu.mult, Alu.add)
                eng.scalar_tensor_tensor(stg, qaf, -1024.0, stg,
                                         Alu.add, Alu.subtract)
                eng.drain()
                eng.scalar_tensor_tensor(stg, D, -EPS24, stg, Alu.mult, Alu.add)
                eng.drain()
                eng.tensor_scalar(A, stg, -0.5, None, Alu.is_lt)
                eng.tensor_scalar(Cc, stg, 0.5, None, Alu.is_gt)
                eng.drain()
                eng.scalar_tensor_tensor(stg, qaf, -1024.0, A, Alu.add, Alu.add)
                eng.drain()
                eng.tensor_tensor(stg, stg, Cc, Alu.subtract)
                eng.drain()
                return eng.tensor_tensor(fq, stg, E, Alu.mult)
            w2 = [("ac", a2)]
            if qstate["f8dma"][pi % 4] is not None:
                w2.append(qstate["f8dma"][pi % 4])
            d2 = op("vector", corr, waits=w2)
            qstate["d2"][setid] = d2
            qstate["f8dma"][pi % 4] = dma(
                "gpsimd", qshard[:, pi * 512:(pi + 1) * 512], fq,
                waits=[("dv", d2)])
            return d2

        def stream_slab(src_ap, ncols, slot, extra_waits=()):
            wf = (wfa, wfb)[slot]
            wq = (wqa, wqb)[slot]
            w = list(extra_waits)
            if slab_state["dv"][slot]:
                w.append(("dv", slab_state["dv"][slot]))
            d = dma("sync", wf[:, 0:ncols], src_ap, waits=w)
            w2 = [d]
            if slab_state["mm"][slot]:
                w2.append(("mm", slab_state["mm"][slot]))

            def cast(eng):
                return eng.tensor_copy(wq[:, 0:ncols], wf[:, 0:ncols])
            cv = op("vector", cast, waits=w2)
            slab_state["dv"][slot] = cv
            return wq, cv

        f8state = {"dv": [0, 0], "cc": False, "half": 0}

        def stream_fp8(kind, idx, slot, extra_waits=()):
            rank, col0, cols = QBYKIND[(kind, idx)]
            wq = (wqa, wqb)[slot]
            cv = 0
            c0 = 0
            first = True
            while c0 < cols:
                cw = min(1024, cols - c0)
                hb = f8state["half"]
                f8state["half"] ^= 1
                stage = f8x[:, hb * 1024: hb * 1024 + cw]
                w = list(extra_waits) if first else []
                if first and not f8state["cc"]:
                    w.append(("cc", 1))
                    f8state["cc"] = True
                if f8state["dv"][hb]:
                    w.append(("dv", f8state["dv"][hb]))
                d = dma("sync", stage, qfull[rank, :, col0 + c0:col0 + c0 + cw],
                        waits=w)
                w2 = [d]
                if first and slab_state["mm"][slot]:
                    w2.append(("mm", slab_state["mm"][slot]))
                cv = op("vector", (lambda o_, s_: (lambda eng: eng.tensor_copy(
                    o_, s_)))(wq[:, c0:c0 + cw], stage), waits=w2)
                f8state["dv"][hb] = cv
                first = False
                c0 += cw
            slab_state["dv"][slot] = cv
            return wq, cv

        def slab_used(slot, mm_count):
            slab_state["mm"][slot] = mm_count

        phq = {"next": 0}

        def phq_emit(n):
            for _ in range(n):
                if phq["next"] < 72:
                    ew = [("dv", c_xn)] if phq["next"] < 2 else ()
                    quant_piece(phq["next"], extra_waits=ew)
                    phq["next"] += 1

        for h in range(HC):
            wqv, cv = stream_slab(wk[h * 128:(h + 1) * 128, :], 2048, h % 2,
                                  extra_waits=[("mm", mm_bcast)] if h < 2 else ())

            def kmm(hh, wt):
                def f(eng):
                    r = None
                    for kc in range(KC):
                        r = eng.matmul(ps_big[:, hh * 128:(hh + 1) * 128],
                                       wt[:, kc * 128:(kc + 1) * 128],
                                       xsh_r[:, kc * 128:(kc + 1) * 128],
                                       start=(kc == 0), stop=(kc == KC - 1))
                    return r
                return f
            m = op("tensor", kmm(h, wqv), waits=[("dv", cv)])
            slab_used(h % 2, m)
            phq_emit(2)
        mm_k = p.at("mm")

        def vb(h):
            if h < 8:
                return ps_b1[:, h * 128:(h + 1) * 128]
            if h < 12:
                return ps_b2[:, (h - 8) * 128:(h - 7) * 128]
            return ps_b3[:, (h - 12) * 128:(h - 11) * 128]

        for h in range(HC):
            wqv, cv = stream_slab(wv[h * 128:(h + 1) * 128, :], 2048, h % 2)

            def vmm(hh, wt):
                def f(eng):
                    r = None
                    for kc in range(KC):
                        r = eng.matmul(vb(hh), xsh_r[:, kc * 128:(kc + 1) * 128],
                                       wt[:, kc * 128:(kc + 1) * 128],
                                       start=(kc == 0), stop=(kc == KC - 1))
                    return r
                return f
            m = op("tensor", vmm(h, wqv), waits=[("dv", cv)])
            slab_used(h % 2, m)
            phq_emit(2)
        mm_v = p.at("mm")

        for h in range(HC):
            op("scalar", (lambda hh: (lambda eng: eng.activation(
                v_sb[:, hh * 128:(hh + 1) * 128], vb(hh), Act.Copy)))(h),
               waits=[("mm", mm_v)] if h == 0 else ())
        a_vevac = p.at("ac")

        rope_done = []
        for h in range(HC):
            w = [("mm", mm_k), ("ac", a_vevac)] if h == 0 else [("dv", rope_done[-1])]
            a_kf = op("scalar", (lambda hh: (lambda eng: eng.activation(
                scr4[:, :], ps_big[:, hh * 128:(hh + 1) * 128], Act.Copy)))(h), waits=w)
            dsw1 = dma("gpsimd", ksw[0:64, :], scr4[64:128, :], waits=[("ac", a_kf)])
            dsw2 = dma("gpsimd", ksw[64:128, :], scr4[0:64, :])

            def rope(hh):
                def f(eng):
                    eng.drain()
                    eng.tensor_tensor(kt1[:, :], scr4[:, :], rC[:, :], Alu.mult)
                    eng.tensor_tensor(ksw[:, :], ksw[:, :], rS[:, :], Alu.mult)
                    eng.drain()
                    return eng.tensor_tensor(k_sb[:, hh * 128:(hh + 1) * 128],
                                             kt1[:, :], ksw[:, :], Alu.add)
                return f
            rope_done.append(op("vector", rope(h), waits=[dsw1, dsw2]))
        c_ropes = p.at("dv")

        attn_tail = []
        shift_dmas = []
        for h in range(HC):
            sps = (ps_b2, ps_b3)[h % 2]
            w = [("dv", c_ropes)] if h == 0 else []
            if h >= 2:
                w.append(("dv", attn_tail[h - 2]))
            mm_sc = op("tensor", (lambda hh, sp: (lambda eng: eng.matmul(
                sp[0:64, 0:128], lqb[:, hh * L:(hh + 1) * L],
                k_sb[:, hh * 128:(hh + 1) * 128], start=True, stop=True)))(h, sps),
                waits=w)

            def smax(hh, sp):
                def f(eng):
                    eng.drain()
                    eng.scalar_tensor_tensor(smt, sp[0:64, 0:128], SCALE, mskt[:, :],
                                             Alu.mult, Alu.add)
                    eng.drain()
                    eng.reduce_max(sc_mx[:, :], smt, Ax.X)
                    eng.drain()
                    return eng.tensor_scalar(sc_nm[:, :], sc_mx[:, :], -1.0, None, Alu.mult)
                return f
            c_sm = op("vector", smax(h, sps), waits=[("mm", mm_sc)])
            a_exp = op("scalar", lambda eng: eng.activation(ewt, smt, Act.Exp,
                                                            bias=sc_nm[:, :]),
                       waits=[("dv", c_sm)])

            def norm(eng):
                eng.reduce_sum(sc_s[:, :], ewt, Ax.X)
                eng.drain()
                return eng.reciprocal(sc_r[:, :], sc_s[:, :])
            c_nrm = op("vector", norm, waits=[("ac", a_exp)])
            a_wn = op("scalar", lambda eng: eng.activation(wfp, ewt, Act.Copy,
                                                           scale=sc_r[:, :]),
                      waits=[("dv", c_nrm)])
            tcol = (h % 2) * 512
            mm_t = op("tensor", (lambda tc: (lambda eng: eng.matmul(
                ps_big[0:128, tc:tc + 64], wfp, idft[0:64, 0:64], start=True, stop=True,
                is_transpose=True)))(tcol), waits=[("ac", a_wn)])
            c_wt = op("vector", (lambda hh, tc: (lambda eng: eng.tensor_copy(
                wT_sb[:, (hh % 2) * 64:(hh % 2 + 1) * 64], ps_big[0:128, tc:tc + 64])))(h, tcol),
                waits=[("mm", mm_t)])
            acol = 1024 + (h % 2) * 512
            mm_a = op("tensor", (lambda hh, cl: (lambda eng: eng.matmul(
                ps_big[0:64, cl:cl + 128], wT_sb[:, (hh % 2) * 64:(hh % 2 + 1) * 64],
                v_sb[:, hh * 128:(hh + 1) * 128], start=True, stop=True)))(h, acol),
                waits=[("dv", c_wt)])
            if h % 2 == 0:
                c_ae = op("vector", (lambda hh, cl: (lambda eng: eng.tensor_copy(
                    attn_sb[0:64, (hh // 2) * 128:(hh // 2 + 1) * 128],
                    ps_big[0:64, cl:cl + 128])))(h, acol), waits=[("mm", mm_a)])
            else:
                c_ae = op("vector", (lambda cl: (lambda eng: eng.tensor_copy(
                    atmp[:, :], ps_big[0:64, cl:cl + 128])))(acol),
                    waits=[("mm", mm_a)] + ([shift_dmas[-1]] if shift_dmas else []))
                shift_dmas.append(
                    dma("gpsimd", attn_sb[64:128, (h // 2) * 128:(h // 2 + 1) * 128],
                        atmp[:, :], waits=[("dv", c_ae)]))
            attn_tail.append(c_ae)
            phq_emit(1)
        c_attn = p.at("dv")
        g_attn = list(shift_dmas)
        phq_emit(72)
        ag_waits = {}
        for t_ in qstate["f8dma"]:
            if t_ is not None:
                ln, v = t_
                ag_waits[ln] = max(ag_waits.get(ln, 0), v)
        op_ag = p.add("gpsimd", lambda eng: eng.collective_compute(
            "AllGather", Alu.bypass, ins=[qshard[:, :]], outs=[qfull[:, :, :]],
            replica_groups=[[0, 1, 2, 3, 4, 5, 6, 7]]),
            waits=sorted(ag_waits.items()), inc=("cc", 1))

        for H in range(2):
            wg_cast = []
            for kc in range(KC):
                if kc < 2:
                    w = [("dv", p.at("dv"))] if H == 0 else [("mm", p.at("mm"))]
                else:
                    w = [("dv", wg_cast[kc - 2])]
                d = dma("sync", (wfa, wfb)[kc % 2][:, 0:512],
                        wg[kc * 128:(kc + 1) * 128, H * 512:(H + 1) * 512], waits=w)
                cv = op("vector", (lambda k2: (lambda eng: eng.tensor_copy(
                    big[:, k2 * 512:(k2 + 1) * 512],
                    (wfa, wfb)[k2 % 2][:, 0:512])))(kc), waits=[d])
                wg_cast.append(cv)
            c_wgh = p.at("dv")

            gt_done = []
            for t in range(8):
                gps = (ps_b2, ps_b3)[t % 2]

                def gmm(tt, gp):
                    def f(eng):
                        r = None
                        for kc in range(KC):
                            r = eng.matmul(gp[:, 0:512],
                                           xs_r[:, kc * NT + tt * 128:kc * NT + (tt + 1) * 128],
                                           big[:, kc * 512:(kc + 1) * 512],
                                           start=(kc == 0), stop=(kc == KC - 1))
                        return r
                    return f
                w = [("dv", c_wgh)]
                if t >= 2:
                    w.append(("dv", gt_done[t - 2]))
                elif H == 0:
                    w.append(("dv", c_attn))
                mm_g = op("tensor", gmm(t, gps), waits=w)

                def gsm1(gp):
                    def f(eng):
                        eng.tensor_reduce(mx8[:, 0:8].rearrange("p (s o) -> p s o", o=1),
                                          gp[:, 0:512].rearrange("p (s t) -> p s t", s=8),
                                          Ax.X, Alu.max)
                        eng.drain()
                        return eng.tensor_scalar(nm8[:, 0:8], mx8[:, 0:8], -SCALE, None,
                                                 Alu.mult)
                    return f
                c_g1 = op("vector", gsm1(gps), waits=[("mm", mm_g)])

                def gexp(gp):
                    def f(eng):
                        r = None
                        for s in range(8):
                            r = eng.activation(scr1[:, s * 64:(s + 1) * 64],
                                               gp[:, s * 64:(s + 1) * 64], Act.Exp,
                                               bias=nm8[:, s:s + 1], scale=SCALE)
                        return r
                    return f
                a_ge = op("scalar", gexp(gps), waits=[("dv", c_g1)])

                def gsum(eng):
                    eng.tensor_reduce(sm8[:, 0:8].rearrange("p (s o) -> p s o", o=1),
                                      scr1[:, 0:512].rearrange("p (s t) -> p s t", s=8),
                                      Ax.X, Alu.add)
                    eng.drain()
                    return eng.reciprocal(rs8[:, 0:8], sm8[:, 0:8])
                c_gs = op("vector", gsum, waits=[("ac", a_ge)])

                def gnorm(eng):
                    r = None
                    for s in range(8):
                        r = eng.activation(scr1[:, 512 + s * 64:512 + (s + 1) * 64],
                                           scr1[:, s * 64:(s + 1) * 64], Act.Copy,
                                           scale=rs8[:, s:s + 1])
                    return r
                a_gn = op("scalar", gnorm, waits=[("dv", c_gs)])

                last = None
                for p2 in range(4):
                    pair = H * 4 + p2
                    tcol = (p2 % 2) * 512
                    mm_tr = op("tensor", (lambda p3, tc: (lambda eng: eng.matmul(
                        ps_b1[0:128, tc:tc + 128], scr1[:, 512 + p3 * 128:512 + (p3 + 1) * 128],
                        idft[:, :], start=True, stop=True, is_transpose=True)))(p2, tcol),
                        waits=[("ac", a_gn)] if p2 == 0 else [("dv", p.at("dv"))])
                    last = op("vector", (lambda pr, tc, tt: (lambda eng: eng.tensor_copy(
                        gFM[:, pr * NT + tt * 128:pr * NT + (tt + 1) * 128],
                        ps_b1[0:128, tc:tc + 128])))(pair, tcol, t),
                        waits=[("mm", mm_tr)])
                gt_done.append(last)
        c_gates = p.at("dv")

        mix_evac = []
        for h in range(HC):
            pair, row = h // 2, (h % 2) * 64
            for half in range(2):
                mps = (ps_b2, ps_b3)[half]
                w = []
                if h == 0 and half == 0:
                    w = [("dv", c_gates)] + g_attn
                if len(mix_evac) >= 2:
                    w.append(("ac", mix_evac[-2]))

                def mmix(hh, pr, rw, hf, mp):
                    def f(eng):
                        return eng.matmul(
                            mp[:, 0:512], attn_sb[rw:rw + 64, pr * 128:(pr + 1) * 128],
                            gFM[rw:rw + 64, pr * NT + hf * NH:pr * NT + (hf + 1) * NH],
                            start=True, stop=True)
                    return f
                mm_m = op("tensor", mmix(h, pair, row, half, mps), waits=w)
                a_me = op("scalar", (lambda hh, hf, mp: (lambda eng: eng.activation(
                    big[:, hh * NT + hf * NH:hh * NT + (hf + 1) * NH], mp[:, 0:512],
                    Act.Copy)))(h, half, mps), waits=[("mm", mm_m)])
                mix_evac.append(a_me)
        a_mix = p.at("ac")

        mm_p = 0
        for oc in range(KC):
            wqv, cv = stream_slab(wp[oc * 128:(oc + 1) * 128, :], 2048, oc % 2,
                                  extra_waits=[("ac", a_mix)] if oc < 2 else ())
            for half in range(2):
                pps = (ps_b2, ps_b3)[half]

                def pmm(wt, hf, pp):
                    def f(eng):
                        r = None
                        for kc in range(KC):
                            r = eng.matmul(pp[:, 0:512], wt[:, kc * 128:(kc + 1) * 128],
                                           big[:, kc * NT + hf * NH:kc * NT + (hf + 1) * NH],
                                           start=(kc == 0), stop=(kc == KC - 1))
                        return r
                    return f
                w = [("dv", cv)]
                if oc > 0 or half > 0:
                    w.append(("dv", p.at("dv")))
                mm_p = op("tensor", pmm(wqv, half, pps), waits=w)
                op("vector", (lambda o2, hf, pp: (lambda eng: eng.tensor_tensor(
                    x_r[:, o2 * NT + hf * NH:o2 * NT + (hf + 1) * NH],
                    x_r[:, o2 * NT + hf * NH:o2 * NT + (hf + 1) * NH],
                    pp[:, 0:512], Alu.add)))(oc, half, pps), waits=[("mm", mm_p)])
            slab_used(oc % 2, mm_p)
        c_wp = p.at("dv")

        mm_d = 0
        for g in range(NG):
            for j in range(GSZ):
                ic = g * GSZ + j
                wqg, cvg = stream_fp8("gate", ic, 0,
                                      extra_waits=[("dv", c_wp)] if (g == 0 and j == 0) else ())

                def gmm2(wt):
                    def f(eng):
                        r = None
                        for half in range(2):
                            for kc in range(KC):
                                r = eng.matmul(ps_big[:, half * NH:(half + 1) * NH],
                                               wt[:, kc * 128:(kc + 1) * 128],
                                               xs_r[:, kc * NT + half * NH:kc * NT + (half + 1) * NH],
                                               start=(kc == 0), stop=(kc == KC - 1))
                        return r
                    return f
                w = [("dv", cvg)]
                if not (g == 0 and j == 0):
                    w.append(("ac", p.at("ac")))
                mm_gg = op("tensor", gmm2(wqg), waits=w)
                slab_used(0, mm_gg)

                wqu, cvu = stream_fp8("up", ic, 1)

                def umm(wt):
                    def f(eng):
                        r = None
                        for half in range(2):
                            for kc in range(KC):
                                r = eng.matmul(ps_big[:, 1024 + half * NH:1024 + (half + 1) * NH],
                                               wt[:, kc * 128:(kc + 1) * 128],
                                               xs_r[:, kc * NT + half * NH:kc * NT + (half + 1) * NH],
                                               start=(kc == 0), stop=(kc == KC - 1))
                        return r
                    return f
                mm_uu = op("tensor", umm(wqu), waits=[("dv", cvu)])
                slab_used(1, mm_uu)

                for half in range(2):
                    def epi(icc, hf):
                        def f(eng):
                            eng.activation(scr1[:, 0:NH], ps_big[:, hf * NH:(hf + 1) * NH],
                                           Act.Silu, scale=gs_t[:, icc:icc + 1])
                            return eng.activation(scr1[:, NH:NT],
                                                  ps_big[:, 1024 + hf * NH:1024 + (hf + 1) * NH],
                                                  Act.Copy, scale=us_t[:, icc:icc + 1])
                        return f
                    w_ep = [("mm", mm_uu)]
                    if half == 1:
                        w_ep.append(("dv", p.at("dv")))
                    a_ep = op("scalar", epi(ic, half), waits=w_ep)
                    op("vector", (lambda jj, hf: (lambda eng: eng.tensor_tensor(
                        big[:, jj * NT + hf * NH:jj * NT + (hf + 1) * NH],
                        scr1[:, 0:NH], scr1[:, NH:NT], Alu.mult)))(j, half),
                        waits=[("ac", a_ep)])
            c_fuse = p.at("dv")

            for oc in range(KC):
                wqd, cvd = stream_fp8("down", g * 16 + oc, oc % 2,
                                      extra_waits=[("dv", c_fuse)] if oc < 2 else ())
                for half in range(2):
                    def dmmf(wt, hf):
                        def f(eng):
                            r = None
                            for j in range(GSZ):
                                r = eng.matmul(ps_b1[:, hf * NH:(hf + 1) * NH],
                                               wt[:, j * 128:(j + 1) * 128],
                                               big[:, j * NT + hf * NH:j * NT + (hf + 1) * NH],
                                               start=(j == 0), stop=(j == GSZ - 1))
                            return r
                        return f
                    w = [("dv", cvd)]
                    if oc > 0 or half > 0 or g > 0:
                        w.append(("dv", p.at("dv")))
                    mm_d = op("tensor", dmmf(wqd, half), waits=w)
                    a_de = op("scalar", (lambda o2, hf: (lambda eng: eng.activation(
                        scr1[:, hf * NH:(hf + 1) * NH], ps_b1[:, hf * NH:(hf + 1) * NH],
                        Act.Copy, scale=ds_t[:, o2:o2 + 1])))(oc, half),
                        waits=[("mm", mm_d)])
                    op("vector", (lambda o2, hf: (lambda eng: eng.tensor_tensor(
                        x_r[:, o2 * NT + hf * NH:o2 * NT + (hf + 1) * NH],
                        x_r[:, o2 * NT + hf * NH:o2 * NT + (hf + 1) * NH],
                        scr1[:, hf * NH:(hf + 1) * NH], Alu.add)))(oc, half),
                        waits=[("ac", a_de)])
                slab_used(oc % 2, mm_d)
        c_ffn = p.at("dv")

        st_dmas = []
        for c in range(KC):
            st_dmas.append(dma("gpsimd", out[c * 128:(c + 1) * 128, :],
                               x_r[:, c * NT:(c + 1) * NT],
                               waits=[("dv", c_ffn)] if c == 0 else ()))
        final_waits = {}
        for (ln, v) in st_dmas:
            final_waits[ln] = max(final_waits.get(ln, 0), v)
        p.add("gpsimd", lambda eng: None, waits=sorted(final_waits.items()))

        sems = {}
        with contextlib.ExitStack() as sst:
            for s in (["dv", "ac", "mm", "cc"] + QLANES["sync"] + QLANES["gpsimd"]):
                sems[s] = sst.enter_context(nc.semaphore(s))
            with nc.Block() as block:
                def make(eng_name):
                    def stream(eng):
                        lastw = {}
                        for (e, waits, fn, incs) in p.ops:
                            if e != eng_name:
                                continue
                            for (s, v) in waits:
                                if v > 0 and lastw.get(s, -1) < v:
                                    eng.wait_ge(sems[s], v)
                                    lastw[s] = v
                            instr = fn(eng)
                            for (s, amt) in incs:
                                instr.then_inc(sems[s], amt)
                    return stream
                block.sync(make("sync"))
                block.gpsimd(make("gpsimd"))
                block.scalar(make("scalar"))
                block.vector(make("vector"))
                block.tensor(make("tensor"))
    return nc


def _host_layouts(inputs):
    x = np.asarray(inputs["L0_attn_x"], dtype=np.float32)
    Wk = np.asarray(inputs["L0_attn_Wk"], dtype=np.float32)
    Wv = np.asarray(inputs["L0_attn_Wv"], dtype=np.float32)
    Wg = np.asarray(inputs["L0_attn_Wg"], dtype=np.float32)
    Wp = np.asarray(inputs["L0_attn_Wp"], dtype=np.float32)
    gw = np.asarray(inputs["L0_ffn_gate_w"], dtype=np.float32)
    uw = np.asarray(inputs["L0_ffn_up_w"], dtype=np.float32)
    dw = np.asarray(inputs["L0_ffn_down_w"], dtype=np.float32)
    gs = np.asarray(inputs["L0_ffn_gate_scale"], dtype=np.float32)
    us = np.asarray(inputs["L0_ffn_up_scale"], dtype=np.float32)
    ds = np.asarray(inputs["L0_ffn_down_scale"], dtype=np.float32)
    lq = np.asarray(inputs["L0_attn_latent_queries"], dtype=np.float32)

    perm_d = np.concatenate([np.arange(0, 128, 2), np.arange(1, 128, 2)])

    Wk_p = Wk.reshape(C, HC, D)[:, :, perm_d].reshape(C, C)
    wk_r = np.ascontiguousarray(
        Wk_p.reshape(KC, 128, HC, D).transpose(2, 1, 0, 3).reshape(C, C))
    wv_r = np.ascontiguousarray(
        Wv.reshape(KC, 128, HC, D).transpose(2, 1, 0, 3).reshape(C, C))
    wp_r = np.ascontiguousarray(
        Wp.reshape(KC, 128, KC, 128).transpose(2, 1, 0, 3).reshape(C, C))
    wgate_r = gw.reshape(KC, 128, IC, 128).transpose(2, 1, 0, 3).reshape(IC * 128, C)
    wup_r = uw.reshape(KC, 128, IC, 128).transpose(2, 1, 0, 3).reshape(IC * 128, C)
    dwr = dw.reshape(NG, GSZ, 128, KC, 128).transpose(0, 3, 2, 1, 4)
    wdown_r = dwr.reshape(NG * KC * 128, GSZ * 128)
    srcs = {"gate": wgate_r, "up": wup_r, "down": wdown_r}
    qsrc_by_rank = {}
    for c in range(8):
        blocks = [srcs[k][i * 128:(i + 1) * 128, :]
                  for (k, i, r, c0, cs) in QTAB if r == c]
        qsrc_by_rank[c] = np.ascontiguousarray(
            np.concatenate(blocks, axis=1))
    lq_r = np.ascontiguousarray(
        lq[0].transpose(1, 2, 0)[:, perm_d, :].transpose(1, 0, 2).reshape(128, HC * L))

    import jax
    import jax.numpy as jnp
    cpu = jax.devices("cpu")[0]
    with jax.default_device(cpu):
        invf = 1.0 / 10000.0 ** (jnp.arange(0, 64, dtype=jnp.float32) * 2.0 / 128.0)
        ang = jnp.arange(128, dtype=jnp.float32)[:, None] * invf[None, :]
        cosv = np.asarray(jnp.cos(ang), dtype=np.float32)
        sinv = np.asarray(jnp.sin(ang), dtype=np.float32)
    rCh = np.empty((128, 128), np.float32)
    rSh = np.empty((128, 128), np.float32)
    rCh[0:64, :] = cosv.T
    rCh[64:128, :] = cosv.T
    rSh[0:64, :] = -sinv.T
    rSh[64:128, :] = sinv.T

    maska = np.where(np.arange(128)[None, :] <= np.arange(64)[:, None],
                     np.float32(0.0), np.float32(NEG)).astype(np.float32)
    idf = np.eye(128, dtype=np.float32)
    gs_r = np.ascontiguousarray(gs.reshape(IC, 128).T)
    us_r = np.ascontiguousarray(us.reshape(IC, 128).T)
    ds_r = np.ascontiguousarray(ds.reshape(KC, 128).T)

    shared = dict(wk=wk_r, wv=wv_r, wg=np.ascontiguousarray(Wg), wp=wp_r,
                  lq=lq_r, ropec=rCh, ropes=rSh, maska=maska, idf=idf,
                  gs=gs_r, us=us_r, ds=ds_r)
    in_maps = []
    for c in range(8):
        b, q = c // 4, c % 4
        m = dict(shared)
        m["xT"] = np.ascontiguousarray(x[b, q * NT:(q + 1) * NT, :].T)
        m["qsrc"] = qsrc_by_rank[c]
        m["xhT"] = np.ascontiguousarray(x[b, 0:128, :].T)
        in_maps.append(m)
    return in_maps


def kernel(**inputs):
    in_maps = _host_layouts(inputs)
    nc = build()
    res = run_bass_kernel_spmd(nc, in_maps, core_ids=list(range(8)))
    x = np.asarray(inputs["L0_attn_x"])
    outp = np.empty((2, 4096, C), dtype=np.float32)
    for c in range(8):
        b, q = c // 4, c % 4
        outp[b, q * NT:(q + 1) * NT, :] = np.asarray(res.results[c]["out"]).T
    return outp


# version 10
# speedup vs baseline: 4.0278x; 1.3126x over previous
import numpy as np
import ml_dtypes
import concourse.bass as bass
import concourse.mybir as mybir
from concourse.bass_utils import run_bass_kernel_spmd

F32 = mybir.dt.float32
BF16 = mybir.dt.bfloat16
FP8 = mybir.dt.float8e4
Alu = mybir.AluOpType
Act = mybir.ActivationFunctionType
Ax = mybir.AxisListType

C = 2048
NT = 1024
NH = 512
KC = 16
HC = 16
D = 128
L = 64
IC = 48
GSZ = 12
NG = IC // GSZ
SCALE = float(1.0 / np.sqrt(128.0))

QSH = 4718592
QCOLS = QSH // 128


def _build_qtab():
    tab = []
    off = 0
    for ic in range(IC):
        tab.append(["gate", ic, off, 2048])
        off += 128 * 2048
    for ic in range(IC):
        tab.append(["up", ic, off, 2048])
        off += 128 * 2048
    for gi in range(NG * 16):
        tab.append(["down", gi, off, 1536])
        off += 128 * 1536
    assert off == 8 * QSH
    out = []
    for kind, idx, off, cols in tab:
        rank = off // QSH
        assert (off + 128 * cols - 1) // QSH == rank
        col0 = (off - rank * QSH) // 128
        out.append((kind, idx, rank, col0, cols))
    return out


QTAB = _build_qtab()
QBYKIND = {(k, i): (r, c0, cs) for (k, i, r, c0, cs) in QTAB}
NEG = -1.0e9
EPS = 1e-5


class Plan:
    def __init__(self):
        self.ops = []
        self.cnt = {}

    def bump(self, sem, amt):
        self.cnt[sem] = self.cnt.get(sem, 0) + amt
        return self.cnt[sem]

    def at(self, sem):
        return self.cnt.get(sem, 0)

    def add(self, eng, fn, waits=(), inc=None):
        incs = []
        tgt = None
        if inc is not None:
            sem, amt = inc
            tgt = self.bump(sem, amt)
            incs = [(sem, amt)]
        self.ops.append((eng, tuple(waits), fn, tuple(incs)))
        return tgt


ESEM = {"vector": "dv", "scalar": "ac", "tensor": "mm"}
NLANES = 4
QLANES = {"sync": ["sq%d" % i for i in range(NLANES)],
          "gpsimd": ["gq%d" % i for i in range(NLANES)]}


def build():
    nc = bass.Bass("TRN2", target_bir_lowering=False)
    p = Plan()

    def dram_in(name, shape, dt=F32):
        return nc.dram_tensor(name, shape, dt, kind="ExternalInput").ap()

    xT = dram_in("xT", [C, NT])
    xhT = dram_in("xhT", [C, 128])
    wk = dram_in("wk", [C, C])
    wv = dram_in("wv", [C, C])
    wg = dram_in("wg", [C, HC * L])
    wp = dram_in("wp", [C, C])
    qsrc = dram_in("qsrc", [128, QCOLS])
    qshard = nc.dram_tensor("qshard", [128, QCOLS], FP8, kind="Internal").ap()
    qfull = nc.dram_tensor("qfull", [8, 128, QCOLS], FP8, kind="Internal").ap()
    lqd = dram_in("lq", [128, HC * L])
    ropec = dram_in("ropec", [128, 128])
    ropes = dram_in("ropes", [128, 128])
    maska = dram_in("maska", [64, 128])
    idf = dram_in("idf", [128, 128])
    gsd = dram_in("gs", [128, IC])
    usd = dram_in("us", [128, IC])
    dsd = dram_in("ds", [128, KC])
    out = nc.dram_tensor("out", [C, NT], F32, kind="ExternalOutput").ap()

    import contextlib
    st = contextlib.ExitStack()
    sb = lambda name, shape, dt: st.enter_context(nc.sbuf_tensor(name, shape, dt))
    psb = lambda name, shape, dt: st.enter_context(nc.psum_tensor(name, shape, dt))

    with st:
        x_r = sb("x_r", [128, KC * NT], F32)
        xs_r = sb("xs_r", [128, KC * NT], BF16)
        xh_r = sb("xh_r", [128, KC * 128], F32)
        xsh_r = sb("xsh_r", [128, KC * 128], BF16)
        big = sb("big", [128, KC * NT], BF16)
        gFM = sb("gFM", [128, 8 * NT], BF16)
        wfa = sb("wfa", [128, 2048], F32)
        wfb = sb("wfb", [128, 2048], F32)
        wqa = sb("wqa", [128, 2048], BF16)
        wqb = sb("wqb", [128, 2048], BF16)
        f8x = sb("f8x", [128, 2048], FP8)
        f8wb = sb("f8wb", [128, 2048], FP8)
        scr1 = sb("scr1", [128, NT], F32)
        scr3 = sb("scr3", [128, 128], F32)
        scr4 = sb("scr4", [128, 128], F32)
        lqb = sb("lqb", [128, HC * L], BF16)
        rC = sb("rC", [128, 128], F32)
        rS = sb("rS", [128, 128], F32)
        mskt = sb("mskt", [64, 128], F32)
        idft = sb("idft", [128, 128], F32)
        ksw = sb("ksw", [128, 128], F32)
        k_sb = sb("k_sb", [128, HC * 128], BF16)
        v_sb = sb("v_sb", [128, HC * 128], BF16)

        wT_sb = sb("wT_sb", [128, 2 * 64], BF16)
        attn_sb = sb("attn_sb", [128, 8 * 128], BF16)
        atmp = sb("atmp", [64, 128], BF16)
        vech = sb("vech", [4, 128], F32)
        vec_hm, vec_hr, vec_h1, vec_h2 = (vech[0:1, :], vech[1:2, :],
                                          vech[2:3, :], vech[3:4, :])
        ones_c = sb("ones_c", [128, 1], F32)
        eps_t = sb("eps_t", [128, 1], F32)
        ones_r2 = sb("ones_r2", [1, 128], F32)
        ones_r = ones_r2[0:1, :]
        gs_t = sb("gs_t", [128, IC], F32)
        us_t = sb("us_t", [128, IC], F32)
        ds_t = sb("ds_t", [128, KC], F32)
        mx8 = sb("mx8", [128, 8], F32)
        nm8 = sb("nm8", [128, 8], F32)
        sm8 = sb("sm8", [128, 8], F32)
        rs8 = sb("rs8", [128, 8], F32)
        sc_mx = sb("sc_mx", [64, 1], F32)
        sc_nm = sb("sc_nm", [64, 1], F32)
        sc_s = sb("sc_s", [64, 1], F32)
        sc_r = sb("sc_r", [64, 1], F32)

        kt1 = scr3
        smt = scr3[0:64, :]
        ewt = scr4[0:64, :]
        wfp = ksw[0:64, :]
        ps_big = psb("ps_big", [128, 2048], F32)
        ps_b1 = psb("ps_b1", [128, 1024], F32)
        ps_b2 = psb("ps_b2", [128, 512], F32)
        ps_b3 = psb("ps_b3", [128, 512], F32)

        lane_idx = {"sync": 0, "gpsimd": 0}

        def dma(q, out_ap, in_ap, waits=()):
            lane = QLANES[q][lane_idx[q] % NLANES]
            lane_idx[q] += 1
            w = list(waits)
            if p.at(lane) > 0:
                w.append((lane, p.at(lane)))
            fn = (lambda o, i: (lambda eng: eng.dma_start(out=o, in_=i)))(out_ap, in_ap)
            cnt = p.add(q, fn, waits=w, inc=(lane, 16))
            return (lane, cnt)

        def op(eng, fn, waits=()):
            return p.add(eng, fn, waits=waits, inc=(ESEM[eng], 1))

        d_lq = dma("gpsimd", wfa[:, 0:HC * L], lqd[:, :])
        dma("gpsimd", rC[:, :], ropec[:, :])
        dma("gpsimd", rS[:, :], ropes[:, :])
        dma("gpsimd", mskt[:, :], maska[:, :])
        dma("gpsimd", idft[:, :], idf[:, :])
        dma("gpsimd", gs_t[:, :], gsd[:, :])
        dma("gpsimd", us_t[:, :], usd[:, :])
        dma("gpsimd", ds_t[:, :], dsd[:, :])
        d_xh = [dma("sync", xh_r[:, c * 128:(c + 1) * 128], xhT[c * 128:(c + 1) * 128, :])
                for c in range(KC)]
        d_x = [dma("sync", x_r[:, c * NT:(c + 1) * NT], xT[c * 128:(c + 1) * 128, :])
               for c in range(KC)]

        op("vector", lambda eng: eng.memset(ones_c[:, :], 1.0))
        op("vector", lambda eng: eng.memset(ones_r, 1.0))
        op("vector", lambda eng: eng.memset(eps_t[:, :], EPS))
        c_ones = p.at("dv")
        op("vector", lambda eng: eng.tensor_copy(lqb[:, :], wfa[:, 0:HC * L]),
           waits=[d_lq])
        slab0_init = p.at("dv")

        sq_done = []
        for c in range(KC):
            w = [d_x[c]]
            if c >= 1:
                w.append(("mm", sq_done[c - 1]))
            scr = scr1
            op("scalar", (lambda cc, s: (lambda eng: eng.activation(
                s[:, :], x_r[:, cc * NT:(cc + 1) * NT], Act.Square)))(c, scr), waits=w)
            a_sq = p.at("ac")
            last = None
            for h in range(2):
                def mmfn(cc, s, hh):
                    def f(eng):
                        eng.matmul(ps_big[0:1, hh * NH:(hh + 1) * NH], ones_c[:, :],
                                   x_r[:, cc * NT + hh * NH: cc * NT + (hh + 1) * NH],
                                   start=(cc == 0), stop=(cc == KC - 1))
                        return eng.matmul(ps_big[0:1, 1024 + hh * NH:1024 + (hh + 1) * NH],
                                          ones_c[:, :], s[:, hh * NH:(hh + 1) * NH],
                                          start=(cc == 0), stop=(cc == KC - 1))
                    return f
                wmm = [("ac", a_sq)] + ([("dv", c_ones)] if c == 0 else [])
                last = op("tensor", mmfn(c, scr, h), waits=wmm)
            sq_done.append(last)
        mm_stats = p.at("mm")

        hs_done = []
        for c in range(KC):
            w = [d_xh[c]]
            if c >= 1:
                w.append(("mm", hs_done[c - 1]))
            op("scalar", (lambda cc: (lambda eng: eng.activation(
                scr3[:, :], xh_r[:, cc * 128:(cc + 1) * 128], Act.Square)))(c), waits=w)
            a_sq = p.at("ac")

            def mmfn(cc):
                def f(eng):
                    eng.matmul(ps_b1[0:1, 0:128], ones_c[:, :],
                               xh_r[:, cc * 128:(cc + 1) * 128],
                               start=(cc == 0), stop=(cc == KC - 1))
                    return eng.matmul(ps_b1[0:1, 512:640], ones_c[:, :], scr3[:, :],
                                      start=(cc == 0), stop=(cc == KC - 1))
                return f
            hs_done.append(op("tensor", mmfn(c), waits=[("ac", a_sq)]))
        mm_hstats = p.at("mm")

        inv_n = 1.0 / float(C)
        A = wfa[0:1, 0:1024]
        B = scr1[0:1, 0:1024]

        op("vector", lambda eng: eng.tensor_scalar(A, ps_big[0:1, 0:1024], inv_n,
                                                   None, Alu.mult),
           waits=[("mm", mm_hstats), ("dv", slab0_init)])
        c_m = p.at("dv")

        def bcast_m(eng):
            r = None
            for h in range(2):
                r = eng.matmul(ps_big[:, h * NH:(h + 1) * NH], ones_r,
                               wfa[0:1, h * NH:(h + 1) * NH], start=True, stop=True)
            return r
        mm_bm = op("tensor", bcast_m, waits=[("dv", c_m)])

        def fin_v(eng):
            eng.tensor_scalar(B, ps_big[0:1, 1024:2048], inv_n, None, Alu.mult)
            eng.tensor_tensor(A, A, A, Alu.mult)
            eng.drain()
            return eng.tensor_tensor(B, B, A, Alu.subtract)
        op("vector", fin_v, waits=[("mm", mm_bm)])
        c_var = p.at("dv")
        op("scalar", lambda eng: eng.activation(A, B, Act.Sqrt, bias=eps_t[0:1, :]),
           waits=[("dv", c_var)])
        a_std = p.at("ac")
        op("vector", lambda eng: eng.reciprocal(B, A), waits=[("ac", a_std)])
        c_r = p.at("dv")

        def bcast_r(eng):
            r = None
            for h in range(2):
                r = eng.matmul(ps_big[:, 1024 + h * NH:1024 + (h + 1) * NH], ones_r,
                               scr1[0:1, h * NH:(h + 1) * NH], start=True, stop=True)
            return r
        mm_br = op("tensor", bcast_r, waits=[("dv", c_r)])

        A2 = vech[0:1, :]
        B2 = scr1[0:1, 0:128]
        op("vector", lambda eng: eng.tensor_scalar(A2, ps_b1[0:1, 0:128], inv_n,
                                                   None, Alu.mult),
           waits=[("mm", mm_br)])
        c_hm = p.at("dv")
        mm_bhm = op("tensor", lambda eng: eng.matmul(ps_b2[:, 0:128], ones_r, A2,
                                                     start=True, stop=True),
                    waits=[("dv", c_hm)])

        def fin_hv(eng):
            eng.tensor_scalar(B2, ps_b1[0:1, 512:640], inv_n, None, Alu.mult)
            eng.tensor_tensor(A2, A2, A2, Alu.mult)
            eng.drain()
            return eng.tensor_tensor(B2, B2, A2, Alu.subtract)
        op("vector", fin_hv, waits=[("mm", mm_bhm)])
        c_hv = p.at("dv")
        op("scalar", lambda eng: eng.activation(A2, B2, Act.Sqrt, bias=eps_t[0:1, :]),
           waits=[("dv", c_hv)])
        a_hstd = p.at("ac")
        op("vector", lambda eng: eng.reciprocal(B2, A2), waits=[("ac", a_hstd)])
        c_hr = p.at("dv")
        mm_bcast = op("tensor", lambda eng: eng.matmul(ps_b2[:, 128:256], ones_r, B2,
                                                       start=True, stop=True),
                      waits=[("dv", c_hr)])

        for c in range(KC):
            def xnfn(cc):
                def f(eng):
                    eng.drain()
                    eng.tensor_tensor(scr1[:, :], x_r[:, cc * NT:(cc + 1) * NT],
                                      ps_big[:, 0:1024], Alu.subtract)
                    eng.drain()
                    return eng.tensor_tensor(xs_r[:, cc * NT:(cc + 1) * NT], scr1[:, :],
                                             ps_big[:, 1024:2048], Alu.mult)
                return f
            op("vector", xnfn(c), waits=[("mm", mm_bcast)] if c == 0 else ())
        for c in range(KC):
            def xnhfn(cc):
                def f(eng):
                    eng.drain()
                    eng.tensor_tensor(scr3[:, :], xh_r[:, cc * 128:(cc + 1) * 128],
                                      ps_b2[:, 0:128], Alu.subtract)
                    eng.drain()
                    return eng.tensor_tensor(xsh_r[:, cc * 128:(cc + 1) * 128], scr3[:, :],
                                             ps_b2[:, 128:256], Alu.mult)
                return f
            op("vector", xnhfn(c))
        c_xn = p.at("dv")

        slab_state = {"dv": [slab0_init, 0], "mm": [0, 0]}
        QSTG = [scr1[:, 0:512], scr1[:, 512:1024]]
        qstate = {"d2": [0, 0], "f8dma": [None] * 4}

        def quant_piece(pi, extra_waits=()):
            setid = pi % 2
            stg = QSTG[setid]
            w0 = list(extra_waits)
            if qstate["d2"][setid]:
                w0.append(("ac", qstate["d2"][setid]))
            dsrc = dma("sync", stg, qsrc[:, pi * 512:(pi + 1) * 512], waits=w0)
            fq = f8x[:, (pi % 4) * 512:(pi % 4 + 1) * 512]
            w2 = [dsrc]
            if qstate["f8dma"][pi % 4] is not None:
                w2.append(qstate["f8dma"][pi % 4])
            a2 = op("scalar", (lambda f_, s_: (lambda eng: eng.activation(
                f_, s_, Act.Copy)))(fq, stg), waits=w2)
            qstate["d2"][setid] = a2
            qstate["f8dma"][pi % 4] = dma(
                "gpsimd", qshard[:, pi * 512:(pi + 1) * 512], fq,
                waits=[("ac", a2)])
            return a2

        def stream_slab(src_ap, ncols, slot, extra_waits=()):
            wf = (wfa, wfb)[slot]
            wq = (wqa, wqb)[slot]
            w = list(extra_waits)
            if slab_state["dv"][slot]:
                w.append(("dv", slab_state["dv"][slot]))
            d = dma("sync", wf[:, 0:ncols], src_ap, waits=w)
            w2 = [d]
            if slab_state["mm"][slot]:
                w2.append(("mm", slab_state["mm"][slot]))

            def cast(eng):
                return eng.tensor_copy(wq[:, 0:ncols], wf[:, 0:ncols])
            cv = op("vector", cast, waits=w2)
            slab_state["dv"][slot] = cv
            return wq, cv

        f8state = {"cc": False}

        def stream_fp8(kind, idx, slot, extra_waits=()):
            rank, col0, cols = QBYKIND[(kind, idx)]
            buf = (f8x, f8wb)[slot]
            w = list(extra_waits)
            if not f8state["cc"]:
                w.append(("cc", 1))
                f8state["cc"] = True
            if slab_state["mm"][slot]:
                w.append(("mm", slab_state["mm"][slot]))
            dmas = []
            c0 = 0
            while c0 < cols:
                cw = min(1024, cols - c0)
                dmas.append(dma("sync", buf[:, c0:c0 + cw],
                                qfull[rank, :, col0 + c0:col0 + c0 + cw],
                                waits=(w if c0 == 0 else ())))
                c0 += cw
            return buf, dmas

        def slab_used(slot, mm_count):
            slab_state["mm"][slot] = mm_count

        phq = {"next": 0}

        def phq_emit(n):
            for _ in range(n):
                if phq["next"] < 72:
                    ew = [("dv", c_xn)] if phq["next"] < 2 else ()
                    quant_piece(phq["next"], extra_waits=ew)
                    phq["next"] += 1

        for h in range(HC):
            wqv, cv = stream_slab(wk[h * 128:(h + 1) * 128, :], 2048, h % 2,
                                  extra_waits=[("mm", mm_bcast)] if h < 2 else ())

            def kmm(hh, wt):
                def f(eng):
                    r = None
                    for kc in range(KC):
                        r = eng.matmul(ps_big[:, hh * 128:(hh + 1) * 128],
                                       wt[:, kc * 128:(kc + 1) * 128],
                                       xsh_r[:, kc * 128:(kc + 1) * 128],
                                       start=(kc == 0), stop=(kc == KC - 1))
                    return r
                return f
            m = op("tensor", kmm(h, wqv), waits=[("dv", cv)])
            slab_used(h % 2, m)
            phq_emit(2)
        mm_k = p.at("mm")

        def vb(h):
            if h < 8:
                return ps_b1[:, h * 128:(h + 1) * 128]
            if h < 12:
                return ps_b2[:, (h - 8) * 128:(h - 7) * 128]
            return ps_b3[:, (h - 12) * 128:(h - 11) * 128]

        for h in range(HC):
            wqv, cv = stream_slab(wv[h * 128:(h + 1) * 128, :], 2048, h % 2)

            def vmm(hh, wt):
                def f(eng):
                    r = None
                    for kc in range(KC):
                        r = eng.matmul(vb(hh), xsh_r[:, kc * 128:(kc + 1) * 128],
                                       wt[:, kc * 128:(kc + 1) * 128],
                                       start=(kc == 0), stop=(kc == KC - 1))
                    return r
                return f
            m = op("tensor", vmm(h, wqv), waits=[("dv", cv)])
            slab_used(h % 2, m)
            phq_emit(2)
        mm_v = p.at("mm")

        for h in range(HC):
            op("scalar", (lambda hh: (lambda eng: eng.activation(
                v_sb[:, hh * 128:(hh + 1) * 128], vb(hh), Act.Copy)))(h),
               waits=[("mm", mm_v)] if h == 0 else ())
        a_vevac = p.at("ac")

        rope_done = []
        for h in range(HC):
            w = [("mm", mm_k), ("ac", a_vevac)] if h == 0 else [("dv", rope_done[-1])]
            a_kf = op("scalar", (lambda hh: (lambda eng: eng.activation(
                scr4[:, :], ps_big[:, hh * 128:(hh + 1) * 128], Act.Copy)))(h), waits=w)
            dsw1 = dma("gpsimd", ksw[0:64, :], scr4[64:128, :], waits=[("ac", a_kf)])
            dsw2 = dma("gpsimd", ksw[64:128, :], scr4[0:64, :])

            def rope(hh):
                def f(eng):
                    eng.drain()
                    eng.tensor_tensor(kt1[:, :], scr4[:, :], rC[:, :], Alu.mult)
                    eng.tensor_tensor(ksw[:, :], ksw[:, :], rS[:, :], Alu.mult)
                    eng.drain()
                    return eng.tensor_tensor(k_sb[:, hh * 128:(hh + 1) * 128],
                                             kt1[:, :], ksw[:, :], Alu.add)
                return f
            rope_done.append(op("vector", rope(h), waits=[dsw1, dsw2]))
        c_ropes = p.at("dv")

        attn_tail = []
        shift_dmas = []
        for h in range(HC):
            sps = (ps_b2, ps_b3)[h % 2]
            w = [("dv", c_ropes)] if h == 0 else []
            if h >= 2:
                w.append(("dv", attn_tail[h - 2]))
            mm_sc = op("tensor", (lambda hh, sp: (lambda eng: eng.matmul(
                sp[0:64, 0:128], lqb[:, hh * L:(hh + 1) * L],
                k_sb[:, hh * 128:(hh + 1) * 128], start=True, stop=True)))(h, sps),
                waits=w)

            def smax(hh, sp):
                def f(eng):
                    eng.drain()
                    eng.scalar_tensor_tensor(smt, sp[0:64, 0:128], SCALE, mskt[:, :],
                                             Alu.mult, Alu.add)
                    eng.drain()
                    eng.reduce_max(sc_mx[:, :], smt, Ax.X)
                    eng.drain()
                    return eng.tensor_scalar(sc_nm[:, :], sc_mx[:, :], -1.0, None, Alu.mult)
                return f
            c_sm = op("vector", smax(h, sps), waits=[("mm", mm_sc)])
            a_exp = op("scalar", lambda eng: eng.activation(ewt, smt, Act.Exp,
                                                            bias=sc_nm[:, :]),
                       waits=[("dv", c_sm)])

            def norm(eng):
                eng.reduce_sum(sc_s[:, :], ewt, Ax.X)
                eng.drain()
                return eng.reciprocal(sc_r[:, :], sc_s[:, :])
            c_nrm = op("vector", norm, waits=[("ac", a_exp)])
            a_wn = op("scalar", lambda eng: eng.activation(wfp, ewt, Act.Copy,
                                                           scale=sc_r[:, :]),
                      waits=[("dv", c_nrm)])
            tcol = (h % 2) * 512
            mm_t = op("tensor", (lambda tc: (lambda eng: eng.matmul(
                ps_big[0:128, tc:tc + 64], wfp, idft[0:64, 0:64], start=True, stop=True,
                is_transpose=True)))(tcol), waits=[("ac", a_wn)])
            c_wt = op("vector", (lambda hh, tc: (lambda eng: eng.tensor_copy(
                wT_sb[:, (hh % 2) * 64:(hh % 2 + 1) * 64], ps_big[0:128, tc:tc + 64])))(h, tcol),
                waits=[("mm", mm_t)])
            acol = 1024 + (h % 2) * 512
            mm_a = op("tensor", (lambda hh, cl: (lambda eng: eng.matmul(
                ps_big[0:64, cl:cl + 128], wT_sb[:, (hh % 2) * 64:(hh % 2 + 1) * 64],
                v_sb[:, hh * 128:(hh + 1) * 128], start=True, stop=True)))(h, acol),
                waits=[("dv", c_wt)])
            if h % 2 == 0:
                c_ae = op("vector", (lambda hh, cl: (lambda eng: eng.tensor_copy(
                    attn_sb[0:64, (hh // 2) * 128:(hh // 2 + 1) * 128],
                    ps_big[0:64, cl:cl + 128])))(h, acol), waits=[("mm", mm_a)])
            else:
                c_ae = op("vector", (lambda cl: (lambda eng: eng.tensor_copy(
                    atmp[:, :], ps_big[0:64, cl:cl + 128])))(acol),
                    waits=[("mm", mm_a)] + ([shift_dmas[-1]] if shift_dmas else []))
                shift_dmas.append(
                    dma("gpsimd", attn_sb[64:128, (h // 2) * 128:(h // 2 + 1) * 128],
                        atmp[:, :], waits=[("dv", c_ae)]))
            attn_tail.append(c_ae)
            phq_emit(1)
        c_attn = p.at("dv")
        g_attn = list(shift_dmas)
        phq_emit(72)
        ag_waits = {}
        for t_ in qstate["f8dma"]:
            if t_ is not None:
                ln, v = t_
                ag_waits[ln] = max(ag_waits.get(ln, 0), v)
        op_ag = p.add("gpsimd", lambda eng: eng.collective_compute(
            "AllGather", Alu.bypass, ins=[qshard[:, :]], outs=[qfull[:, :, :]],
            replica_groups=[[0, 1, 2, 3, 4, 5, 6, 7]]),
            waits=sorted(ag_waits.items()), inc=("cc", 1))

        for H in range(2):
            wg_cast = []
            for kc in range(KC):
                if kc < 2:
                    w = [("dv", p.at("dv"))] if H == 0 else [("mm", p.at("mm"))]
                else:
                    w = [("dv", wg_cast[kc - 2])]
                d = dma("sync", (wfa, wfb)[kc % 2][:, 0:512],
                        wg[kc * 128:(kc + 1) * 128, H * 512:(H + 1) * 512], waits=w)
                cv = op("vector", (lambda k2: (lambda eng: eng.tensor_copy(
                    big[:, k2 * 512:(k2 + 1) * 512],
                    (wfa, wfb)[k2 % 2][:, 0:512])))(kc), waits=[d])
                wg_cast.append(cv)
            c_wgh = p.at("dv")

            gt_done = []
            for t in range(8):
                gps = (ps_b2, ps_b3)[t % 2]

                def gmm(tt, gp):
                    def f(eng):
                        r = None
                        for kc in range(KC):
                            r = eng.matmul(gp[:, 0:512],
                                           xs_r[:, kc * NT + tt * 128:kc * NT + (tt + 1) * 128],
                                           big[:, kc * 512:(kc + 1) * 512],
                                           start=(kc == 0), stop=(kc == KC - 1))
                        return r
                    return f
                w = [("dv", c_wgh)]
                if t >= 2:
                    w.append(("dv", gt_done[t - 2]))
                elif H == 0:
                    w.append(("dv", c_attn))
                mm_g = op("tensor", gmm(t, gps), waits=w)

                def gsm1(gp):
                    def f(eng):
                        eng.tensor_reduce(mx8[:, 0:8].rearrange("p (s o) -> p s o", o=1),
                                          gp[:, 0:512].rearrange("p (s t) -> p s t", s=8),
                                          Ax.X, Alu.max)
                        eng.drain()
                        return eng.tensor_scalar(nm8[:, 0:8], mx8[:, 0:8], -SCALE, None,
                                                 Alu.mult)
                    return f
                c_g1 = op("vector", gsm1(gps), waits=[("mm", mm_g)])

                def gexp(gp):
                    def f(eng):
                        r = None
                        for s in range(8):
                            r = eng.activation(scr1[:, s * 64:(s + 1) * 64],
                                               gp[:, s * 64:(s + 1) * 64], Act.Exp,
                                               bias=nm8[:, s:s + 1], scale=SCALE)
                        return r
                    return f
                a_ge = op("scalar", gexp(gps), waits=[("dv", c_g1)])

                def gsum(eng):
                    eng.tensor_reduce(sm8[:, 0:8].rearrange("p (s o) -> p s o", o=1),
                                      scr1[:, 0:512].rearrange("p (s t) -> p s t", s=8),
                                      Ax.X, Alu.add)
                    eng.drain()
                    return eng.reciprocal(rs8[:, 0:8], sm8[:, 0:8])
                c_gs = op("vector", gsum, waits=[("ac", a_ge)])

                def gnorm(eng):
                    r = None
                    for s in range(8):
                        r = eng.activation(scr1[:, 512 + s * 64:512 + (s + 1) * 64],
                                           scr1[:, s * 64:(s + 1) * 64], Act.Copy,
                                           scale=rs8[:, s:s + 1])
                    return r
                a_gn = op("scalar", gnorm, waits=[("dv", c_gs)])

                last = None
                for p2 in range(4):
                    pair = H * 4 + p2
                    tcol = (p2 % 2) * 512
                    mm_tr = op("tensor", (lambda p3, tc: (lambda eng: eng.matmul(
                        ps_b1[0:128, tc:tc + 128], scr1[:, 512 + p3 * 128:512 + (p3 + 1) * 128],
                        idft[:, :], start=True, stop=True, is_transpose=True)))(p2, tcol),
                        waits=[("ac", a_gn)] if p2 == 0 else [("dv", p.at("dv"))])
                    last = op("vector", (lambda pr, tc, tt: (lambda eng: eng.tensor_copy(
                        gFM[:, pr * NT + tt * 128:pr * NT + (tt + 1) * 128],
                        ps_b1[0:128, tc:tc + 128])))(pair, tcol, t),
                        waits=[("mm", mm_tr)])
                gt_done.append(last)
        c_gates = p.at("dv")

        mix_evac = []
        for h in range(HC):
            pair, row = h // 2, (h % 2) * 64
            for half in range(2):
                mps = (ps_b2, ps_b3)[half]
                w = []
                if h == 0 and half == 0:
                    w = [("dv", c_gates)] + g_attn
                if len(mix_evac) >= 2:
                    w.append(("ac", mix_evac[-2]))

                def mmix(hh, pr, rw, hf, mp):
                    def f(eng):
                        return eng.matmul(
                            mp[:, 0:512], attn_sb[rw:rw + 64, pr * 128:(pr + 1) * 128],
                            gFM[rw:rw + 64, pr * NT + hf * NH:pr * NT + (hf + 1) * NH],
                            start=True, stop=True)
                    return f
                mm_m = op("tensor", mmix(h, pair, row, half, mps), waits=w)
                a_me = op("scalar", (lambda hh, hf, mp: (lambda eng: eng.activation(
                    big[:, hh * NT + hf * NH:hh * NT + (hf + 1) * NH], mp[:, 0:512],
                    Act.Copy)))(h, half, mps), waits=[("mm", mm_m)])
                mix_evac.append(a_me)
        a_mix = p.at("ac")

        mm_p = 0
        for oc in range(KC):
            wqv, cv = stream_slab(wp[oc * 128:(oc + 1) * 128, :], 2048, oc % 2,
                                  extra_waits=[("ac", a_mix)] if oc < 2 else ())
            for half in range(2):
                pps = (ps_b2, ps_b3)[half]

                def pmm(wt, hf, pp):
                    def f(eng):
                        r = None
                        for kc in range(KC):
                            r = eng.matmul(pp[:, 0:512], wt[:, kc * 128:(kc + 1) * 128],
                                           big[:, kc * NT + hf * NH:kc * NT + (hf + 1) * NH],
                                           start=(kc == 0), stop=(kc == KC - 1))
                        return r
                    return f
                w = [("dv", cv)]
                if oc > 0 or half > 0:
                    w.append(("dv", p.at("dv")))
                mm_p = op("tensor", pmm(wqv, half, pps), waits=w)
                op("vector", (lambda o2, hf, pp: (lambda eng: eng.tensor_tensor(
                    x_r[:, o2 * NT + hf * NH:o2 * NT + (hf + 1) * NH],
                    x_r[:, o2 * NT + hf * NH:o2 * NT + (hf + 1) * NH],
                    pp[:, 0:512], Alu.add)))(oc, half, pps), waits=[("mm", mm_p)])
            slab_used(oc % 2, mm_p)
        c_wp = p.at("dv")

        mm_d = 0
        for g in range(NG):
            for j in range(GSZ):
                ic = g * GSZ + j
                wqg, dmg = stream_fp8("gate", ic, 0,
                                      extra_waits=[("dv", c_wp)] if (g == 0 and j == 0) else ())

                def gmm2(wt):
                    def f(eng):
                        r = None
                        for half in range(2):
                            for kc in range(KC):
                                r = eng.matmul(ps_big[:, half * NH:(half + 1) * NH],
                                               wt[:, kc * 128:(kc + 1) * 128],
                                               xs_r[:, kc * NT + half * NH:kc * NT + (half + 1) * NH],
                                               start=(kc == 0), stop=(kc == KC - 1))
                        return r
                    return f
                w = list(dmg)
                if not (g == 0 and j == 0):
                    w.append(("ac", p.at("ac")))
                mm_gg = op("tensor", gmm2(wqg), waits=w)
                slab_used(0, mm_gg)

                wqu, dmu = stream_fp8("up", ic, 1)

                def umm(wt):
                    def f(eng):
                        r = None
                        for half in range(2):
                            for kc in range(KC):
                                r = eng.matmul(ps_big[:, 1024 + half * NH:1024 + (half + 1) * NH],
                                               wt[:, kc * 128:(kc + 1) * 128],
                                               xs_r[:, kc * NT + half * NH:kc * NT + (half + 1) * NH],
                                               start=(kc == 0), stop=(kc == KC - 1))
                        return r
                    return f
                mm_uu = op("tensor", umm(wqu), waits=list(dmu))
                slab_used(1, mm_uu)

                for half in range(2):
                    def epi(icc, hf):
                        def f(eng):
                            eng.activation(scr1[:, 0:NH], ps_big[:, hf * NH:(hf + 1) * NH],
                                           Act.Silu, scale=gs_t[:, icc:icc + 1])
                            return eng.activation(scr1[:, NH:NT],
                                                  ps_big[:, 1024 + hf * NH:1024 + (hf + 1) * NH],
                                                  Act.Copy, scale=us_t[:, icc:icc + 1])
                        return f
                    w_ep = [("mm", mm_uu)]
                    if half == 1:
                        w_ep.append(("dv", p.at("dv")))
                    a_ep = op("scalar", epi(ic, half), waits=w_ep)
                    op("vector", (lambda jj, hf: (lambda eng: eng.tensor_tensor(
                        big[:, jj * NT + hf * NH:jj * NT + (hf + 1) * NH],
                        scr1[:, 0:NH], scr1[:, NH:NT], Alu.mult)))(j, half),
                        waits=[("ac", a_ep)])
            c_fuse = p.at("dv")

            for oc in range(KC):
                wqd, dmd = stream_fp8("down", g * 16 + oc, oc % 2,
                                      extra_waits=[("dv", c_fuse)] if oc < 2 else ())
                for half in range(2):
                    def dmmf(wt, hf):
                        def f(eng):
                            r = None
                            for j in range(GSZ):
                                r = eng.matmul(ps_b1[:, hf * NH:(hf + 1) * NH],
                                               wt[:, j * 128:(j + 1) * 128],
                                               big[:, j * NT + hf * NH:j * NT + (hf + 1) * NH],
                                               start=(j == 0), stop=(j == GSZ - 1))
                            return r
                        return f
                    w = list(dmd)
                    if oc > 0 or half > 0 or g > 0:
                        w.append(("dv", p.at("dv")))
                    mm_d = op("tensor", dmmf(wqd, half), waits=w)
                    a_de = op("scalar", (lambda o2, hf: (lambda eng: eng.activation(
                        scr1[:, hf * NH:(hf + 1) * NH], ps_b1[:, hf * NH:(hf + 1) * NH],
                        Act.Copy, scale=ds_t[:, o2:o2 + 1])))(oc, half),
                        waits=[("mm", mm_d)])
                    op("vector", (lambda o2, hf: (lambda eng: eng.tensor_tensor(
                        x_r[:, o2 * NT + hf * NH:o2 * NT + (hf + 1) * NH],
                        x_r[:, o2 * NT + hf * NH:o2 * NT + (hf + 1) * NH],
                        scr1[:, hf * NH:(hf + 1) * NH], Alu.add)))(oc, half),
                        waits=[("ac", a_de)])
                slab_used(oc % 2, mm_d)
        c_ffn = p.at("dv")

        st_dmas = []
        for c in range(KC):
            st_dmas.append(dma("gpsimd", out[c * 128:(c + 1) * 128, :],
                               x_r[:, c * NT:(c + 1) * NT],
                               waits=[("dv", c_ffn)] if c == 0 else ()))
        final_waits = {}
        for (ln, v) in st_dmas:
            final_waits[ln] = max(final_waits.get(ln, 0), v)
        p.add("gpsimd", lambda eng: None, waits=sorted(final_waits.items()))

        sems = {}
        with contextlib.ExitStack() as sst:
            for s in (["dv", "ac", "mm", "cc"] + QLANES["sync"] + QLANES["gpsimd"]):
                sems[s] = sst.enter_context(nc.semaphore(s))
            with nc.Block() as block:
                def make(eng_name):
                    def stream(eng):
                        lastw = {}
                        for (e, waits, fn, incs) in p.ops:
                            if e != eng_name:
                                continue
                            for (s, v) in waits:
                                if v > 0 and lastw.get(s, -1) < v:
                                    eng.wait_ge(sems[s], v)
                                    lastw[s] = v
                            instr = fn(eng)
                            for (s, amt) in incs:
                                instr.then_inc(sems[s], amt)
                    return stream
                block.sync(make("sync"))
                block.gpsimd(make("gpsimd"))
                block.scalar(make("scalar"))
                block.vector(make("vector"))
                block.tensor(make("tensor"))
    return nc


def _host_layouts(inputs):
    x = np.asarray(inputs["L0_attn_x"], dtype=np.float32)
    Wk = np.asarray(inputs["L0_attn_Wk"], dtype=np.float32)
    Wv = np.asarray(inputs["L0_attn_Wv"], dtype=np.float32)
    Wg = np.asarray(inputs["L0_attn_Wg"], dtype=np.float32)
    Wp = np.asarray(inputs["L0_attn_Wp"], dtype=np.float32)
    gw = np.asarray(inputs["L0_ffn_gate_w"], dtype=np.float32)
    uw = np.asarray(inputs["L0_ffn_up_w"], dtype=np.float32)
    dw = np.asarray(inputs["L0_ffn_down_w"], dtype=np.float32)
    gs = np.asarray(inputs["L0_ffn_gate_scale"], dtype=np.float32)
    us = np.asarray(inputs["L0_ffn_up_scale"], dtype=np.float32)
    ds = np.asarray(inputs["L0_ffn_down_scale"], dtype=np.float32)
    lq = np.asarray(inputs["L0_attn_latent_queries"], dtype=np.float32)

    perm_d = np.concatenate([np.arange(0, 128, 2), np.arange(1, 128, 2)])

    Wk_p = Wk.reshape(C, HC, D)[:, :, perm_d].reshape(C, C)
    wk_r = np.ascontiguousarray(
        Wk_p.reshape(KC, 128, HC, D).transpose(2, 1, 0, 3).reshape(C, C))
    wv_r = np.ascontiguousarray(
        Wv.reshape(KC, 128, HC, D).transpose(2, 1, 0, 3).reshape(C, C))
    wp_r = np.ascontiguousarray(
        Wp.reshape(KC, 128, KC, 128).transpose(2, 1, 0, 3).reshape(C, C))
    wgate_r = gw.reshape(KC, 128, IC, 128).transpose(2, 1, 0, 3).reshape(IC * 128, C)
    wup_r = uw.reshape(KC, 128, IC, 128).transpose(2, 1, 0, 3).reshape(IC * 128, C)
    dwr = dw.reshape(NG, GSZ, 128, KC, 128).transpose(0, 3, 2, 1, 4)
    wdown_r = dwr.reshape(NG * KC * 128, GSZ * 128)
    srcs = {"gate": wgate_r, "up": wup_r, "down": wdown_r}
    qsrc_by_rank = {}
    for c in range(8):
        blocks = [srcs[k][i * 128:(i + 1) * 128, :]
                  for (k, i, r, c0, cs) in QTAB if r == c]
        qsrc_by_rank[c] = np.ascontiguousarray(
            np.concatenate(blocks, axis=1))
    lq_r = np.ascontiguousarray(
        lq[0].transpose(1, 2, 0)[:, perm_d, :].transpose(1, 0, 2).reshape(128, HC * L))

    import jax
    import jax.numpy as jnp
    cpu = jax.devices("cpu")[0]
    with jax.default_device(cpu):
        invf = 1.0 / 10000.0 ** (jnp.arange(0, 64, dtype=jnp.float32) * 2.0 / 128.0)
        ang = jnp.arange(128, dtype=jnp.float32)[:, None] * invf[None, :]
        cosv = np.asarray(jnp.cos(ang), dtype=np.float32)
        sinv = np.asarray(jnp.sin(ang), dtype=np.float32)
    rCh = np.empty((128, 128), np.float32)
    rSh = np.empty((128, 128), np.float32)
    rCh[0:64, :] = cosv.T
    rCh[64:128, :] = cosv.T
    rSh[0:64, :] = -sinv.T
    rSh[64:128, :] = sinv.T

    maska = np.where(np.arange(128)[None, :] <= np.arange(64)[:, None],
                     np.float32(0.0), np.float32(NEG)).astype(np.float32)
    idf = np.eye(128, dtype=np.float32)
    gs_r = np.ascontiguousarray(gs.reshape(IC, 128).T)
    us_r = np.ascontiguousarray(us.reshape(IC, 128).T)
    ds_r = np.ascontiguousarray(ds.reshape(KC, 128).T)

    shared = dict(wk=wk_r, wv=wv_r, wg=np.ascontiguousarray(Wg), wp=wp_r,
                  lq=lq_r, ropec=rCh, ropes=rSh, maska=maska, idf=idf,
                  gs=gs_r, us=us_r, ds=ds_r)
    in_maps = []
    for c in range(8):
        b, q = c // 4, c % 4
        m = dict(shared)
        m["xT"] = np.ascontiguousarray(x[b, q * NT:(q + 1) * NT, :].T)
        m["qsrc"] = qsrc_by_rank[c]
        m["xhT"] = np.ascontiguousarray(x[b, 0:128, :].T)
        in_maps.append(m)
    return in_maps


def kernel(**inputs):
    in_maps = _host_layouts(inputs)
    nc = build()
    res = run_bass_kernel_spmd(nc, in_maps, core_ids=list(range(8)))
    x = np.asarray(inputs["L0_attn_x"])
    outp = np.empty((2, 4096, C), dtype=np.float32)
    for c in range(8):
        b, q = c // 4, c % 4
        outp[b, q * NT:(q + 1) * NT, :] = np.asarray(res.results[c]["out"]).T
    return outp


# version 11
# speedup vs baseline: 4.8313x; 1.1995x over previous
import numpy as np
import ml_dtypes
import concourse.bass as bass
import concourse.mybir as mybir
from concourse.bass_utils import run_bass_kernel_spmd

F32 = mybir.dt.float32
BF16 = mybir.dt.bfloat16
FP8 = mybir.dt.float8e4
Alu = mybir.AluOpType
Act = mybir.ActivationFunctionType
Ax = mybir.AxisListType

C = 2048
NT = 1024
NH = 512
KC = 16
HC = 16
D = 128
L = 64
IC = 48
GSZ = 12
NG = IC // GSZ
SCALE = float(1.0 / np.sqrt(128.0))

QSH = 4718592
QCOLS = QSH // 128


def _build_qtab():
    tab = []
    off = 0
    for ic in range(IC):
        tab.append(["gate", ic, off, 2048])
        off += 128 * 2048
    for ic in range(IC):
        tab.append(["up", ic, off, 2048])
        off += 128 * 2048
    for gi in range(NG * 16):
        tab.append(["down", gi, off, 1536])
        off += 128 * 1536
    assert off == 8 * QSH
    out = []
    for kind, idx, off, cols in tab:
        rank = off // QSH
        assert (off + 128 * cols - 1) // QSH == rank
        col0 = (off - rank * QSH) // 128
        out.append((kind, idx, rank, col0, cols))
    return out


QTAB = _build_qtab()
QBYKIND = {(k, i): (r, c0, cs) for (k, i, r, c0, cs) in QTAB}
NEG = -1.0e9
EPS = 1e-5


class Plan:
    def __init__(self):
        self.ops = []
        self.cnt = {}

    def bump(self, sem, amt):
        self.cnt[sem] = self.cnt.get(sem, 0) + amt
        return self.cnt[sem]

    def at(self, sem):
        return self.cnt.get(sem, 0)

    def add(self, eng, fn, waits=(), inc=None):
        incs = []
        tgt = None
        if inc is not None:
            sem, amt = inc
            tgt = self.bump(sem, amt)
            incs = [(sem, amt)]
        self.ops.append((eng, tuple(waits), fn, tuple(incs)))
        return tgt


ESEM = {"vector": "dv", "scalar": "ac", "tensor": "mm"}
NLANES = 4
QLANES = {"sync": ["sq%d" % i for i in range(NLANES)],
          "gpsimd": ["gq%d" % i for i in range(NLANES)]}


def build():
    nc = bass.Bass("TRN2", target_bir_lowering=False)
    p = Plan()

    def dram_in(name, shape, dt=F32):
        return nc.dram_tensor(name, shape, dt, kind="ExternalInput").ap()

    xT = dram_in("xT", [C, NT])
    xhT = dram_in("xhT", [C, 128])
    wk = dram_in("wk", [C, C])
    wv = dram_in("wv", [C, C])
    wg = dram_in("wg", [C, HC * L])
    wp = dram_in("wp", [C, C])
    qsrc = dram_in("qsrc", [128, QCOLS])
    qshard = nc.dram_tensor("qshard", [128, QCOLS], FP8, kind="Internal").ap()
    qfull = nc.dram_tensor("qfull", [8, 128, QCOLS], FP8, kind="Internal").ap()
    lqd = dram_in("lq", [128, HC * L])
    ropec = dram_in("ropec", [128, 128])
    ropes = dram_in("ropes", [128, 128])
    maska = nc.dram_tensor("maska", [64, 128], BF16, kind="ExternalInput").ap()
    idf = dram_in("idf", [128, 128])
    gsd = dram_in("gs", [128, IC])
    usd = dram_in("us", [128, IC])
    dsd = dram_in("ds", [128, KC])
    out = nc.dram_tensor("out", [C, NT], F32, kind="ExternalOutput").ap()

    import contextlib
    st = contextlib.ExitStack()
    sb = lambda name, shape, dt: st.enter_context(nc.sbuf_tensor(name, shape, dt))
    psb = lambda name, shape, dt: st.enter_context(nc.psum_tensor(name, shape, dt))

    with st:
        x_r = sb("x_r", [128, KC * NT], F32)
        xs_r = sb("xs_r", [128, KC * NT], BF16)
        xh_r = sb("xh_r", [128, KC * 128], F32)
        xsh_r = sb("xsh_r", [128, KC * 128], BF16)
        big = sb("big", [128, KC * NT], BF16)
        gFM = sb("gFM", [128, 8 * NT], BF16)
        wfa = sb("wfa", [128, 2048], F32)
        wfb = sb("wfb", [128, 2048], F32)
        wqa = sb("wqa", [128, 2048], BF16)
        wqb = sb("wqb", [128, 2048], BF16)
        f8x = sb("f8x", [128, 2048], FP8)
        f8wb = sb("f8wb", [128, 2048], FP8)
        f8wd = sb("f8wd", [128, 1536], FP8)
        f8we = sb("f8we", [128, 1536], FP8)
        scr1 = sb("scr1", [128, NT], F32)
        scr3 = sb("scr3", [128, 128], F32)
        scr4 = sb("scr4", [128, 128], F32)
        lqb = sb("lqb", [128, HC * L], BF16)
        rC = sb("rC", [128, 128], F32)
        rS = sb("rS", [128, 128], F32)
        mskt = sb("mskt", [64, 128], BF16)
        idft = sb("idft", [128, 128], F32)
        ksw = sb("ksw", [128, 128], F32)
        k_sb = sb("k_sb", [128, HC * 128], BF16)
        v_sb = sb("v_sb", [128, HC * 128], BF16)

        wT_sb = sb("wT_sb", [128, 2 * 64], BF16)
        attn_sb = sb("attn_sb", [128, 8 * 128], BF16)
        atmp = sb("atmp", [64, 128], BF16)
        vech = sb("vech", [4, 128], F32)
        ones_c = sb("ones_c", [128, 1], F32)
        eps_t = sb("eps_t", [128, 1], F32)
        ones_r = None
        gs_t = sb("gs_t", [128, IC], F32)
        us_t = sb("us_t", [128, IC], F32)
        ds_t = sb("ds_t", [128, KC], F32)
        mx8 = sb("mx8", [128, 10], F32)
        sm8 = sb("sm8", [128, 8], F32)
        sc_mx = mx8[0:64, 8:9]
        sc_s = mx8[0:64, 9:10]

        kt1 = scr3
        smt = scr3[0:64, :]
        ewt = scr4[0:64, :]
        ones_r = scr4[0:1, :]
        wfp = ksw[0:64, :]
        ps_big = psb("ps_big", [128, 2048], F32)
        ps_b1 = psb("ps_b1", [128, 1024], F32)
        ps_b2 = psb("ps_b2", [128, 512], F32)
        ps_b3 = psb("ps_b3", [128, 512], F32)

        lane_idx = {"sync": 0, "gpsimd": 0}

        def dma(q, out_ap, in_ap, waits=()):
            lane = QLANES[q][lane_idx[q] % NLANES]
            lane_idx[q] += 1
            w = list(waits)
            if p.at(lane) > 0:
                w.append((lane, p.at(lane)))
            fn = (lambda o, i: (lambda eng: eng.dma_start(out=o, in_=i)))(out_ap, in_ap)
            cnt = p.add(q, fn, waits=w, inc=(lane, 16))
            return (lane, cnt)

        def op(eng, fn, waits=()):
            return p.add(eng, fn, waits=waits, inc=(ESEM[eng], 1))

        d_lq = dma("gpsimd", wfa[:, 0:HC * L], lqd[:, :])
        dma("gpsimd", rC[:, :], ropec[:, :])
        dma("gpsimd", rS[:, :], ropes[:, :])
        dma("gpsimd", mskt[:, :], maska[:, :])
        dma("gpsimd", idft[:, :], idf[:, :])
        dma("gpsimd", gs_t[:, :], gsd[:, :])
        dma("gpsimd", us_t[:, :], usd[:, :])
        dma("gpsimd", ds_t[:, :], dsd[:, :])
        d_xh = [dma("sync", xh_r[:, c * 128:(c + 1) * 128], xhT[c * 128:(c + 1) * 128, :])
                for c in range(KC)]
        d_x = [dma("sync", x_r[:, c * NT:(c + 1) * NT], xT[c * 128:(c + 1) * 128, :])
               for c in range(KC)]

        op("vector", lambda eng: eng.memset(ones_c[:, :], 1.0))
        op("vector", lambda eng: eng.memset(ones_r, 1.0))
        op("vector", lambda eng: eng.memset(eps_t[:, :], EPS))
        c_ones = p.at("dv")
        op("vector", lambda eng: eng.tensor_copy(lqb[:, :], wfa[:, 0:HC * L]),
           waits=[d_lq])
        slab0_init = p.at("dv")

        sq_done = []
        for c in range(KC):
            w = [d_x[c]]
            if c >= 1:
                w.append(("mm", sq_done[c - 1]))
            scr = scr1
            op("scalar", (lambda cc, s: (lambda eng: eng.activation(
                s[:, :], x_r[:, cc * NT:(cc + 1) * NT], Act.Square)))(c, scr), waits=w)
            a_sq = p.at("ac")
            last = None
            for h in range(2):
                def mmfn(cc, s, hh):
                    def f(eng):
                        eng.matmul(ps_big[0:1, hh * NH:(hh + 1) * NH], ones_c[:, :],
                                   x_r[:, cc * NT + hh * NH: cc * NT + (hh + 1) * NH],
                                   start=(cc == 0), stop=(cc == KC - 1))
                        return eng.matmul(ps_big[0:1, 1024 + hh * NH:1024 + (hh + 1) * NH],
                                          ones_c[:, :], s[:, hh * NH:(hh + 1) * NH],
                                          start=(cc == 0), stop=(cc == KC - 1))
                    return f
                wmm = [("ac", a_sq)] + ([("dv", c_ones)] if c == 0 else [])
                last = op("tensor", mmfn(c, scr, h), waits=wmm)
            sq_done.append(last)
        mm_stats = p.at("mm")

        hs_done = []
        for c in range(KC):
            w = [d_xh[c]]
            if c >= 1:
                w.append(("mm", hs_done[c - 1]))
            op("scalar", (lambda cc: (lambda eng: eng.activation(
                scr3[:, :], xh_r[:, cc * 128:(cc + 1) * 128], Act.Square)))(c), waits=w)
            a_sq = p.at("ac")

            def mmfn(cc):
                def f(eng):
                    eng.matmul(ps_b1[0:1, 0:128], ones_c[:, :],
                               xh_r[:, cc * 128:(cc + 1) * 128],
                               start=(cc == 0), stop=(cc == KC - 1))
                    return eng.matmul(ps_b1[0:1, 512:640], ones_c[:, :], scr3[:, :],
                                      start=(cc == 0), stop=(cc == KC - 1))
                return f
            hs_done.append(op("tensor", mmfn(c), waits=[("ac", a_sq)]))
        mm_hstats = p.at("mm")

        inv_n = 1.0 / float(C)
        A = wfa[0:1, 0:1024]
        B = scr1[0:1, 0:1024]

        op("vector", lambda eng: eng.tensor_scalar(A, ps_big[0:1, 0:1024], inv_n,
                                                   None, Alu.mult),
           waits=[("mm", mm_hstats), ("dv", slab0_init)])
        c_m = p.at("dv")

        def bcast_m(eng):
            r = None
            for h in range(2):
                r = eng.matmul(ps_big[:, h * NH:(h + 1) * NH], ones_r,
                               wfa[0:1, h * NH:(h + 1) * NH], start=True, stop=True)
            return r
        mm_bm = op("tensor", bcast_m, waits=[("dv", c_m)])

        def fin_v(eng):
            eng.tensor_scalar(B, ps_big[0:1, 1024:2048], inv_n, None, Alu.mult)
            eng.tensor_tensor(A, A, A, Alu.mult)
            eng.drain()
            return eng.tensor_tensor(B, B, A, Alu.subtract)
        op("vector", fin_v, waits=[("mm", mm_bm)])
        c_var = p.at("dv")
        op("scalar", lambda eng: eng.activation(A, B, Act.Sqrt, bias=eps_t[0:1, :]),
           waits=[("dv", c_var)])
        a_std = p.at("ac")
        op("vector", lambda eng: eng.reciprocal(B, A), waits=[("ac", a_std)])
        c_r = p.at("dv")

        def bcast_r(eng):
            r = None
            for h in range(2):
                r = eng.matmul(ps_big[:, 1024 + h * NH:1024 + (h + 1) * NH], ones_r,
                               scr1[0:1, h * NH:(h + 1) * NH], start=True, stop=True)
            return r
        mm_br = op("tensor", bcast_r, waits=[("dv", c_r)])

        A2 = vech[0:1, :]
        B2 = scr1[0:1, 0:128]
        op("vector", lambda eng: eng.tensor_scalar(A2, ps_b1[0:1, 0:128], inv_n,
                                                   None, Alu.mult),
           waits=[("mm", mm_br)])
        c_hm = p.at("dv")
        mm_bhm = op("tensor", lambda eng: eng.matmul(ps_b2[:, 0:128], ones_r, A2,
                                                     start=True, stop=True),
                    waits=[("dv", c_hm)])

        def fin_hv(eng):
            eng.tensor_scalar(B2, ps_b1[0:1, 512:640], inv_n, None, Alu.mult)
            eng.tensor_tensor(A2, A2, A2, Alu.mult)
            eng.drain()
            return eng.tensor_tensor(B2, B2, A2, Alu.subtract)
        op("vector", fin_hv, waits=[("mm", mm_bhm)])
        c_hv = p.at("dv")
        op("scalar", lambda eng: eng.activation(A2, B2, Act.Sqrt, bias=eps_t[0:1, :]),
           waits=[("dv", c_hv)])
        a_hstd = p.at("ac")
        op("vector", lambda eng: eng.reciprocal(B2, A2), waits=[("ac", a_hstd)])
        c_hr = p.at("dv")
        mm_bcast = op("tensor", lambda eng: eng.matmul(ps_b2[:, 128:256], ones_r, B2,
                                                       start=True, stop=True),
                      waits=[("dv", c_hr)])

        for c in range(KC):
            def xnfn(cc):
                def f(eng):
                    eng.drain()
                    eng.tensor_tensor(scr1[:, :], x_r[:, cc * NT:(cc + 1) * NT],
                                      ps_big[:, 0:1024], Alu.subtract)
                    eng.drain()
                    return eng.tensor_tensor(xs_r[:, cc * NT:(cc + 1) * NT], scr1[:, :],
                                             ps_big[:, 1024:2048], Alu.mult)
                return f
            op("vector", xnfn(c), waits=[("mm", mm_bcast)] if c == 0 else ())
        for c in range(KC):
            def xnhfn(cc):
                def f(eng):
                    eng.drain()
                    eng.tensor_tensor(scr3[:, :], xh_r[:, cc * 128:(cc + 1) * 128],
                                      ps_b2[:, 0:128], Alu.subtract)
                    eng.drain()
                    return eng.tensor_tensor(xsh_r[:, cc * 128:(cc + 1) * 128], scr3[:, :],
                                             ps_b2[:, 128:256], Alu.mult)
                return f
            op("vector", xnhfn(c))
        c_xn = p.at("dv")

        slab_state = {"dv": [slab0_init, 0], "mm": [0, 0, 0, 0]}
        QSTG = [scr1[:, 0:512], scr1[:, 512:1024]]
        qstate = {"d2": [0, 0], "f8dma": [None] * 4}

        def quant_piece(pi, extra_waits=()):
            setid = pi % 2
            stg = QSTG[setid]
            w0 = list(extra_waits)
            if qstate["d2"][setid]:
                w0.append(("ac", qstate["d2"][setid]))
            dsrc = dma("sync", stg, qsrc[:, pi * 512:(pi + 1) * 512], waits=w0)
            fq = f8x[:, (pi % 4) * 512:(pi % 4 + 1) * 512]
            w2 = [dsrc]
            if qstate["f8dma"][pi % 4] is not None:
                w2.append(qstate["f8dma"][pi % 4])
            a2 = op("scalar", (lambda f_, s_: (lambda eng: eng.activation(
                f_, s_, Act.Copy)))(fq, stg), waits=w2)
            qstate["d2"][setid] = a2
            qstate["f8dma"][pi % 4] = dma(
                "gpsimd", qshard[:, pi * 512:(pi + 1) * 512], fq,
                waits=[("ac", a2)])
            return a2

        def stream_slab(src_ap, ncols, slot, extra_waits=()):
            wf = (wfa, wfb)[slot]
            wq = (wqa, wqb)[slot]
            w = list(extra_waits)
            if slab_state["dv"][slot]:
                w.append(("dv", slab_state["dv"][slot]))
            d = dma("sync", wf[:, 0:ncols], src_ap, waits=w)
            w2 = [d]
            if slab_state["mm"][slot]:
                w2.append(("mm", slab_state["mm"][slot]))

            def cast(eng):
                return eng.tensor_copy(wq[:, 0:ncols], wf[:, 0:ncols])
            cv = op("vector", cast, waits=w2)
            slab_state["dv"][slot] = cv
            return wq, cv

        f8state = {"cc": False}

        def stream_fp8(kind, idx, slot, extra_waits=()):
            rank, col0, cols = QBYKIND[(kind, idx)]
            buf = (f8x, f8wb, f8wd, f8we)[slot]
            w = list(extra_waits)
            if not f8state["cc"]:
                w.append(("cc", 1))
                f8state["cc"] = True
            if slab_state["mm"][slot]:
                w.append(("mm", slab_state["mm"][slot]))
            dmas = []
            c0 = 0
            while c0 < cols:
                cw = min(1024, cols - c0)
                dmas.append(dma("sync", buf[:, c0:c0 + cw],
                                qfull[rank, :, col0 + c0:col0 + c0 + cw],
                                waits=(w if c0 == 0 else ())))
                c0 += cw
            return buf, dmas

        def slab_used(slot, mm_count):
            slab_state["mm"][slot] = mm_count

        phq = {"next": 0}

        def phq_emit(n):
            for _ in range(n):
                if phq["next"] < 72:
                    ew = [("dv", c_xn)] if phq["next"] < 2 else ()
                    quant_piece(phq["next"], extra_waits=ew)
                    phq["next"] += 1

        wgp = {"next": 0, "cast": [0, 0, 0, 0], "last": 0}

        def wgp_emit(n):
            for _ in range(n):
                q = wgp["next"]
                if q >= 32:
                    return
                wgp["next"] += 1
                kc, H = q // 2, q % 2
                s4 = q % 4
                stg = xh_r[:, s4 * 512:(s4 + 1) * 512]
                w0 = [("dv", c_xn)] if q < 4 else [("dv", wgp["cast"][s4])]
                d = dma("sync", stg, wg[kc * 128:(kc + 1) * 128,
                                        H * 512:(H + 1) * 512], waits=w0)
                cv = op("vector", (lambda o_, s_: (lambda eng: eng.tensor_copy(
                    o_, s_)))(big[:, H * 8192 + kc * 512:H * 8192 + (kc + 1) * 512],
                              stg), waits=[d])
                wgp["cast"][s4] = cv
                wgp["last"] = cv

        for h in range(HC):
            wqv, cv = stream_slab(wk[h * 128:(h + 1) * 128, :], 2048, h % 2,
                                  extra_waits=[("mm", mm_bcast)] if h < 2 else ())

            def kmm(hh, wt):
                def f(eng):
                    r = None
                    for kc in range(KC):
                        r = eng.matmul(ps_big[:, hh * 128:(hh + 1) * 128],
                                       wt[:, kc * 128:(kc + 1) * 128],
                                       xsh_r[:, kc * 128:(kc + 1) * 128],
                                       start=(kc == 0), stop=(kc == KC - 1))
                    return r
                return f
            m = op("tensor", kmm(h, wqv), waits=[("dv", cv)])
            slab_used(h % 2, m)
            phq_emit(3)
            wgp_emit(2)
        mm_k = p.at("mm")

        def vb(h):
            if h < 8:
                return ps_b1[:, h * 128:(h + 1) * 128]
            if h < 12:
                return ps_b2[:, (h - 8) * 128:(h - 7) * 128]
            return ps_b3[:, (h - 12) * 128:(h - 11) * 128]

        for h in range(HC):
            wqv, cv = stream_slab(wv[h * 128:(h + 1) * 128, :], 2048, h % 2)

            def vmm(hh, wt):
                def f(eng):
                    r = None
                    for kc in range(KC):
                        r = eng.matmul(vb(hh), xsh_r[:, kc * 128:(kc + 1) * 128],
                                       wt[:, kc * 128:(kc + 1) * 128],
                                       start=(kc == 0), stop=(kc == KC - 1))
                    return r
                return f
            m = op("tensor", vmm(h, wqv), waits=[("dv", cv)])
            slab_used(h % 2, m)
            phq_emit(3)
        mm_v = p.at("mm")

        for h in range(HC):
            op("scalar", (lambda hh: (lambda eng: eng.activation(
                v_sb[:, hh * 128:(hh + 1) * 128], vb(hh), Act.Copy)))(h),
               waits=[("mm", mm_v)] if h == 0 else ())
        a_vevac = p.at("ac")

        rope_done = []
        for h in range(HC):
            w = [("mm", mm_k), ("ac", a_vevac)] if h == 0 else [("dv", rope_done[-1])]
            a_kf = op("scalar", (lambda hh: (lambda eng: eng.activation(
                scr4[:, :], ps_big[:, hh * 128:(hh + 1) * 128], Act.Copy)))(h), waits=w)
            dsw1 = dma("gpsimd", ksw[0:64, :], scr4[64:128, :], waits=[("ac", a_kf)])
            dsw2 = dma("gpsimd", ksw[64:128, :], scr4[0:64, :])

            def rope(hh):
                def f(eng):
                    eng.drain()
                    eng.tensor_tensor(kt1[:, :], scr4[:, :], rC[:, :], Alu.mult)
                    eng.tensor_tensor(ksw[:, :], ksw[:, :], rS[:, :], Alu.mult)
                    eng.drain()
                    return eng.tensor_tensor(k_sb[:, hh * 128:(hh + 1) * 128],
                                             kt1[:, :], ksw[:, :], Alu.add)
                return f
            rope_done.append(op("vector", rope(h), waits=[dsw1, dsw2]))
        c_ropes = p.at("dv")

        phq_emit(72)
        ag_waits = {}
        for t_ in qstate["f8dma"]:
            if t_ is not None:
                ln, v_ = t_
                ag_waits[ln] = max(ag_waits.get(ln, 0), v_)
        op_ag = p.add("gpsimd", lambda eng: eng.collective_compute(
            "AllGather", Alu.bypass, ins=[qshard[:, :]], outs=[qfull[:, :, :]],
            replica_groups=[[0, 1, 2, 3, 4, 5, 6, 7]]),
            waits=sorted(ag_waits.items()), inc=("cc", 1))

        attn_tail = []
        shift_dmas = []
        for h in range(HC):
            sps = (ps_b2, ps_b3)[h % 2]
            w = [("dv", c_ropes)] if h == 0 else []
            if h >= 2:
                w.append(("dv", attn_tail[h - 2]))
            mm_sc = op("tensor", (lambda hh, sp: (lambda eng: eng.matmul(
                sp[0:64, 0:128], lqb[:, hh * L:(hh + 1) * L],
                k_sb[:, hh * 128:(hh + 1) * 128], start=True, stop=True)))(h, sps),
                waits=w)

            def smax(hh, sp):
                def f(eng):
                    eng.drain()
                    eng.scalar_tensor_tensor(smt, sp[0:64, 0:128], SCALE, mskt[:, :],
                                             Alu.mult, Alu.add)
                    eng.drain()
                    eng.reduce_max(sc_mx[:, :], smt, Ax.X)
                    eng.drain()
                    return eng.tensor_scalar(sc_mx[:, :], sc_mx[:, :], -1.0, None, Alu.mult)
                return f
            c_sm = op("vector", smax(h, sps), waits=[("mm", mm_sc)])
            a_exp = op("scalar", lambda eng: eng.activation(ewt, smt, Act.Exp,
                                                            bias=sc_mx[:, :]),
                       waits=[("dv", c_sm)])

            def norm(eng):
                eng.reduce_sum(sc_s[:, :], ewt, Ax.X)
                eng.drain()
                return eng.reciprocal(sc_s[:, :], sc_s[:, :])
            c_nrm = op("vector", norm, waits=[("ac", a_exp)])
            a_wn = op("scalar", lambda eng: eng.activation(wfp, ewt, Act.Copy,
                                                           scale=sc_s[:, :]),
                      waits=[("dv", c_nrm)])
            tcol = (h % 2) * 512
            mm_t = op("tensor", (lambda tc: (lambda eng: eng.matmul(
                ps_big[0:128, tc:tc + 64], wfp, idft[0:64, 0:64], start=True, stop=True,
                is_transpose=True)))(tcol), waits=[("ac", a_wn)])
            c_wt = op("vector", (lambda hh, tc: (lambda eng: eng.tensor_copy(
                wT_sb[:, (hh % 2) * 64:(hh % 2 + 1) * 64], ps_big[0:128, tc:tc + 64])))(h, tcol),
                waits=[("mm", mm_t)])
            acol = 1024 + (h % 2) * 512
            mm_a = op("tensor", (lambda hh, cl: (lambda eng: eng.matmul(
                ps_big[0:64, cl:cl + 128], wT_sb[:, (hh % 2) * 64:(hh % 2 + 1) * 64],
                v_sb[:, hh * 128:(hh + 1) * 128], start=True, stop=True)))(h, acol),
                waits=[("dv", c_wt)])
            if h % 2 == 0:
                c_ae = op("vector", (lambda hh, cl: (lambda eng: eng.tensor_copy(
                    attn_sb[0:64, (hh // 2) * 128:(hh // 2 + 1) * 128],
                    ps_big[0:64, cl:cl + 128])))(h, acol), waits=[("mm", mm_a)])
            else:
                c_ae = op("vector", (lambda cl: (lambda eng: eng.tensor_copy(
                    atmp[:, :], ps_big[0:64, cl:cl + 128])))(acol),
                    waits=[("mm", mm_a)] + ([shift_dmas[-1]] if shift_dmas else []))
                shift_dmas.append(
                    dma("sync", attn_sb[64:128, (h // 2) * 128:(h // 2 + 1) * 128],
                        atmp[:, :], waits=[("dv", c_ae)]))
            attn_tail.append(c_ae)
        c_attn = p.at("dv")
        g_attn = list(shift_dmas)

        c_wgh = wgp["last"]
        gt_done = []
        for tile in range(16):
            H, t = tile // 8, tile % 8
            if True:
                gps = (ps_b2, ps_b3)[tile % 2]

                def gmm(tt, gp, HH):
                    def f(eng):
                        r = None
                        for kc in range(KC):
                            r = eng.matmul(gp[:, 0:512],
                                           xs_r[:, kc * NT + tt * 128:kc * NT + (tt + 1) * 128],
                                           big[:, HH * 8192 + kc * 512:HH * 8192 + (kc + 1) * 512],
                                           start=(kc == 0), stop=(kc == KC - 1))
                        return r
                    return f
                w = [("dv", c_wgh)]
                if tile >= 2:
                    w.append(("dv", gt_done[tile - 2]))
                else:
                    w.append(("dv", c_attn))
                mm_g = op("tensor", gmm(t, gps, H), waits=w)

                def gsm1(gp):
                    def f(eng):
                        eng.tensor_reduce(mx8[:, 0:8].rearrange("p (s o) -> p s o", o=1),
                                          gp[:, 0:512].rearrange("p (s t) -> p s t", s=8),
                                          Ax.X, Alu.max)
                        eng.drain()
                        return eng.tensor_scalar(mx8[:, 0:8], mx8[:, 0:8], -SCALE, None,
                                                 Alu.mult)
                    return f
                c_g1 = op("vector", gsm1(gps), waits=[("mm", mm_g)])

                def gexp(gp):
                    def f(eng):
                        r = None
                        for s in range(8):
                            r = eng.activation(scr1[:, s * 64:(s + 1) * 64],
                                               gp[:, s * 64:(s + 1) * 64], Act.Exp,
                                               bias=mx8[:, s:s + 1], scale=SCALE)
                        return r
                    return f
                a_ge = op("scalar", gexp(gps), waits=[("dv", c_g1)])

                def gsum(eng):
                    eng.tensor_reduce(sm8[:, 0:8].rearrange("p (s o) -> p s o", o=1),
                                      scr1[:, 0:512].rearrange("p (s t) -> p s t", s=8),
                                      Ax.X, Alu.add)
                    eng.drain()
                    return eng.reciprocal(sm8[:, 0:8], sm8[:, 0:8])
                c_gs = op("vector", gsum, waits=[("ac", a_ge)])

                def gnorm(eng):
                    r = None
                    for s in range(8):
                        r = eng.activation(scr1[:, 512 + s * 64:512 + (s + 1) * 64],
                                           scr1[:, s * 64:(s + 1) * 64], Act.Copy,
                                           scale=sm8[:, s:s + 1])
                    return r
                a_gn = op("scalar", gnorm, waits=[("dv", c_gs)])

                last = None
                for p2 in range(4):
                    pair = H * 4 + p2
                    tcol = (p2 % 2) * 512
                    mm_tr = op("tensor", (lambda p3, tc: (lambda eng: eng.matmul(
                        ps_b1[0:128, tc:tc + 128], scr1[:, 512 + p3 * 128:512 + (p3 + 1) * 128],
                        idft[:, :], start=True, stop=True, is_transpose=True)))(p2, tcol),
                        waits=[("ac", a_gn)] if p2 == 0 else [("dv", p.at("dv"))])
                    last = op("vector", (lambda pr, tc, tt: (lambda eng: eng.tensor_copy(
                        gFM[:, pr * NT + tt * 128:pr * NT + (tt + 1) * 128],
                        ps_b1[0:128, tc:tc + 128])))(pair, tcol, t),
                        waits=[("mm", mm_tr)])
                gt_done.append(last)
        c_gates = p.at("dv")

        mix_evac = []
        for h in range(HC):
            pair, row = h // 2, (h % 2) * 64
            for half in range(2):
                mps = (ps_b2, ps_b3)[half]
                w = []
                if h == 0 and half == 0:
                    w = [("dv", c_gates)] + g_attn
                if len(mix_evac) >= 2:
                    w.append(("ac", mix_evac[-2]))

                def mmix(hh, pr, rw, hf, mp):
                    def f(eng):
                        return eng.matmul(
                            mp[:, 0:512], attn_sb[rw:rw + 64, pr * 128:(pr + 1) * 128],
                            gFM[rw:rw + 64, pr * NT + hf * NH:pr * NT + (hf + 1) * NH],
                            start=True, stop=True)
                    return f
                mm_m = op("tensor", mmix(h, pair, row, half, mps), waits=w)
                a_me = op("scalar", (lambda hh, hf, mp: (lambda eng: eng.activation(
                    big[:, hh * NT + hf * NH:hh * NT + (hf + 1) * NH], mp[:, 0:512],
                    Act.Copy)))(h, half, mps), waits=[("mm", mm_m)])
                mix_evac.append(a_me)
        a_mix = p.at("ac")

        mm_p = 0
        for oc in range(KC):
            wqv, cv = stream_slab(wp[oc * 128:(oc + 1) * 128, :], 2048, oc % 2,
                                  extra_waits=[("ac", a_mix)] if oc < 2 else ())
            for half in range(2):
                pps = (ps_b2, ps_b3)[half]

                def pmm(wt, hf, pp):
                    def f(eng):
                        r = None
                        for kc in range(KC):
                            r = eng.matmul(pp[:, 0:512], wt[:, kc * 128:(kc + 1) * 128],
                                           big[:, kc * NT + hf * NH:kc * NT + (hf + 1) * NH],
                                           start=(kc == 0), stop=(kc == KC - 1))
                        return r
                    return f
                w = [("dv", cv)]
                if oc > 0 or half > 0:
                    w.append(("dv", p.at("dv")))
                mm_p = op("tensor", pmm(wqv, half, pps), waits=w)
                op("vector", (lambda o2, hf, pp: (lambda eng: eng.tensor_tensor(
                    x_r[:, o2 * NT + hf * NH:o2 * NT + (hf + 1) * NH],
                    x_r[:, o2 * NT + hf * NH:o2 * NT + (hf + 1) * NH],
                    pp[:, 0:512], Alu.add)))(oc, half, pps), waits=[("mm", mm_p)])
            slab_used(oc % 2, mm_p)
        c_wp = p.at("dv")

        slab_state["mm"][0] = slab_state["mm"][1] = 0
        psG = [ps_big[:, 0:NH], ps_big[:, NH:2 * NH]]
        psU = [ps_big[:, 1024:1024 + NH], ps_big[:, 1024 + NH:2048]]
        DB = [ps_b1[:, 0:NH], ps_b1[:, NH:2 * NH], ps_b2[:, 0:NH], ps_b3[:, 0:NH]]
        epi_prev = [0, 0]
        down_guard = [0, 0, 0, 0]
        add_prev = [0, 0]
        down_last = 0
        mm_d = 0
        for g in range(NG):
            mm_g_last = mm_u_last = 0
            for j in range(GSZ):
                ic = g * GSZ + j
                bufg, dmg = stream_fp8("gate", ic, 0)
                bufu, dmu = stream_fp8("up", ic, 1)

                def pmmh(wt, hf, pdst):
                    def f(eng):
                        r = None
                        for kc in range(KC):
                            r = eng.matmul(pdst,
                                           wt[:, kc * 128:(kc + 1) * 128],
                                           xs_r[:, kc * NT + hf * NH:kc * NT + (hf + 1) * NH],
                                           start=(kc == 0), stop=(kc == KC - 1))
                        return r
                    return f

                for half in range(2):
                    w = list(dmg) if half == 0 else []
                    if epi_prev[half]:
                        w.append(("ac", epi_prev[half]))
                    mm_g_last = op("tensor", pmmh(bufg, half, psG[half]), waits=w)
                    w2 = list(dmu) if half == 0 else []
                    mm_u_last = op("tensor", pmmh(bufu, half, psU[half]), waits=w2)

                    def epi(icc, hf):
                        def f(eng):
                            eng.activation(scr1[:, 0:NH], psG[hf],
                                           Act.Silu, scale=gs_t[:, icc:icc + 1])
                            return eng.activation(scr1[:, NH:NT], psU[hf],
                                                  Act.Copy, scale=us_t[:, icc:icc + 1])
                        return f
                    a_ep = op("scalar", epi(ic, half),
                              waits=[("mm", mm_u_last), ("dv", p.at("dv"))])
                    epi_prev[half] = a_ep
                    wf = [("ac", a_ep)]
                    if j == 0 and half == 0:
                        wf.append(("mm", down_last if g > 0 else mm_p))
                    op("vector", (lambda jj, hf: (lambda eng: eng.tensor_tensor(
                        big[:, jj * NT + hf * NH:jj * NT + (hf + 1) * NH],
                        scr1[:, 0:NH], scr1[:, NH:NT], Alu.mult)))(j, half),
                        waits=wf)
                slab_used(0, mm_g_last)
                slab_used(1, mm_u_last)
            c_fuse = p.at("dv")

            for oc in range(KC):
                bufd, dmd = stream_fp8("down", g * 16 + oc, 2 + oc % 2)
                for half in range(2):
                    blk = (2 * oc + half) % 4

                    def dmmf(wt, hf, pdst):
                        def f(eng):
                            r = None
                            for j in range(GSZ):
                                r = eng.matmul(pdst,
                                               wt[:, j * 128:(j + 1) * 128],
                                               big[:, j * NT + hf * NH:j * NT + (hf + 1) * NH],
                                               start=(j == 0), stop=(j == GSZ - 1))
                            return r
                        return f
                    w = list(dmd) if half == 0 else []
                    if oc == 0 and half == 0:
                        w.append(("dv", c_fuse))
                    if down_guard[blk]:
                        w.append(("ac", down_guard[blk]))
                    mm_d = op("tensor", dmmf(bufd, half, DB[blk]), waits=w)
                    w_de = [("mm", mm_d)]
                    if add_prev[half]:
                        w_de.append(("dv", add_prev[half]))
                    a_de = op("scalar", (lambda o2, hf, bk: (lambda eng: eng.activation(
                        scr1[:, hf * NH:(hf + 1) * NH], DB[bk],
                        Act.Copy, scale=ds_t[:, o2:o2 + 1])))(oc, half, blk),
                        waits=w_de)
                    down_guard[blk] = a_de
                    add_prev[half] = op(
                        "vector", (lambda o2, hf: (lambda eng: eng.tensor_tensor(
                            x_r[:, o2 * NT + hf * NH:o2 * NT + (hf + 1) * NH],
                            x_r[:, o2 * NT + hf * NH:o2 * NT + (hf + 1) * NH],
                            scr1[:, hf * NH:(hf + 1) * NH], Alu.add)))(oc, half),
                        waits=[("ac", a_de)])
                slab_used(2 + oc % 2, mm_d)
            down_last = mm_d
        c_ffn = p.at("dv")

        st_dmas = []
        for c in range(KC):
            st_dmas.append(dma("gpsimd", out[c * 128:(c + 1) * 128, :],
                               x_r[:, c * NT:(c + 1) * NT],
                               waits=[("dv", c_ffn)] if c == 0 else ()))
        final_waits = {}
        for (ln, v) in st_dmas:
            final_waits[ln] = max(final_waits.get(ln, 0), v)
        p.add("gpsimd", lambda eng: None, waits=sorted(final_waits.items()))

        sems = {}
        with contextlib.ExitStack() as sst:
            for s in (["dv", "ac", "mm", "cc"] + QLANES["sync"] + QLANES["gpsimd"]):
                sems[s] = sst.enter_context(nc.semaphore(s))
            with nc.Block() as block:
                def make(eng_name):
                    def stream(eng):
                        lastw = {}
                        for (e, waits, fn, incs) in p.ops:
                            if e != eng_name:
                                continue
                            for (s, v) in waits:
                                if v > 0 and lastw.get(s, -1) < v:
                                    eng.wait_ge(sems[s], v)
                                    lastw[s] = v
                            instr = fn(eng)
                            for (s, amt) in incs:
                                instr.then_inc(sems[s], amt)
                    return stream
                block.sync(make("sync"))
                block.gpsimd(make("gpsimd"))
                block.scalar(make("scalar"))
                block.vector(make("vector"))
                block.tensor(make("tensor"))
    return nc


def _host_layouts(inputs):
    x = np.asarray(inputs["L0_attn_x"], dtype=np.float32)
    Wk = np.asarray(inputs["L0_attn_Wk"], dtype=np.float32)
    Wv = np.asarray(inputs["L0_attn_Wv"], dtype=np.float32)
    Wg = np.asarray(inputs["L0_attn_Wg"], dtype=np.float32)
    Wp = np.asarray(inputs["L0_attn_Wp"], dtype=np.float32)
    gw = np.asarray(inputs["L0_ffn_gate_w"], dtype=np.float32)
    uw = np.asarray(inputs["L0_ffn_up_w"], dtype=np.float32)
    dw = np.asarray(inputs["L0_ffn_down_w"], dtype=np.float32)
    gs = np.asarray(inputs["L0_ffn_gate_scale"], dtype=np.float32)
    us = np.asarray(inputs["L0_ffn_up_scale"], dtype=np.float32)
    ds = np.asarray(inputs["L0_ffn_down_scale"], dtype=np.float32)
    lq = np.asarray(inputs["L0_attn_latent_queries"], dtype=np.float32)

    perm_d = np.concatenate([np.arange(0, 128, 2), np.arange(1, 128, 2)])

    Wk_p = Wk.reshape(C, HC, D)[:, :, perm_d].reshape(C, C)
    wk_r = np.ascontiguousarray(
        Wk_p.reshape(KC, 128, HC, D).transpose(2, 1, 0, 3).reshape(C, C))
    wv_r = np.ascontiguousarray(
        Wv.reshape(KC, 128, HC, D).transpose(2, 1, 0, 3).reshape(C, C))
    wp_r = np.ascontiguousarray(
        Wp.reshape(KC, 128, KC, 128).transpose(2, 1, 0, 3).reshape(C, C))
    wgate_r = gw.reshape(KC, 128, IC, 128).transpose(2, 1, 0, 3).reshape(IC * 128, C)
    wup_r = uw.reshape(KC, 128, IC, 128).transpose(2, 1, 0, 3).reshape(IC * 128, C)
    dwr = dw.reshape(NG, GSZ, 128, KC, 128).transpose(0, 3, 2, 1, 4)
    wdown_r = dwr.reshape(NG * KC * 128, GSZ * 128)
    srcs = {"gate": wgate_r, "up": wup_r, "down": wdown_r}
    qsrc_by_rank = {}
    for c in range(8):
        blocks = [srcs[k][i * 128:(i + 1) * 128, :]
                  for (k, i, r, c0, cs) in QTAB if r == c]
        qsrc_by_rank[c] = np.ascontiguousarray(
            np.concatenate(blocks, axis=1))
    lq_r = np.ascontiguousarray(
        lq[0].transpose(1, 2, 0)[:, perm_d, :].transpose(1, 0, 2).reshape(128, HC * L))

    import jax
    import jax.numpy as jnp
    cpu = jax.devices("cpu")[0]
    with jax.default_device(cpu):
        invf = 1.0 / 10000.0 ** (jnp.arange(0, 64, dtype=jnp.float32) * 2.0 / 128.0)
        ang = jnp.arange(128, dtype=jnp.float32)[:, None] * invf[None, :]
        cosv = np.asarray(jnp.cos(ang), dtype=np.float32)
        sinv = np.asarray(jnp.sin(ang), dtype=np.float32)
    rCh = np.empty((128, 128), np.float32)
    rSh = np.empty((128, 128), np.float32)
    rCh[0:64, :] = cosv.T
    rCh[64:128, :] = cosv.T
    rSh[0:64, :] = -sinv.T
    rSh[64:128, :] = sinv.T

    maska = np.where(np.arange(128)[None, :] <= np.arange(64)[:, None],
                     np.float32(0.0), np.float32(NEG)).astype(ml_dtypes.bfloat16)
    idf = np.eye(128, dtype=np.float32)
    gs_r = np.ascontiguousarray(gs.reshape(IC, 128).T)
    us_r = np.ascontiguousarray(us.reshape(IC, 128).T)
    ds_r = np.ascontiguousarray(ds.reshape(KC, 128).T)

    shared = dict(wk=wk_r, wv=wv_r, wg=np.ascontiguousarray(Wg), wp=wp_r,
                  lq=lq_r, ropec=rCh, ropes=rSh, maska=maska, idf=idf,
                  gs=gs_r, us=us_r, ds=ds_r)
    in_maps = []
    for c in range(8):
        b, q = c // 4, c % 4
        m = dict(shared)
        m["xT"] = np.ascontiguousarray(x[b, q * NT:(q + 1) * NT, :].T)
        m["qsrc"] = qsrc_by_rank[c]
        m["xhT"] = np.ascontiguousarray(x[b, 0:128, :].T)
        in_maps.append(m)
    return in_maps


def kernel(**inputs):
    in_maps = _host_layouts(inputs)
    nc = build()
    res = run_bass_kernel_spmd(nc, in_maps, core_ids=list(range(8)))
    x = np.asarray(inputs["L0_attn_x"])
    outp = np.empty((2, 4096, C), dtype=np.float32)
    for c in range(8):
        b, q = c // 4, c % 4
        outp[b, q * NT:(q + 1) * NT, :] = np.asarray(res.results[c]["out"]).T
    return outp


# version 12
# speedup vs baseline: 5.6693x; 1.1735x over previous
import numpy as np
import ml_dtypes
import concourse.bass as bass
import concourse.mybir as mybir
from concourse.bass_utils import run_bass_kernel_spmd

F32 = mybir.dt.float32
BF16 = mybir.dt.bfloat16
FP8 = mybir.dt.float8e4
Alu = mybir.AluOpType
Act = mybir.ActivationFunctionType
Ax = mybir.AxisListType

C = 2048
NT = 1024
NH = 512
KC = 16
HC = 16
D = 128
L = 64
IC = 48
GSZ = 12
NG = IC // GSZ
SCALE = float(1.0 / np.sqrt(128.0))

QSH = 4718592
QCOLS = QSH // 128


def _build_qtab():
    tab = []
    off = 0
    for ic in range(IC):
        tab.append(["gate", ic, off, 2048])
        off += 128 * 2048
    for ic in range(IC):
        tab.append(["up", ic, off, 2048])
        off += 128 * 2048
    for gi in range(NG * 16):
        tab.append(["down", gi, off, 1536])
        off += 128 * 1536
    assert off == 8 * QSH
    out = []
    for kind, idx, off, cols in tab:
        rank = off // QSH
        assert (off + 128 * cols - 1) // QSH == rank
        col0 = (off - rank * QSH) // 128
        out.append((kind, idx, rank, col0, cols))
    return out


QTAB = _build_qtab()
QBYKIND = {(k, i): (r, c0, cs) for (k, i, r, c0, cs) in QTAB}
NEG = -1.0e9
EPS = 1e-5


class Plan:
    def __init__(self):
        self.ops = []
        self.cnt = {}

    def bump(self, sem, amt):
        self.cnt[sem] = self.cnt.get(sem, 0) + amt
        return self.cnt[sem]

    def at(self, sem):
        return self.cnt.get(sem, 0)

    def add(self, eng, fn, waits=(), inc=None):
        incs = []
        tgt = None
        if inc is not None:
            sem, amt = inc
            tgt = self.bump(sem, amt)
            incs = [(sem, amt)]
        self.ops.append((eng, tuple(waits), fn, tuple(incs)))
        return tgt


ESEM = {"vector": "dv", "scalar": "ac", "tensor": "mm"}
NLANES = 4
QLANES = {"sync": ["sq%d" % i for i in range(NLANES)],
          "gpsimd": ["gq%d" % i for i in range(NLANES)]}


def build():
    nc = bass.Bass("TRN2", target_bir_lowering=False)
    p = Plan()

    def dram_in(name, shape, dt=F32):
        return nc.dram_tensor(name, shape, dt, kind="ExternalInput").ap()

    xT = dram_in("xT", [C, NT])
    xhT = dram_in("xhT", [C, 128])
    wkq = dram_in("wkq", [512, C])
    wvq = dram_in("wvq", [512, C])
    wg = dram_in("wg", [C, HC * L])
    wp = dram_in("wp", [C, C])
    qsrc = dram_in("qsrc", [128, QCOLS])
    qshard = nc.dram_tensor("qshard", [128, QCOLS], FP8, kind="Internal").ap()
    qfull = nc.dram_tensor("qfull", [8, 128, QCOLS], FP8, kind="Internal").ap()
    lqd = dram_in("lqq", [128, 4 * L])
    attn_loc = nc.dram_tensor("attn_loc", [64, 512], BF16, kind="Internal").ap()
    attn_glob = nc.dram_tensor("attn_glob", [4, 64, 512], BF16, kind="Internal").ap()
    ropec = dram_in("ropec", [128, 128])
    ropes = dram_in("ropes", [128, 128])
    maska = nc.dram_tensor("maska", [64, 128], BF16, kind="ExternalInput").ap()
    idf = dram_in("idf", [128, 128])
    gsd = dram_in("gs", [128, IC])
    usd = dram_in("us", [128, IC])
    dsd = dram_in("ds", [128, KC])
    out = nc.dram_tensor("out", [C, NT], F32, kind="ExternalOutput").ap()

    import contextlib
    st = contextlib.ExitStack()
    sb = lambda name, shape, dt: st.enter_context(nc.sbuf_tensor(name, shape, dt))
    psb = lambda name, shape, dt: st.enter_context(nc.psum_tensor(name, shape, dt))

    with st:
        x_r = sb("x_r", [128, KC * NT], F32)
        xs_r = sb("xs_r", [128, KC * NT], BF16)
        xh_r = sb("xh_r", [128, KC * 128], F32)
        xsh_r = sb("xsh_r", [128, KC * 128], BF16)
        big = sb("big", [128, KC * NT], BF16)
        gFM = sb("gFM", [128, 8 * NT], BF16)
        wfa = sb("wfa", [128, 2048], F32)
        wfb = sb("wfb", [128, 2048], F32)
        wqa = sb("wqa", [128, 2048], BF16)
        wqb = sb("wqb", [128, 2048], BF16)
        f8x = sb("f8x", [128, 2048], FP8)
        f8wb = sb("f8wb", [128, 2048], FP8)
        f8wd = sb("f8wd", [128, 1536], FP8)
        f8we = sb("f8we", [128, 1536], FP8)
        scr1 = sb("scr1", [128, NT], F32)
        scr3 = sb("scr3", [128, 128], F32)
        scr4 = sb("scr4", [128, 128], F32)
        lqb = sb("lqb", [128, 4 * L], BF16)
        rC = sb("rC", [128, 128], F32)
        rS = sb("rS", [128, 128], F32)
        mskt = sb("mskt", [64, 128], BF16)
        idft = sb("idft", [128, 128], F32)
        ksw = sb("ksw", [128, 128], F32)
        k_sb = sb("k_sb", [128, 512], BF16)
        v_sb = sb("v_sb", [128, 512], BF16)

        wT_sb = sb("wT_sb", [128, 2 * 64], BF16)
        attn_sb = sb("attn_sb", [128, 8 * 128], BF16)
        vech = sb("vech", [4, 128], F32)
        ones_c = sb("ones_c", [128, 1], F32)
        eps_t = sb("eps_t", [128, 1], F32)
        ones_r = None
        gs_t = sb("gs_t", [128, IC], F32)
        us_t = sb("us_t", [128, IC], F32)
        ds_t = sb("ds_t", [128, KC], F32)
        mx8 = sb("mx8", [128, 10], F32)
        sm8 = sb("sm8", [128, 8], F32)
        sc_mx = mx8[0:64, 8:9]
        sc_s = mx8[0:64, 9:10]

        kt1 = scr3
        smt = scr3[0:64, :]
        ewt = scr4[0:64, :]
        ones_r = scr4[0:1, :]
        wfp = ksw[0:64, :]
        ps_big = psb("ps_big", [128, 2048], F32)
        ps_b1 = psb("ps_b1", [128, 1024], F32)
        ps_b2 = psb("ps_b2", [128, 512], F32)
        ps_b3 = psb("ps_b3", [128, 512], F32)

        lane_idx = {"sync": 0, "gpsimd": 0}

        def dma(q, out_ap, in_ap, waits=()):
            lane = QLANES[q][lane_idx[q] % NLANES]
            lane_idx[q] += 1
            w = list(waits)
            if p.at(lane) > 0:
                w.append((lane, p.at(lane)))
            fn = (lambda o, i: (lambda eng: eng.dma_start(out=o, in_=i)))(out_ap, in_ap)
            cnt = p.add(q, fn, waits=w, inc=(lane, 16))
            return (lane, cnt)

        def op(eng, fn, waits=()):
            return p.add(eng, fn, waits=waits, inc=(ESEM[eng], 1))

        d_lq = dma("gpsimd", wfa[:, 0:4 * L], lqd[:, :])
        dma("gpsimd", rC[:, :], ropec[:, :])
        dma("gpsimd", rS[:, :], ropes[:, :])
        dma("gpsimd", mskt[:, :], maska[:, :])
        dma("gpsimd", idft[:, :], idf[:, :])
        dma("gpsimd", gs_t[:, :], gsd[:, :])
        dma("gpsimd", us_t[:, :], usd[:, :])
        dma("gpsimd", ds_t[:, :], dsd[:, :])
        d_xh = [dma("sync", xh_r[:, c * 128:(c + 1) * 128], xhT[c * 128:(c + 1) * 128, :])
                for c in range(KC)]
        d_x = [dma("sync", x_r[:, c * NT:(c + 1) * NT], xT[c * 128:(c + 1) * 128, :])
               for c in range(KC)]

        op("vector", lambda eng: eng.memset(ones_c[:, :], 1.0))
        op("vector", lambda eng: eng.memset(ones_r, 1.0))
        op("vector", lambda eng: eng.memset(eps_t[:, :], EPS))
        c_ones = p.at("dv")
        op("vector", lambda eng: eng.tensor_copy(lqb[:, :], wfa[:, 0:4 * L]),
           waits=[d_lq])
        slab0_init = p.at("dv")

        SQB = [wfa[:, 0:NT], wfa[:, NT:2 * NT], wfb[:, 0:NT], wfb[:, NT:2 * NT]]
        sq_mm = [0, 0, 0, 0]
        for c in range(KC):
            b4 = c % 4
            w = [d_x[c]]
            if c == 0:
                w.append(("dv", slab0_init))
            if sq_mm[b4]:
                w.append(("mm", sq_mm[b4]))
            op("scalar", (lambda cc, s: (lambda eng: eng.activation(
                s, x_r[:, cc * NT:(cc + 1) * NT], Act.Square)))(c, SQB[b4]), waits=w)
            a_sq = p.at("ac")
            last = None
            for h in range(2):
                def mmfn(cc, s, hh):
                    def f(eng):
                        eng.matmul(ps_big[0:1, hh * NH:(hh + 1) * NH], ones_c[:, :],
                                   x_r[:, cc * NT + hh * NH: cc * NT + (hh + 1) * NH],
                                   start=(cc == 0), stop=(cc == KC - 1))
                        return eng.matmul(ps_big[0:1, 1024 + hh * NH:1024 + (hh + 1) * NH],
                                          ones_c[:, :], s[:, hh * NH:(hh + 1) * NH],
                                          start=(cc == 0), stop=(cc == KC - 1))
                    return f
                wmm = [("ac", a_sq)] + ([("dv", c_ones)] if c == 0 else [])
                last = op("tensor", mmfn(c, SQB[b4], h), waits=wmm)
            sq_mm[b4] = last
        mm_stats = p.at("mm")

        hs_done = []
        for c in range(KC):
            w = [d_xh[c]]
            if c >= 1:
                w.append(("mm", hs_done[c - 1]))
            op("scalar", (lambda cc: (lambda eng: eng.activation(
                scr3[:, :], xh_r[:, cc * 128:(cc + 1) * 128], Act.Square)))(c), waits=w)
            a_sq = p.at("ac")

            def mmfn(cc):
                def f(eng):
                    eng.matmul(ps_b1[0:1, 0:128], ones_c[:, :],
                               xh_r[:, cc * 128:(cc + 1) * 128],
                               start=(cc == 0), stop=(cc == KC - 1))
                    return eng.matmul(ps_b1[0:1, 512:640], ones_c[:, :], scr3[:, :],
                                      start=(cc == 0), stop=(cc == KC - 1))
                return f
            hs_done.append(op("tensor", mmfn(c), waits=[("ac", a_sq)]))
        mm_hstats = p.at("mm")

        inv_n = 1.0 / float(C)
        A = wfa[0:1, 0:1024]
        B = scr1[0:1, 0:1024]

        op("vector", lambda eng: eng.tensor_scalar(A, ps_big[0:1, 0:1024], inv_n,
                                                   None, Alu.mult),
           waits=[("mm", mm_hstats), ("dv", slab0_init)])
        c_m = p.at("dv")

        def bcast_m(eng):
            r = None
            for h in range(2):
                r = eng.matmul(ps_big[:, h * NH:(h + 1) * NH], ones_r,
                               wfa[0:1, h * NH:(h + 1) * NH], start=True, stop=True)
            return r
        mm_bm = op("tensor", bcast_m, waits=[("dv", c_m)])

        def fin_v(eng):
            eng.tensor_scalar(B, ps_big[0:1, 1024:2048], inv_n, None, Alu.mult)
            eng.tensor_tensor(A, A, A, Alu.mult)
            eng.drain()
            return eng.tensor_tensor(B, B, A, Alu.subtract)
        op("vector", fin_v, waits=[("mm", mm_bm)])
        c_var = p.at("dv")
        op("scalar", lambda eng: eng.activation(A, B, Act.Sqrt, bias=eps_t[0:1, :]),
           waits=[("dv", c_var)])
        a_std = p.at("ac")
        op("vector", lambda eng: eng.reciprocal(B, A), waits=[("ac", a_std)])
        c_r = p.at("dv")

        def bcast_r(eng):
            r = None
            for h in range(2):
                r = eng.matmul(ps_big[:, 1024 + h * NH:1024 + (h + 1) * NH], ones_r,
                               scr1[0:1, h * NH:(h + 1) * NH], start=True, stop=True)
            return r
        mm_br = op("tensor", bcast_r, waits=[("dv", c_r)])

        A2 = vech[0:1, :]
        B2 = scr1[0:1, 0:128]
        op("vector", lambda eng: eng.tensor_scalar(A2, ps_b1[0:1, 0:128], inv_n,
                                                   None, Alu.mult),
           waits=[("mm", mm_br)])
        c_hm = p.at("dv")
        mm_bhm = op("tensor", lambda eng: eng.matmul(ps_b2[:, 0:128], ones_r, A2,
                                                     start=True, stop=True),
                    waits=[("dv", c_hm)])

        def fin_hv(eng):
            eng.tensor_scalar(B2, ps_b1[0:1, 512:640], inv_n, None, Alu.mult)
            eng.tensor_tensor(A2, A2, A2, Alu.mult)
            eng.drain()
            return eng.tensor_tensor(B2, B2, A2, Alu.subtract)
        op("vector", fin_hv, waits=[("mm", mm_bhm)])
        c_hv = p.at("dv")
        op("scalar", lambda eng: eng.activation(A2, B2, Act.Sqrt, bias=eps_t[0:1, :]),
           waits=[("dv", c_hv)])
        a_hstd = p.at("ac")
        op("vector", lambda eng: eng.reciprocal(B2, A2), waits=[("ac", a_hstd)])
        c_hr = p.at("dv")
        mm_bcast = op("tensor", lambda eng: eng.matmul(ps_b2[:, 128:256], ones_r, B2,
                                                       start=True, stop=True),
                      waits=[("dv", c_hr)])

        for c in range(KC):
            def xnfn(cc):
                def f(eng):
                    eng.drain()
                    eng.tensor_tensor(scr1[:, :], x_r[:, cc * NT:(cc + 1) * NT],
                                      ps_big[:, 0:1024], Alu.subtract)
                    eng.drain()
                    return eng.tensor_tensor(xs_r[:, cc * NT:(cc + 1) * NT], scr1[:, :],
                                             ps_big[:, 1024:2048], Alu.mult)
                return f
            op("vector", xnfn(c), waits=[("mm", mm_bcast)] if c == 0 else ())
        for c in range(KC):
            def xnhfn(cc):
                def f(eng):
                    eng.drain()
                    eng.tensor_tensor(scr3[:, :], xh_r[:, cc * 128:(cc + 1) * 128],
                                      ps_b2[:, 0:128], Alu.subtract)
                    eng.drain()
                    return eng.tensor_tensor(xsh_r[:, cc * 128:(cc + 1) * 128], scr3[:, :],
                                             ps_b2[:, 128:256], Alu.mult)
                return f
            op("vector", xnhfn(c))
        c_xn = p.at("dv")

        slab_state = {"dv": [slab0_init, 0], "mm": [0, 0, 0, 0]}
        QSTG = [scr1[:, 0:512], scr1[:, 512:1024]]
        qstate = {"d2": [0, 0], "f8dma": [None] * 4}

        def quant_piece(pi, extra_waits=()):
            setid = pi % 2
            stg = QSTG[setid]
            w0 = list(extra_waits)
            if qstate["d2"][setid]:
                w0.append(("ac", qstate["d2"][setid]))
            dsrc = dma("sync", stg, qsrc[:, pi * 512:(pi + 1) * 512], waits=w0)
            fq = f8x[:, (pi % 4) * 512:(pi % 4 + 1) * 512]
            w2 = [dsrc]
            if qstate["f8dma"][pi % 4] is not None:
                w2.append(qstate["f8dma"][pi % 4])
            a2 = op("scalar", (lambda f_, s_: (lambda eng: eng.activation(
                f_, s_, Act.Copy)))(fq, stg), waits=w2)
            qstate["d2"][setid] = a2
            qstate["f8dma"][pi % 4] = dma(
                "gpsimd", qshard[:, pi * 512:(pi + 1) * 512], fq,
                waits=[("ac", a2)])
            return a2

        def stream_slab(src_ap, ncols, slot, extra_waits=()):
            wf = (wfa, wfb)[slot]
            wq = (wqa, wqb)[slot]
            w = list(extra_waits)
            if slab_state["dv"][slot]:
                w.append(("dv", slab_state["dv"][slot]))
            d = dma("sync", wf[:, 0:ncols], src_ap, waits=w)
            w2 = [d]
            if slab_state["mm"][slot]:
                w2.append(("mm", slab_state["mm"][slot]))

            def cast(eng):
                return eng.tensor_copy(wq[:, 0:ncols], wf[:, 0:ncols])
            cv = op("vector", cast, waits=w2)
            slab_state["dv"][slot] = cv
            return wq, cv

        f8state = {"cc": False}

        def stream_fp8(kind, idx, slot, extra_waits=()):
            rank, col0, cols = QBYKIND[(kind, idx)]
            buf = (f8x, f8wb, f8wd, f8we)[slot]
            w = list(extra_waits)
            if not f8state["cc"]:
                w.append(("cc", 1))
                f8state["cc"] = True
            if slab_state["mm"][slot]:
                w.append(("mm", slab_state["mm"][slot]))
            dmas = []
            c0 = 0
            while c0 < cols:
                cw = min(1024, cols - c0)
                dmas.append(dma("sync", buf[:, c0:c0 + cw],
                                qfull[rank, :, col0 + c0:col0 + c0 + cw],
                                waits=(w if c0 == 0 else ())))
                c0 += cw
            return buf, dmas

        def slab_used(slot, mm_count):
            slab_state["mm"][slot] = mm_count

        phq = {"next": 0}

        def phq_emit(n):
            for _ in range(n):
                if phq["next"] < 72:
                    ew = [("dv", c_xn)] if phq["next"] < 2 else ()
                    quant_piece(phq["next"], extra_waits=ew)
                    phq["next"] += 1

        wgp = {"next": 0, "cast": [0, 0, 0, 0], "last": 0}

        def wgp_emit(n):
            for _ in range(n):
                q = wgp["next"]
                if q >= 32:
                    return
                wgp["next"] += 1
                kc, H = q // 2, q % 2
                s4 = q % 4
                stg = xh_r[:, s4 * 512:(s4 + 1) * 512]
                w0 = [("dv", c_xn)] if q < 4 else [("dv", wgp["cast"][s4])]
                d = dma("sync", stg, wg[kc * 128:(kc + 1) * 128,
                                        H * 512:(H + 1) * 512], waits=w0)
                cv = op("vector", (lambda o_, s_: (lambda eng: eng.tensor_copy(
                    o_, s_)))(big[:, H * 8192 + kc * 512:H * 8192 + (kc + 1) * 512],
                              stg), waits=[d])
                wgp["cast"][s4] = cv
                wgp["last"] = cv

        for lh in range(4):
            wqv, cv = stream_slab(wkq[lh * 128:(lh + 1) * 128, :], 2048, lh % 2,
                                  extra_waits=[("mm", mm_stats)] if lh < 2 else ())

            def kmm(hh, wt):
                def f(eng):
                    r = None
                    for kc in range(KC):
                        r = eng.matmul(ps_big[:, hh * 128:(hh + 1) * 128],
                                       wt[:, kc * 128:(kc + 1) * 128],
                                       xsh_r[:, kc * 128:(kc + 1) * 128],
                                       start=(kc == 0), stop=(kc == KC - 1))
                    return r
                return f
            w = [("dv", cv)]
            if lh < 2:
                w.append(("dv", c_xn))
            m = op("tensor", kmm(lh, wqv), waits=w)
            slab_used(lh % 2, m)
            phq_emit(9)
            wgp_emit(4)
        mm_k = p.at("mm")

        def vb(h):
            return ps_b1[:, h * 128:(h + 1) * 128]

        for lh in range(4):
            wqv, cv = stream_slab(wvq[lh * 128:(lh + 1) * 128, :], 2048, lh % 2)

            def vmm(hh, wt):
                def f(eng):
                    r = None
                    for kc in range(KC):
                        r = eng.matmul(vb(hh), xsh_r[:, kc * 128:(kc + 1) * 128],
                                       wt[:, kc * 128:(kc + 1) * 128],
                                       start=(kc == 0), stop=(kc == KC - 1))
                    return r
                return f
            m = op("tensor", vmm(lh, wqv), waits=[("dv", cv)])
            slab_used(lh % 2, m)
            phq_emit(9)
            wgp_emit(4)
        mm_v = p.at("mm")

        for lh in range(4):
            op("scalar", (lambda hh: (lambda eng: eng.activation(
                v_sb[:, hh * 128:(hh + 1) * 128], vb(hh), Act.Copy)))(lh),
               waits=[("mm", mm_v)] if lh == 0 else ())
        a_vevac = p.at("ac")

        rope_done = []
        for h in range(4):
            w = [("mm", mm_k), ("ac", a_vevac)] if h == 0 else [("dv", rope_done[-1])]
            a_kf = op("scalar", (lambda hh: (lambda eng: eng.activation(
                scr4[:, :], ps_big[:, hh * 128:(hh + 1) * 128], Act.Copy)))(h), waits=w)
            dsw1 = dma("gpsimd", ksw[0:64, :], scr4[64:128, :], waits=[("ac", a_kf)])
            dsw2 = dma("gpsimd", ksw[64:128, :], scr4[0:64, :])

            def rope(hh):
                def f(eng):
                    eng.drain()
                    eng.tensor_tensor(kt1[:, :], scr4[:, :], rC[:, :], Alu.mult)
                    eng.tensor_tensor(ksw[:, :], ksw[:, :], rS[:, :], Alu.mult)
                    eng.drain()
                    return eng.tensor_tensor(k_sb[:, hh * 128:(hh + 1) * 128],
                                             kt1[:, :], ksw[:, :], Alu.add)
                return f
            rope_done.append(op("vector", rope(h), waits=[dsw1, dsw2]))
        c_ropes = p.at("dv")

        attn_tail = []
        for h in range(4):
            sps = (ps_b2, ps_b3)[h % 2]
            w = [("dv", c_ropes)] if h == 0 else []
            if h >= 2:
                w.append(("dv", attn_tail[h - 2]))
            mm_sc = op("tensor", (lambda hh, sp: (lambda eng: eng.matmul(
                sp[0:64, 0:128], lqb[:, hh * L:(hh + 1) * L],
                k_sb[:, hh * 128:(hh + 1) * 128], start=True, stop=True)))(h, sps),
                waits=w)

            def smax(hh, sp):
                def f(eng):
                    eng.drain()
                    eng.scalar_tensor_tensor(smt, sp[0:64, 0:128], SCALE, mskt[:, :],
                                             Alu.mult, Alu.add)
                    eng.drain()
                    eng.reduce_max(sc_mx[:, :], smt, Ax.X)
                    eng.drain()
                    return eng.tensor_scalar(sc_mx[:, :], sc_mx[:, :], -1.0, None, Alu.mult)
                return f
            c_sm = op("vector", smax(h, sps), waits=[("mm", mm_sc)])
            a_exp = op("scalar", lambda eng: eng.activation(ewt, smt, Act.Exp,
                                                            bias=sc_mx[:, :]),
                       waits=[("dv", c_sm)])

            def norm(eng):
                eng.reduce_sum(sc_s[:, :], ewt, Ax.X)
                eng.drain()
                return eng.reciprocal(sc_s[:, :], sc_s[:, :])
            c_nrm = op("vector", norm, waits=[("ac", a_exp)])
            a_wn = op("scalar", lambda eng: eng.activation(wfp, ewt, Act.Copy,
                                                           scale=sc_s[:, :]),
                      waits=[("dv", c_nrm)])
            tcol = (h % 2) * 512
            mm_t = op("tensor", (lambda tc: (lambda eng: eng.matmul(
                ps_big[0:128, tc:tc + 64], wfp, idft[0:64, 0:64], start=True, stop=True,
                is_transpose=True)))(tcol), waits=[("ac", a_wn)])
            c_wt = op("vector", (lambda hh, tc: (lambda eng: eng.tensor_copy(
                wT_sb[:, (hh % 2) * 64:(hh % 2 + 1) * 64], ps_big[0:128, tc:tc + 64])))(h, tcol),
                waits=[("mm", mm_t)])
            acol = 1024 + (h % 2) * 512
            mm_a = op("tensor", (lambda hh, cl: (lambda eng: eng.matmul(
                ps_big[0:64, cl:cl + 128], wT_sb[:, (hh % 2) * 64:(hh % 2 + 1) * 64],
                v_sb[:, hh * 128:(hh + 1) * 128], start=True, stop=True)))(h, acol),
                waits=[("dv", c_wt)])
            c_ae = op("vector", (lambda hh, cl: (lambda eng: eng.tensor_copy(
                attn_sb[0:64, hh * 128:(hh + 1) * 128],
                ps_big[0:64, cl:cl + 128])))(h, acol), waits=[("mm", mm_a)])
            attn_tail.append(c_ae)
        c_attn = p.at("dv")

        d_loc = dma("sync", attn_loc[:, :], attn_sb[0:64, 0:512],
                    waits=[("dv", c_attn)])
        p.add("gpsimd", lambda eng: eng.collective_compute(
            "AllGather", Alu.bypass, ins=[attn_loc[:, :]],
            outs=[attn_glob[:, :, :]],
            replica_groups=[[0, 1, 2, 3], [4, 5, 6, 7]]),
            waits=[d_loc], inc=("c2", 1))

        phq_emit(72)
        ag_waits = {}
        for t_ in qstate["f8dma"]:
            if t_ is not None:
                ln, v_ = t_
                ag_waits[ln] = max(ag_waits.get(ln, 0), v_)
        p.add("gpsimd", lambda eng: eng.collective_compute(
            "AllGather", Alu.bypass, ins=[qshard[:, :]], outs=[qfull[:, :, :]],
            replica_groups=[[0, 1, 2, 3, 4, 5, 6, 7]]),
            waits=sorted(ag_waits.items()), inc=("cc", 1))
        g_attn = []
        for h in range(HC):
            g_attn.append(dma(
                "sync", attn_sb[(h % 2) * 64:(h % 2) * 64 + 64,
                                (h // 2) * 128:(h // 2 + 1) * 128],
                attn_glob[h // 4, :, (h % 4) * 128:(h % 4 + 1) * 128],
                waits=[("c2", 1)] if h == 0 else ()))

        c_wgh = wgp["last"]
        gt_done = []
        for tile in range(16):
            H, t = tile // 8, tile % 8
            if True:
                gps = (ps_b2, ps_b3)[tile % 2]

                def gmm(tt, gp, HH):
                    def f(eng):
                        r = None
                        for kc in range(KC):
                            r = eng.matmul(gp[:, 0:512],
                                           xs_r[:, kc * NT + tt * 128:kc * NT + (tt + 1) * 128],
                                           big[:, HH * 8192 + kc * 512:HH * 8192 + (kc + 1) * 512],
                                           start=(kc == 0), stop=(kc == KC - 1))
                        return r
                    return f
                w = [("dv", c_wgh)]
                if tile >= 2:
                    w.append(("dv", gt_done[tile - 2]))
                else:
                    w.append(("dv", c_attn))
                mm_g = op("tensor", gmm(t, gps, H), waits=w)

                def gsm1(gp):
                    def f(eng):
                        eng.tensor_reduce(mx8[:, 0:8].rearrange("p (s o) -> p s o", o=1),
                                          gp[:, 0:512].rearrange("p (s t) -> p s t", s=8),
                                          Ax.X, Alu.max)
                        eng.drain()
                        return eng.tensor_scalar(mx8[:, 0:8], mx8[:, 0:8], -SCALE, None,
                                                 Alu.mult)
                    return f
                c_g1 = op("vector", gsm1(gps), waits=[("mm", mm_g)])

                def gexp(gp):
                    def f(eng):
                        r = None
                        for s in range(8):
                            r = eng.activation(scr1[:, s * 64:(s + 1) * 64],
                                               gp[:, s * 64:(s + 1) * 64], Act.Exp,
                                               bias=mx8[:, s:s + 1], scale=SCALE)
                        return r
                    return f
                a_ge = op("scalar", gexp(gps), waits=[("dv", c_g1)])

                def gsum(eng):
                    eng.tensor_reduce(sm8[:, 0:8].rearrange("p (s o) -> p s o", o=1),
                                      scr1[:, 0:512].rearrange("p (s t) -> p s t", s=8),
                                      Ax.X, Alu.add)
                    eng.drain()
                    return eng.reciprocal(sm8[:, 0:8], sm8[:, 0:8])
                c_gs = op("vector", gsum, waits=[("ac", a_ge)])

                def gnorm(eng):
                    r = None
                    for s in range(8):
                        r = eng.activation(scr1[:, 512 + s * 64:512 + (s + 1) * 64],
                                           scr1[:, s * 64:(s + 1) * 64], Act.Copy,
                                           scale=sm8[:, s:s + 1])
                    return r
                a_gn = op("scalar", gnorm, waits=[("dv", c_gs)])

                last = None
                for p2 in range(4):
                    pair = H * 4 + p2
                    tcol = (p2 % 2) * 512
                    mm_tr = op("tensor", (lambda p3, tc: (lambda eng: eng.matmul(
                        ps_b1[0:128, tc:tc + 128], scr1[:, 512 + p3 * 128:512 + (p3 + 1) * 128],
                        idft[:, :], start=True, stop=True, is_transpose=True)))(p2, tcol),
                        waits=[("ac", a_gn)] if p2 == 0 else [("dv", p.at("dv"))])
                    last = op("vector", (lambda pr, tc, tt: (lambda eng: eng.tensor_copy(
                        gFM[:, pr * NT + tt * 128:pr * NT + (tt + 1) * 128],
                        ps_b1[0:128, tc:tc + 128])))(pair, tcol, t),
                        waits=[("mm", mm_tr)])
                gt_done.append(last)
        c_gates = p.at("dv")

        mix_evac = []
        for h in range(HC):
            pair, row = h // 2, (h % 2) * 64
            for half in range(2):
                mps = (ps_b2, ps_b3)[half]
                w = []
                if h == 0 and half == 0:
                    w = [("dv", c_gates)] + g_attn
                if len(mix_evac) >= 2:
                    w.append(("ac", mix_evac[-2]))

                def mmix(hh, pr, rw, hf, mp):
                    def f(eng):
                        return eng.matmul(
                            mp[:, 0:512], attn_sb[rw:rw + 64, pr * 128:(pr + 1) * 128],
                            gFM[rw:rw + 64, pr * NT + hf * NH:pr * NT + (hf + 1) * NH],
                            start=True, stop=True)
                    return f
                mm_m = op("tensor", mmix(h, pair, row, half, mps), waits=w)
                a_me = op("scalar", (lambda hh, hf, mp: (lambda eng: eng.activation(
                    big[:, hh * NT + hf * NH:hh * NT + (hf + 1) * NH], mp[:, 0:512],
                    Act.Copy)))(h, half, mps), waits=[("mm", mm_m)])
                mix_evac.append(a_me)
        a_mix = p.at("ac")

        mm_p = 0
        for oc in range(KC):
            wqv, cv = stream_slab(wp[oc * 128:(oc + 1) * 128, :], 2048, oc % 2,
                                  extra_waits=[("ac", a_mix)] if oc < 2 else ())
            for half in range(2):
                pps = (ps_b2, ps_b3)[half]

                def pmm(wt, hf, pp):
                    def f(eng):
                        r = None
                        for kc in range(KC):
                            r = eng.matmul(pp[:, 0:512], wt[:, kc * 128:(kc + 1) * 128],
                                           big[:, kc * NT + hf * NH:kc * NT + (hf + 1) * NH],
                                           start=(kc == 0), stop=(kc == KC - 1))
                        return r
                    return f
                w = [("dv", cv)]
                if oc > 0 or half > 0:
                    w.append(("dv", p.at("dv")))
                mm_p = op("tensor", pmm(wqv, half, pps), waits=w)
                op("vector", (lambda o2, hf, pp: (lambda eng: eng.tensor_tensor(
                    x_r[:, o2 * NT + hf * NH:o2 * NT + (hf + 1) * NH],
                    x_r[:, o2 * NT + hf * NH:o2 * NT + (hf + 1) * NH],
                    pp[:, 0:512], Alu.add)))(oc, half, pps), waits=[("mm", mm_p)])
            slab_used(oc % 2, mm_p)
        c_wp = p.at("dv")

        slab_state["mm"][0] = slab_state["mm"][1] = 0
        psG = [ps_big[:, 0:NH], ps_big[:, NH:2 * NH]]
        psU = [ps_big[:, 1024:1024 + NH], ps_big[:, 1024 + NH:2048]]
        DB = [ps_b1[:, 0:NH], ps_b1[:, NH:2 * NH], ps_b2[:, 0:NH], ps_b3[:, 0:NH]]
        epi_prev = [0, 0]
        down_guard = [0, 0, 0, 0]
        add_prev = [0, 0]
        down_last = 0
        mm_d = 0
        for g in range(NG):
            mm_g_last = mm_u_last = 0
            for j in range(GSZ):
                ic = g * GSZ + j
                bufg, dmg = stream_fp8("gate", ic, 0)
                bufu, dmu = stream_fp8("up", ic, 1)

                def pmmh(wt, hf, pdst):
                    def f(eng):
                        r = None
                        for kc in range(KC):
                            r = eng.matmul(pdst,
                                           wt[:, kc * 128:(kc + 1) * 128],
                                           xs_r[:, kc * NT + hf * NH:kc * NT + (hf + 1) * NH],
                                           start=(kc == 0), stop=(kc == KC - 1))
                        return r
                    return f

                for half in range(2):
                    w = list(dmg) if half == 0 else []
                    if epi_prev[half]:
                        w.append(("ac", epi_prev[half]))
                    mm_g_last = op("tensor", pmmh(bufg, half, psG[half]), waits=w)
                    w2 = list(dmu) if half == 0 else []
                    mm_u_last = op("tensor", pmmh(bufu, half, psU[half]), waits=w2)

                    def epi(icc, hf):
                        def f(eng):
                            eng.activation(scr1[:, 0:NH], psG[hf],
                                           Act.Silu, scale=gs_t[:, icc:icc + 1])
                            return eng.activation(scr1[:, NH:NT], psU[hf],
                                                  Act.Copy, scale=us_t[:, icc:icc + 1])
                        return f
                    a_ep = op("scalar", epi(ic, half),
                              waits=[("mm", mm_u_last), ("dv", p.at("dv"))])
                    epi_prev[half] = a_ep
                    wf = [("ac", a_ep)]
                    if j == 0 and half == 0:
                        wf.append(("mm", down_last if g > 0 else mm_p))
                    op("vector", (lambda jj, hf: (lambda eng: eng.tensor_tensor(
                        big[:, jj * NT + hf * NH:jj * NT + (hf + 1) * NH],
                        scr1[:, 0:NH], scr1[:, NH:NT], Alu.mult)))(j, half),
                        waits=wf)
                slab_used(0, mm_g_last)
                slab_used(1, mm_u_last)
            c_fuse = p.at("dv")

            for oc in range(KC):
                bufd, dmd = stream_fp8("down", g * 16 + oc, 2 + oc % 2)
                for half in range(2):
                    blk = (2 * oc + half) % 4

                    def dmmf(wt, hf, pdst):
                        def f(eng):
                            r = None
                            for j in range(GSZ):
                                r = eng.matmul(pdst,
                                               wt[:, j * 128:(j + 1) * 128],
                                               big[:, j * NT + hf * NH:j * NT + (hf + 1) * NH],
                                               start=(j == 0), stop=(j == GSZ - 1))
                            return r
                        return f
                    w = list(dmd) if half == 0 else []
                    if oc == 0 and half == 0:
                        w.append(("dv", c_fuse))
                    if down_guard[blk]:
                        w.append(("ac", down_guard[blk]))
                    mm_d = op("tensor", dmmf(bufd, half, DB[blk]), waits=w)
                    w_de = [("mm", mm_d)]
                    if add_prev[half]:
                        w_de.append(("dv", add_prev[half]))
                    a_de = op("scalar", (lambda o2, hf, bk: (lambda eng: eng.activation(
                        scr1[:, hf * NH:(hf + 1) * NH], DB[bk],
                        Act.Copy, scale=ds_t[:, o2:o2 + 1])))(oc, half, blk),
                        waits=w_de)
                    down_guard[blk] = a_de
                    add_prev[half] = op(
                        "vector", (lambda o2, hf: (lambda eng: eng.tensor_tensor(
                            x_r[:, o2 * NT + hf * NH:o2 * NT + (hf + 1) * NH],
                            x_r[:, o2 * NT + hf * NH:o2 * NT + (hf + 1) * NH],
                            scr1[:, hf * NH:(hf + 1) * NH], Alu.add)))(oc, half),
                        waits=[("ac", a_de)])
                slab_used(2 + oc % 2, mm_d)
            down_last = mm_d
        c_ffn = p.at("dv")

        st_dmas = []
        for c in range(KC):
            st_dmas.append(dma("gpsimd", out[c * 128:(c + 1) * 128, :],
                               x_r[:, c * NT:(c + 1) * NT],
                               waits=[("dv", c_ffn)] if c == 0 else ()))
        final_waits = {}
        for (ln, v) in st_dmas:
            final_waits[ln] = max(final_waits.get(ln, 0), v)
        p.add("gpsimd", lambda eng: None, waits=sorted(final_waits.items()))

        sems = {}
        with contextlib.ExitStack() as sst:
            for s in (["dv", "ac", "mm", "cc", "c2"] + QLANES["sync"] + QLANES["gpsimd"]):
                sems[s] = sst.enter_context(nc.semaphore(s))
            with nc.Block() as block:
                def make(eng_name):
                    def stream(eng):
                        lastw = {}
                        for (e, waits, fn, incs) in p.ops:
                            if e != eng_name:
                                continue
                            for (s, v) in waits:
                                if v > 0 and lastw.get(s, -1) < v:
                                    eng.wait_ge(sems[s], v)
                                    lastw[s] = v
                            instr = fn(eng)
                            for (s, amt) in incs:
                                instr.then_inc(sems[s], amt)
                    return stream
                block.sync(make("sync"))
                block.gpsimd(make("gpsimd"))
                block.scalar(make("scalar"))
                block.vector(make("vector"))
                block.tensor(make("tensor"))
    return nc


def _host_layouts(inputs):
    x = np.asarray(inputs["L0_attn_x"], dtype=np.float32)
    Wk = np.asarray(inputs["L0_attn_Wk"], dtype=np.float32)
    Wv = np.asarray(inputs["L0_attn_Wv"], dtype=np.float32)
    Wg = np.asarray(inputs["L0_attn_Wg"], dtype=np.float32)
    Wp = np.asarray(inputs["L0_attn_Wp"], dtype=np.float32)
    gw = np.asarray(inputs["L0_ffn_gate_w"], dtype=np.float32)
    uw = np.asarray(inputs["L0_ffn_up_w"], dtype=np.float32)
    dw = np.asarray(inputs["L0_ffn_down_w"], dtype=np.float32)
    gs = np.asarray(inputs["L0_ffn_gate_scale"], dtype=np.float32)
    us = np.asarray(inputs["L0_ffn_up_scale"], dtype=np.float32)
    ds = np.asarray(inputs["L0_ffn_down_scale"], dtype=np.float32)
    lq = np.asarray(inputs["L0_attn_latent_queries"], dtype=np.float32)

    perm_d = np.concatenate([np.arange(0, 128, 2), np.arange(1, 128, 2)])

    Wk_p = Wk.reshape(C, HC, D)[:, :, perm_d].reshape(C, C)
    wk_r = np.ascontiguousarray(
        Wk_p.reshape(KC, 128, HC, D).transpose(2, 1, 0, 3).reshape(C, C))
    wv_r = np.ascontiguousarray(
        Wv.reshape(KC, 128, HC, D).transpose(2, 1, 0, 3).reshape(C, C))
    wp_r = np.ascontiguousarray(
        Wp.reshape(KC, 128, KC, 128).transpose(2, 1, 0, 3).reshape(C, C))
    wgate_r = gw.reshape(KC, 128, IC, 128).transpose(2, 1, 0, 3).reshape(IC * 128, C)
    wup_r = uw.reshape(KC, 128, IC, 128).transpose(2, 1, 0, 3).reshape(IC * 128, C)
    dwr = dw.reshape(NG, GSZ, 128, KC, 128).transpose(0, 3, 2, 1, 4)
    wdown_r = dwr.reshape(NG * KC * 128, GSZ * 128)
    srcs = {"gate": wgate_r, "up": wup_r, "down": wdown_r}
    qsrc_by_rank = {}
    for c in range(8):
        blocks = [srcs[k][i * 128:(i + 1) * 128, :]
                  for (k, i, r, c0, cs) in QTAB if r == c]
        qsrc_by_rank[c] = np.ascontiguousarray(
            np.concatenate(blocks, axis=1))
    lq_r = np.ascontiguousarray(
        lq[0].transpose(1, 2, 0)[:, perm_d, :].transpose(1, 0, 2).reshape(128, HC * L))

    import jax
    import jax.numpy as jnp
    cpu = jax.devices("cpu")[0]
    with jax.default_device(cpu):
        invf = 1.0 / 10000.0 ** (jnp.arange(0, 64, dtype=jnp.float32) * 2.0 / 128.0)
        ang = jnp.arange(128, dtype=jnp.float32)[:, None] * invf[None, :]
        cosv = np.asarray(jnp.cos(ang), dtype=np.float32)
        sinv = np.asarray(jnp.sin(ang), dtype=np.float32)
    rCh = np.empty((128, 128), np.float32)
    rSh = np.empty((128, 128), np.float32)
    rCh[0:64, :] = cosv.T
    rCh[64:128, :] = cosv.T
    rSh[0:64, :] = -sinv.T
    rSh[64:128, :] = sinv.T

    maska = np.where(np.arange(128)[None, :] <= np.arange(64)[:, None],
                     np.float32(0.0), np.float32(NEG)).astype(ml_dtypes.bfloat16)
    idf = np.eye(128, dtype=np.float32)
    gs_r = np.ascontiguousarray(gs.reshape(IC, 128).T)
    us_r = np.ascontiguousarray(us.reshape(IC, 128).T)
    ds_r = np.ascontiguousarray(ds.reshape(KC, 128).T)

    shared = dict(wg=np.ascontiguousarray(Wg), wp=wp_r,
                  ropec=rCh, ropes=rSh, maska=maska, idf=idf,
                  gs=gs_r, us=us_r, ds=ds_r)
    in_maps = []
    for c in range(8):
        b, q = c // 4, c % 4
        m = dict(shared)
        m["xT"] = np.ascontiguousarray(x[b, q * NT:(q + 1) * NT, :].T)
        m["qsrc"] = qsrc_by_rank[c]
        m["xhT"] = np.ascontiguousarray(x[b, 0:128, :].T)
        m["wkq"] = np.ascontiguousarray(wk_r[4 * q * 128:(4 * q + 4) * 128, :])
        m["wvq"] = np.ascontiguousarray(wv_r[4 * q * 128:(4 * q + 4) * 128, :])
        m["lqq"] = np.ascontiguousarray(lq_r[:, 4 * q * 64:(4 * q + 4) * 64])
        in_maps.append(m)
    return in_maps


def kernel(**inputs):
    in_maps = _host_layouts(inputs)
    nc = build()
    res = run_bass_kernel_spmd(nc, in_maps, core_ids=list(range(8)))
    x = np.asarray(inputs["L0_attn_x"])
    outp = np.empty((2, 4096, C), dtype=np.float32)
    for c in range(8):
        b, q = c // 4, c % 4
        outp[b, q * NT:(q + 1) * NT, :] = np.asarray(res.results[c]["out"]).T
    return outp
